# version 02
# baseline (speedup 1.0000x reference)
from contextlib import ExitStack
import numpy as np
import concourse.bass as bass
import concourse.mybir as mybir
from concourse.bass_utils import run_bass_kernel_spmd

F32 = mybir.dt.float32
BF16 = mybir.dt.bfloat16
AF = mybir.ActivationFunctionType
ALU = mybir.AluOpType
AX = mybir.AxisListType

T = 2048
C = 1024
NTT = T // 128
NCC = C // 128

A_XN = 0
A_HT = 0
A_DW = 16384
A_QT = 0
A_KT = 8192
A_VT = 16384
A_OT = 24576
B_P0, B_P1, B_T0, B_T1 = 0, 2048, 4096, 6144


def _as_f32(x):
    return np.ascontiguousarray(np.asarray(x), dtype=np.float32)


def host_prep(inp):
    d = {}
    d["L0_W_in"] = _as_f32(inp["L0_attn_W_in"])
    b = _as_f32(inp["L0_attn_b_in"])
    bi = np.zeros((128, 2), np.float32)
    bi[:, 0] = b[0:128]
    bi[0:64, 1] = b[128:192]
    d["L0_b_in"] = bi
    d["L0_W_x"] = _as_f32(inp["L0_attn_W_x"])
    d["L0_b_x"] = _as_f32(inp["L0_attn_b_x"]).reshape(64, 1)
    d["L0_W_dt"] = _as_f32(inp["L0_attn_W_dt"])
    d["L0_b_dt"] = _as_f32(inp["L0_attn_b_dt"]).reshape(64, 1)
    d["L0_W_out"] = _as_f32(inp["L0_attn_W_out"])
    d["L0_b_out"] = _as_f32(inp["L0_attn_b_out"]).reshape(1, C)
    d["L0_norm_w"] = _as_f32(inp["L0_attn_norm_w"]).reshape(NCC, 128).T.copy()
    d["L0_W_r"] = _as_f32(inp["L0_ffn_W_r"]).T.copy()
    d["L0_W_k"] = _as_f32(inp["L0_ffn_W_k"]).T.copy()
    d["L0_W_v"] = _as_f32(inp["L0_ffn_W_v"]).T.copy()
    d["L0_mu_r"] = _as_f32(inp["L0_ffn_mu_r"]).reshape(NCC, 128).T.copy()
    d["L0_mu_k"] = _as_f32(inp["L0_ffn_mu_k"]).reshape(NCC, 128).T.copy()

    d["L1_W_attn"] = _as_f32(inp["L1_attn_W_attn"])
    ba = _as_f32(inp["L1_attn_b_attn"])
    d["L1_battn_feat"] = ba[:2048].reshape(16, 128).T.copy()
    d["L1_bv_row"] = ba[2048:].reshape(1, C)
    d["L1_W_proj"] = _as_f32(inp["L1_attn_W_proj"])
    d["L1_bproj_row"] = _as_f32(inp["L1_attn_b_proj"]).reshape(1, C)
    d["L1_gate_w"] = _as_f32(inp["L1_ffn_gate_w"])
    d["L1_gate_b"] = _as_f32(inp["L1_ffn_gate_b"]).reshape(1, 8)
    d["L1_w1"] = _as_f32(inp["L1_ffn_w1"])
    d["L1_b1"] = _as_f32(inp["L1_ffn_b1"]).reshape(8, 16, 128).transpose(0, 2, 1).copy()
    d["L1_w2"] = _as_f32(inp["L1_ffn_w2"])
    d["L1_b2"] = _as_f32(inp["L1_ffn_b2"])

    d["L2_Wq"] = _as_f32(inp["L2_attn_Wq"])
    d["L2_Wk"] = _as_f32(inp["L2_attn_Wk"])
    d["L2_Wv"] = _as_f32(inp["L2_attn_Wv"])
    d["L2_Wo"] = _as_f32(inp["L2_attn_Wo"])
    d["L2_bo_row"] = _as_f32(inp["L2_attn_bo"]).reshape(1, C)
    d["L2_router_w"] = _as_f32(inp["L2_ffn_router_w"]).T.copy()
    d["L2_router_b"] = _as_f32(inp["L2_ffn_router_b"]).reshape(1, 8)
    d["L2_gate_w"] = _as_f32(inp["L2_ffn_gate_w"]).transpose(0, 2, 1).copy()
    d["L2_up_w"] = _as_f32(inp["L2_ffn_up_w"]).transpose(0, 2, 1).copy()
    d["L2_down_w"] = _as_f32(inp["L2_ffn_down_w"]).transpose(0, 2, 1).copy()
    H = 16
    slope = (1.0 / ((2.0 ** 8) ** (1.0 / H)) ** np.arange(1, H + 1)).astype(np.float32)
    pos = np.arange(T, dtype=np.float32)
    d["L2_aq"] = np.ascontiguousarray(
        np.stack([np.outer(slope, pos), np.ones((H, T), np.float32)], axis=1))
    d["L2_ak"] = np.ascontiguousarray(
        np.stack([np.ones((H, T), np.float32), -np.outer(slope, pos)], axis=1))

    DH = 64
    perm = np.zeros(C, dtype=np.int64)
    for hp in range(H // 2):
        base = hp * 128
        perm[base + 0:base + 32] = (2 * hp) * DH + np.arange(0, DH, 2)
        perm[base + 32:base + 64] = (2 * hp + 1) * DH + np.arange(0, DH, 2)
        perm[base + 64:base + 96] = (2 * hp) * DH + np.arange(1, DH, 2)
        perm[base + 96:base + 128] = (2 * hp + 1) * DH + np.arange(1, DH, 2)
    wq = _as_f32(inp["L3_attn_w_qkv"]).T.copy()
    wq3 = np.concatenate([wq[:, perm], wq[:, C + perm], wq[:, 2 * C:]], axis=1)
    d["L3_w_qkv"] = np.ascontiguousarray(wq3)
    bq = _as_f32(inp["L3_attn_b_qkv"])
    bq3 = np.concatenate([bq[perm], bq[C + perm], bq[2 * C:]])
    d["L3_b_qkv_feat"] = bq3[:2048].reshape(16, 128).T.copy()
    d["L3_bv_row"] = bq3[2048:].reshape(1, C)
    d["L3_w_out"] = _as_f32(inp["L3_attn_w_out"]).T.copy()
    d["L3_bout_row"] = _as_f32(inp["L3_attn_b_out"]).reshape(1, C)

    def gcol(g):
        out = np.zeros((128, 8), np.float32)
        for hp in range(8):
            out[0:32, hp] = g[2 * hp, 0::2]
            out[32:64, hp] = g[2 * hp + 1, 0::2]
            out[64:96, hp] = g[2 * hp, 1::2]
            out[96:128, hp] = g[2 * hp + 1, 1::2]
        return out
    d["L3_qg_col"] = gcol(_as_f32(inp["L3_attn_q_gamma"]))
    d["L3_kg_col"] = gcol(_as_f32(inp["L3_attn_k_gamma"]))
    half = DH // 2
    invf = (1.0 / (10000.0 ** (np.arange(half, dtype=np.float32) / half))).astype(np.float32)
    ang = pos[None, :] * invf[:, None]
    d["L3_cos"] = np.ascontiguousarray(np.concatenate(
        [np.cos(ang)] * 4, axis=0).astype(np.float32))
    d["L3_sin"] = np.ascontiguousarray(np.concatenate(
        [np.sin(ang)] * 4, axis=0).astype(np.float32))
    d["L3_norm_scale_row"] = _as_f32(inp["L3_ffn_norm_scale"]).reshape(1, C)
    d["L3_norm_scale"] = _as_f32(inp["L3_ffn_norm_scale"]).reshape(NCC, 128).T.copy()
    d["L3_gate_w"] = _as_f32(inp["L3_ffn_gate_w"])
    d["L3_gate_b"] = _as_f32(inp["L3_ffn_gate_b"]).reshape(1, 8)
    m1 = _as_f32(inp["L3_ffn_mlp1_w"])
    I2 = 4096
    permh = np.concatenate([np.arange(0, I2, 2), np.arange(1, I2, 2)])
    d["L3_mlp1_w"] = np.ascontiguousarray(m1[:, :, permh])
    b1 = _as_f32(inp["L3_ffn_mlp1_b"])[:, permh]
    d["L3_mlp1_b"] = b1.reshape(8, 32, 128).transpose(0, 2, 1).copy()
    d["L3_mlp2_w"] = _as_f32(inp["L3_ffn_mlp2_w"])
    d["L3_mlp2_b"] = _as_f32(inp["L3_ffn_mlp2_b"])
    return d


class Sem:
    def __init__(self, es, nc, name):
        self.h = es.enter_context(nc.semaphore(name))
        self.n = 0

    def inc(self, instr, k=1):
        instr.then_inc(self.h, k)
        self.n += k
        return instr


class Ctx:
    pass


def alloc_ctx(nc, es):
    c = Ctx()
    c.nc = nc

    def sbt(name, shape, dt):
        return es.enter_context(nc.sbuf_tensor(name, shape, dt))
    c.stream = sbt("stream", [128, NTT * C], F32)
    c.xnT = sbt("xnT", [128, NCC * T], BF16)
    c.scrA = sbt("scrA", [128, 32768], BF16)
    c.scrB = sbt("scrB", [128, 8192], BF16)
    c.scrBf = c.scrB.bitcast(F32)
    c.wU = sbt("wU", [128, 8192], BF16)
    c.wD = sbt("wD", [128, 4096], BF16)
    c.rowb = sbt("rowb", [128, 2048], BF16)
    c.stats = sbt("stats", [128, 64], F32)
    c.gates = sbt("gates", [128, 256], F32)
    c.bfeat = sbt("bfeat", [128, 64], F32)
    c.cvec = sbt("cvec", [128, 24], F32)
    c.csel = sbt("csel", [128, 8], BF16)
    c.rsel = sbt("rsel", [2, 128], BF16)
    c.ones1 = sbt("ones1", [1, 128], BF16)
    c.ident = sbt("ident", [128, 128], BF16)
    c.pp = [es.enter_context(nc.psum_tensor(f"p{i}", [128, 512], F32)) for i in range(6)]
    c.tp = [es.enter_context(nc.psum_tensor(f"tp{i}", [128, 1024], BF16)) for i in range(2)]
    for nm in ["ld", "mm", "ev", "sc", "pl", "tpos", "mx", "pe", "pt", "st",
               "vs"]:
        setattr(c, "s_" + nm, Sem(es, nc, "s_" + nm))

    def vser(eng, instr):
        instr.then_inc(c.s_vs.h, 1)
        c.s_vs.n += 1
        eng.wait_ge(c.s_vs.h, c.s_vs.n)
        return instr
    c.vser = vser
    return c


def phase(nc, **fns):
    with nc.Block() as blk:
        for name, f in fns.items():
            getattr(blk, name)(f)




def emit_setup(c, x_in):
    nc = c.nc
    ld0 = c.s_ld.n

    def g(g):
        c.s_ld.inc(g.dma_start(
            out=c.stream[:].rearrange("p (tt c) -> p tt c", c=C),
            in_=x_in.rearrange("(tt p) c -> p tt c", p=128)), 16)
        g.memset(c.ones1[:], 1.0)
        c.vser(g, g.memset(c.ident[:], 1.0))
        g.affine_select(c.ident[:], c.ident[:], [[-1, 128]], ALU.is_equal, 0.0,
                        base=0, channel_multiplier=1)
        g.memset(c.cvec[:, 16:17], 1e-5)
        g.memset(c.cvec[:, 17:18], 1e-6)
        g.memset(c.cvec[:, 18:19], 1e-8)
        c.vser(g, g.memset(c.csel[:], 1.0))
        c.vser(g, g.affine_select(c.csel[:, 0:4], c.csel[:, 0:4], [[-32, 4]],
                                  ALU.is_ge, 0.0, base=0, channel_multiplier=1))
        g.affine_select(c.csel[:, 0:4], c.csel[:, 0:4], [[32, 4]], ALU.is_ge,
                        0.0, base=31, channel_multiplier=-1)
        c.vser(g, g.memset(c.rsel[:], 1.0))
        g.affine_select(
            c.rsel[:].rearrange("p (a h j) -> p a h j", a=2, h=2, j=32),
            c.rsel[:].rearrange("p (a h j) -> p a h j", a=2, h=2, j=32),
            [[0, 2], [-1, 2], [0, 32]], ALU.is_equal, 0.0,
            base=0, channel_multiplier=1)
        g.wait_ge(c.s_ld.h, ld0 + 16)
    phase(nc, gpsimd=g)

    def v(v):
        v.tensor_add(c.csel[:, 4:5], c.csel[:, 0:1], c.csel[:, 2:3])
        v.tensor_add(c.csel[:, 5:6], c.csel[:, 1:2], c.csel[:, 3:4])
    phase(nc, vector=v)


def emit_store(c, y_out):
    nc = c.nc
    st0 = c.s_st.n

    def s(s):
        c.s_st.inc(s.dma_start(
            out=y_out.rearrange("(tt p) c -> p tt c", p=128),
            in_=c.stream[:].rearrange("p (tt c) -> p tt c", c=C)), 16)
        s.wait_ge(c.s_st.h, st0 + 16)
    phase(nc, sync=s)


def emit_ln(c):
    nc = c.nc
    xn = c.scrA

    def p1s(s):
        for tt in range(NTT):
            s.activation(c.scrBf[:, (tt % 4) * 1024:(tt % 4) * 1024 + 1024],
                         c.stream[:, tt * C:(tt + 1) * C],
                         AF.Square, accum_out=c.stats[:, 16 + tt:17 + tt])

    def p1v(v):
        for tt in range(NTT):
            v.reduce_sum(c.stats[:, tt:tt + 1],
                         c.stream[:, tt * C:(tt + 1) * C], axis=AX.X)
    phase(nc, scalar=p1s, vector=p1v)

    def p2v(v):
        st = c.stats
        c.vser(v, v.tensor_scalar(c.cvec[:, 0:16], st[:, 0:16], 1.0 / C, None,
                                  ALU.mult))
        c.vser(v, v.tensor_scalar(st[:, 16:32], st[:, 16:32], 1.0 / C, None,
                                  ALU.mult))
        c.vser(v, v.tensor_tensor(st[:, 32:48], c.cvec[:, 0:16],
                                  c.cvec[:, 0:16], ALU.mult))
        v.tensor_tensor(st[:, 16:32], st[:, 16:32], st[:, 32:48], ALU.subtract)
    phase(nc, vector=p2v)

    def p3s(s):
        s.activation(c.stats[:, 16:32], c.stats[:, 16:32], AF.Sqrt, bias=c.cvec[:, 16:17])
    phase(nc, scalar=p3s)

    def p4v(v):
        c.vser(v, v.reciprocal(c.stats[:, 48:64], c.stats[:, 16:32]))
        for tt in range(NTT):
            v.tensor_scalar(
                xn[:, A_XN + tt * C:A_XN + (tt + 1) * C],
                c.stream[:, tt * C:(tt + 1) * C],
                c.cvec[:, tt:tt + 1], c.stats[:, 48 + tt:49 + tt],
                ALU.subtract, ALU.mult)
    phase(nc, vector=p4v)
    emit_transpose_to(c, xn, A_XN, c.xnT, 0)


def emit_transpose_to(c, src, src_col0, dstT, dst_col0, ncc=NCC, ntt=NTT):
    nc = c.nc
    tp0 = c.s_tpos.n
    xe0 = c.s_ev.n

    def tens(t):
        k = 0
        for cc in range(ncc):
            for tt in range(ntt):
                if k >= 16:
                    t.wait_ge(c.s_ev.h, xe0 + (k - 16) // 8 + 1)
                slot = k % 16
                bank = c.tp[slot // 8]
                c.s_tpos.inc(t.transpose(
                    bank[:, (slot % 8) * 128:(slot % 8 + 1) * 128],
                    src[:, src_col0 + tt * ncc * 128 + cc * 128:
                        src_col0 + tt * ncc * 128 + (cc + 1) * 128],
                    c.ident[:]))
                k += 1

    def vec(v):
        k = 0
        for cc in range(ncc):
            for tt in range(ntt):
                k += 1
                if k % 8 == 0:
                    v.wait_ge(c.s_tpos.h, tp0 + k)
                    bank = c.tp[((k - 8) % 16) // 8]
                    c.s_ev.inc(v.tensor_copy(
                        dstT[:, dst_col0 + cc * ntt * 128 + (tt - 7) * 128:
                             dst_col0 + cc * ntt * 128 + (tt + 1) * 128],
                        bank[:]))
    phase(nc, tensor=tens, vector=vec)


def emit_router(c, dram_gw, dram_gb, renorm, scale_col=None):
    nc = c.nc
    ld0 = c.s_ld.n
    mm0 = c.s_mm.n
    ev0 = c.s_ev.n
    LG = c.scrBf

    def g(g):
        c.s_ld.inc(g.dma_start(
            out=c.wD[:, 0:64].rearrange("p (cc e) -> p cc e", e=8),
            in_=dram_gw.rearrange("(cc p) e -> p cc e", p=128)), 16)
        c.s_ld.inc(g.dma_start(out=c.rowb[0:1, 0:8], in_=dram_gb), 16)

    def pre_v(v):
        if scale_col is not None:
            v.wait_ge(c.s_ld.h, ld0 + 32)
            for cc in range(NCC):
                v.tensor_scalar(c.wD[:, cc * 8:cc * 8 + 8],
                                c.wD[:, cc * 8:cc * 8 + 8],
                                scale_col(cc), None, ALU.mult)
    phase(nc, gpsimd=g, vector=pre_v)

    def tens(t):
        t.wait_ge(c.s_ld.h, ld0 + 32)
        for tt in range(NTT):
            bank = c.pp[tt % 4]
            if tt >= 4:
                t.wait_ge(c.s_ev.h, ev0 + (tt - 4) + 1)
            for cc in range(NCC):
                t.matmul(bank[:, 0:8],
                         c.xnT[:, cc * T + tt * 128: cc * T + tt * 128 + 128],
                         c.wD[:, cc * 8:cc * 8 + 8],
                         start=(cc == 0), stop=False)
            c.s_mm.inc(t.matmul(bank[:, 0:8], c.ones1[:], c.rowb[0:1, 0:8],
                                start=False, stop=True))

    def vec0(v):
        for tt in range(NTT):
            v.wait_ge(c.s_mm.h, mm0 + tt + 1)
            c.s_ev.inc(v.tensor_copy(LG[:, tt * 8:tt * 8 + 8],
                                     c.pp[tt % 4][:, 0:8]))
    phase(nc, tensor=tens, vector=vec0)

    import os
    if os.environ.get("K_GATE_STOP"):
        return
    st = c.stats
    if not renorm:
        def v1(v):
            for tt in range(NTT):
                v.reduce_max(st[:, 32 + tt:33 + tt], LG[:, tt * 8:tt * 8 + 8],
                             axis=AX.X, negate=True)
        phase(nc, vector=v1)

        def sc(s):
            for tt in range(NTT):
                s.activation(LG[:, 256 + tt * 8:264 + tt * 8],
                             LG[:, tt * 8:tt * 8 + 8], AF.Exp,
                             bias=st[:, 32 + tt:33 + tt],
                             accum_out=st[:, 48 + tt:49 + tt])
        phase(nc, scalar=sc)

        def v2(v):
            E = LG[:, 256:384]
            P = LG[:, 640:768]
            c.vser(v, v.reciprocal(st[:, 0:16], st[:, 48:64]))
            for tt in range(NTT):
                v.tensor_scalar(P[:, tt * 8:tt * 8 + 8],
                                E[:, tt * 8:tt * 8 + 8],
                                st[:, tt:tt + 1], None, ALU.mult)
            for tt in range(NTT):
                v.reduce_max(st[:, 16 + tt:17 + tt], P[:, tt * 8:tt * 8 + 8],
                             axis=AX.X)
            for tt in range(NTT):
                v.tensor_scalar(LG[:, 384 + tt * 8:392 + tt * 8],
                                P[:, tt * 8:tt * 8 + 8],
                                st[:, 16 + tt:17 + tt], None, ALU.is_ge)
            for tt in range(NTT):
                v.scalar_tensor_tensor(LG[:, 512 + tt * 8:520 + tt * 8],
                                       LG[:, 384 + tt * 8:392 + tt * 8],
                                       -2.0, P[:, tt * 8:tt * 8 + 8],
                                       ALU.mult, ALU.add)
            for tt in range(NTT):
                v.reduce_max(st[:, 32 + tt:33 + tt],
                             LG[:, 512 + tt * 8:520 + tt * 8], axis=AX.X)
            for tt in range(NTT):
                v.tensor_scalar(LG[:, 384 + tt * 8:392 + tt * 8],
                                P[:, tt * 8:tt * 8 + 8],
                                st[:, 32 + tt:33 + tt], None, ALU.is_ge)
            for tt in range(NTT):
                v.tensor_tensor(c.gates[:, tt * 8:tt * 8 + 8],
                                P[:, tt * 8:tt * 8 + 8],
                                LG[:, 384 + tt * 8:392 + tt * 8], ALU.mult)
            v.tensor_scalar(c.gates[:, 128:256], c.gates[:, 0:128], 0.5,
                            None, ALU.mult)
        phase(nc, vector=v2)
    else:
        def v1(v):
            for tt in range(NTT):
                v.reduce_max(st[:, tt:tt + 1], LG[:, tt * 8:tt * 8 + 8],
                             axis=AX.X)
            for tt in range(NTT):
                v.tensor_scalar(LG[:, 256 + tt * 8:264 + tt * 8],
                                LG[:, tt * 8:tt * 8 + 8],
                                st[:, tt:tt + 1], None, ALU.is_ge)
            for tt in range(NTT):
                v.scalar_tensor_tensor(LG[:, 384 + tt * 8:392 + tt * 8],
                                       LG[:, 256 + tt * 8:264 + tt * 8],
                                       -1e30, LG[:, tt * 8:tt * 8 + 8],
                                       ALU.mult, ALU.add)
            for tt in range(NTT):
                v.reduce_max(st[:, 16 + tt:17 + tt],
                             LG[:, 384 + tt * 8:392 + tt * 8], axis=AX.X)
            for tt in range(NTT):
                v.tensor_tensor(st[:, 32 + tt:33 + tt], st[:, tt:tt + 1],
                                st[:, 16 + tt:17 + tt], ALU.subtract)
        phase(nc, vector=v1)

        def sc(s):
            s.activation(st[:, 48:64], st[:, 32:48], AF.Sigmoid)
        phase(nc, scalar=sc)

        def v2(v):
            os_ = c.cvec[:, 0:16]
            c.vser(v, v.tensor_scalar(os_, st[:, 48:64], -1.0, 1.0, ALU.mult,
                                      ALU.add))
            for tt in range(NTT):
                v.tensor_scalar(LG[:, 512 + tt * 8:520 + tt * 8],
                                LG[:, 384 + tt * 8:392 + tt * 8],
                                st[:, 16 + tt:17 + tt], None, ALU.is_ge)
            for tt in range(NTT):
                v.tensor_scalar(LG[:, 256 + tt * 8:264 + tt * 8],
                                LG[:, 256 + tt * 8:264 + tt * 8],
                                st[:, 48 + tt:49 + tt], None, ALU.mult)
            for tt in range(NTT):
                v.tensor_scalar(LG[:, 512 + tt * 8:520 + tt * 8],
                                LG[:, 512 + tt * 8:520 + tt * 8],
                                os_[:, tt:tt + 1], None, ALU.mult)
            for tt in range(NTT):
                v.tensor_tensor(c.gates[:, tt * 8:tt * 8 + 8],
                                LG[:, 256 + tt * 8:264 + tt * 8],
                                LG[:, 512 + tt * 8:520 + tt * 8], ALU.add)
            v.tensor_scalar(c.gates[:, 128:256], c.gates[:, 0:128], 0.5,
                            None, ALU.mult)
        phase(nc, vector=v2)




def emit_moe_up_relu(c, e, dram_w1, dram_b1, dram_w2, dram_b2, FH, t0, tw,
                     load_b=True):
    nc = c.nc
    ld0 = c.s_ld.n
    mm0 = c.s_mm.n
    sc0 = c.s_sc.n
    NFB = FH // 512
    ntg = tw // 512
    NF = FH // 128
    nb = 1 if load_b else 0

    def g(g):
        if load_b:
            c.s_ld.inc(g.dma_start(out=c.bfeat[:, 0:NF], in_=dram_b1[e]), 16)
        for fb in range(NFB):
            if fb >= 2:
                g.wait_ge(c.s_mm.h, mm0 + (fb - 1) * 4 * ntg)
            c.s_ld.inc(g.dma_start(
                out=c.wU[:, (fb % 2) * 4096:(fb % 2) * 4096 + 4096].rearrange(
                    "p (cc f) -> p cc f", f=512),
                in_=dram_w1[e].rearrange("(cc p) f -> p cc f", p=128)[
                    :, :, fb * 512:(fb + 1) * 512]), 16)
        for j in range(NF // 4):
            c.s_ld.inc(g.dma_start(
                out=c.scrA[:, A_DW + j * 4096:A_DW + (j + 1) * 4096].rearrange(
                    "p (fc cw) -> p fc cw", cw=1024),
                in_=dram_w2[e].rearrange("(fc p) cw -> p fc cw", p=128)[
                    :, j * 4:(j + 1) * 4, :]), 16)
        if dram_b2 is not None:
            c.s_ld.inc(g.dma_start(out=c.rowb[0:1, 1024:2048],
                                   in_=dram_b2[e:e + 1, :]), 16)

    def tens(t):
        for fb in range(NFB):
            t.wait_ge(c.s_ld.h, ld0 + (nb + fb + 1) * 16)
            slot = (fb % 2) * 4096
            for fs in range(4):
                for tg in range(ntg):
                    k = (fb * 4 + fs) * ntg + tg
                    bank = c.pp[k % 4]
                    if k >= 4:
                        t.wait_ge(c.s_sc.h, sc0 + k - 4 + 1)
                    for cc in range(NCC):
                        mm = t.matmul(
                            bank[:],
                            c.wU[:, slot + cc * 512 + fs * 128:
                                 slot + cc * 512 + fs * 128 + 128],
                            c.xnT[:, cc * T + t0 + tg * 512:
                                  cc * T + t0 + tg * 512 + 512],
                            start=(cc == 0), stop=(cc == NCC - 1))
                        if cc == NCC - 1:
                            c.s_mm.inc(mm)

    def sup(s):
        if load_b:
            s.wait_ge(c.s_ld.h, ld0 + 16)
        for k in range(NF * ntg):
            fc = k // ntg
            tg = k % ntg
            s.wait_ge(c.s_mm.h, mm0 + k + 1)
            c.s_sc.inc(s.activation(
                c.scrA[:, A_HT + fc * tw + tg * 512:
                       A_HT + fc * tw + tg * 512 + 512],
                c.pp[k % 4][:], AF.Relu, bias=c.bfeat[:, fc:fc + 1]))
    phase(nc, gpsimd=g, tensor=tens, scalar=sup)
    return ld0, nb + NFB + NF // 4 + (1 if dram_b2 is not None else 0)


def emit_moe_down(c, e, FH, ld0, n_dma, tw=T, t0=0, dw_col0=A_DW, fg_tiles=None,
                  add_bias=True):
    nc = c.nc
    mm0 = c.s_mm.n
    ev0 = c.s_ev.n
    ntt_g = tw // 128
    NF = FH // 128 if fg_tiles is None else fg_tiles
    f_off = 0

    def tens(t):
        t.wait_ge(c.s_ld.h, ld0 + n_dma * 16)
        for tt in range(ntt_g):
            pr = (c.pp[0], c.pp[1]) if tt % 2 == 0 else (c.pp[2], c.pp[3])
            if tt >= 2:
                t.wait_ge(c.s_ev.h, ev0 + (tt - 2) * 2 + 2)
            for fc in range(NF):
                lhs = c.scrA[:, A_HT + (f_off + fc) * tw + tt * 128:
                             A_HT + (f_off + fc) * tw + tt * 128 + 128]
                for ch in range(2):
                    mm = t.matmul(pr[ch][:], lhs,
                                  c.scrA[:, dw_col0 + fc * 1024 + ch * 512:
                                         dw_col0 + fc * 1024 + ch * 512 + 512],
                                  start=(fc == 0),
                                  stop=(fc == NF - 1 and not add_bias))
                    if fc == NF - 1 and not add_bias:
                        c.s_mm.inc(mm)
            if add_bias:
                for ch in range(2):
                    c.s_mm.inc(t.matmul(
                        pr[ch][:], c.ones1[:],
                        c.rowb[0:1, 1024 + ch * 512:1536 + ch * 512],
                        start=False, stop=True))

    def vec(v):
        for tt in range(ntt_g):
            pr = (c.pp[0], c.pp[1]) if tt % 2 == 0 else (c.pp[2], c.pp[3])
            v.wait_ge(c.s_mm.h, mm0 + (tt + 1) * 2)
            gcol = c.gates[:, 128 + (t0 // 128 + tt) * 8 + e:
                           129 + (t0 // 128 + tt) * 8 + e]
            col0 = (t0 // 128 + tt) * C
            for ch in range(2):
                c.s_ev.inc(v.scalar_tensor_tensor(
                    c.stream[:, col0 + ch * 512:col0 + ch * 512 + 512],
                    pr[ch][:], gcol,
                    c.stream[:, col0 + ch * 512:col0 + ch * 512 + 512],
                    ALU.mult, ALU.add))
    phase(nc, tensor=tens, vector=vec)


def emit_l1_ffn(c, w):
    import os
    emit_ln(c)
    emit_router(c, w["L1_gate_w"], w["L1_gate_b"], renorm=False)
    for e in range(int(os.environ.get("K_EXPERTS", "8"))):
        for tg in range(2):
            ld0, nd = emit_moe_up_relu(c, e, w["L1_w1"], w["L1_b1"],
                                       w["L1_w2"], w["L1_b2"], 2048,
                                       t0=tg * 1024, tw=1024, load_b=(tg == 0))
            emit_moe_down(c, e, 2048, ld0, nd, tw=1024, t0=tg * 1024)




def emit_attn_qkv_wave(c, wave, Wq_ap, Wk_ap, Wv_ap, bq_col, bk_col, bv_row_ap,
                      fq0, fk0, fv0, qk_scale=0.125):
    nc = c.nc
    ld0 = c.s_ld.n
    mm0 = c.s_mm.n
    sc0 = c.s_sc.n
    ev0 = c.s_ev.n

    def g(g):
        c.s_ld.inc(g.dma_start(
            out=c.wU[:, 0:4096].rearrange("p (cc f) -> p cc f", f=512),
            in_=Wq_ap.rearrange("(cc p) f -> p cc f", p=128)[:, :, fq0:fq0 + 512]), 16)
        c.s_ld.inc(g.dma_start(
            out=c.wU[:, 4096:8192].rearrange("p (cc f) -> p cc f", f=512),
            in_=Wk_ap.rearrange("(cc p) f -> p cc f", p=128)[:, :, fk0:fk0 + 512]), 16)
        c.s_ld.inc(g.dma_start(
            out=c.wD[:, 0:4096].rearrange("p (cc f) -> p cc f", f=512),
            in_=Wv_ap.rearrange("(cc p) f -> p cc f", p=128)[:, :, fv0:fv0 + 512]), 16)
        c.s_ld.inc(g.dma_start(out=c.rowb[0:1, 0:512], in_=bv_row_ap), 16)

    def tens(t):
        t.wait_ge(c.s_ld.h, ld0 + 64)
        k = 0
        for which in range(2):
            for ft in range(4):
                for tg in range(4):
                    bank = c.pp[k % 4]
                    if k >= 4:
                        t.wait_ge(c.s_sc.h, sc0 + k - 4 + 1)
                    for cc in range(NCC):
                        mm = t.matmul(
                            bank[:],
                            c.wU[:, which * 4096 + cc * 512 + ft * 128:
                                 which * 4096 + cc * 512 + ft * 128 + 128],
                            c.xnT[:, cc * T + tg * 512:cc * T + tg * 512 + 512],
                            start=(cc == 0), stop=(cc == NCC - 1))
                        if cc == NCC - 1:
                            c.s_mm.inc(mm)
                    k += 1
        for tt in range(NTT):
            bank = c.pp[4 + tt % 2]
            if tt >= 2:
                t.wait_ge(c.s_ev.h, ev0 + (tt - 2) + 1)
            for cc in range(NCC):
                t.matmul(bank[:],
                         c.xnT[:, cc * T + tt * 128:cc * T + tt * 128 + 128],
                         c.wD[:, cc * 512:(cc + 1) * 512],
                         start=(cc == 0), stop=False)
            c.s_mm.inc(t.matmul(bank[:], c.ones1[:], c.rowb[0:1, 0:512],
                                start=False, stop=True))

    def sc(s):
        s.wait_ge(c.s_ld.h, ld0 + 64)
        k = 0
        for which in range(2):
            base = A_QT if which == 0 else A_KT
            bcol = bq_col if which == 0 else bk_col
            for ft in range(4):
                for tg in range(4):
                    s.wait_ge(c.s_mm.h, mm0 + k + 1)
                    kw = {}
                    if bcol is not None:
                        kw["bias"] = bcol(ft)
                    c.s_sc.inc(s.activation(
                        c.scrA[:, base + ft * T + tg * 512:
                               base + ft * T + tg * 512 + 512],
                        c.pp[k % 4][:], AF.Identity,
                        scale=(qk_scale if which == 0 else 1.0), **kw))
                    k += 1

    def vec(v):
        for tt in range(NTT):
            v.wait_ge(c.s_mm.h, mm0 + 32 + tt + 1)
            c.s_ev.inc(v.tensor_copy(
                c.scrA[:, A_VT + tt * 512:A_VT + (tt + 1) * 512],
                c.pp[4 + tt % 2][:]))
    phase(nc, gpsimd=g, tensor=tens, scalar=sc, vector=vec)


def emit_attn_core_wave(c, causal, nheads=8, l2_ext=False, vstep=512):
    nc = c.nc
    strips = [(hl, qt, (qt + 1) * 128 if causal else T)
              for hl in range(nheads) for qt in range(NTT)]
    N = len(strips)
    nkg = [(kl + 511) // 512 for (_, _, kl) in strips]
    nkt = [(kl + 127) // 128 for (_, _, kl) in strips]
    mm0 = c.s_mm.n
    mx0 = c.s_mx.n
    sc0 = c.s_sc.n
    pe0 = c.s_pe.n
    pt0 = c.s_pt.n
    pl0 = c.s_pl.n
    tp0 = c.s_tpos.n
    ev0 = c.s_ev.n
    s1_done = {}
    s2_done = {}
    pv_done = {}
    trans_done = {}
    mm = mm0
    tpn = tp0
    for i in range(-1, N + 2):
        if 0 <= i - 2 < N:
            mm += nkt[i - 2]
            pv_done[i - 2] = mm
        if 0 <= i + 1 < N:
            for kg in range(nkg[i + 1]):
                mm += 1
                s1_done[(i + 1, kg)] = mm
        if 0 <= i < N:
            for kg in range(nkg[i]):
                mm += 1
                s2_done[(i, kg)] = mm
        if 0 <= i - 1 < N:
            for kt in range(nkt[i - 1]):
                tpn += 1
                trans_done[(i - 1, kt)] = tpn
    max_done = {}
    maxred_done = {}
    scale_done = {}
    ptev_done = {}
    ptev_all = {}
    ot_done = {}
    mx = mx0
    pe = pe0
    pt = pt0
    ev = ev0
    gidx = 0
    for i in range(N + 2):
        if i < N:
            for kg in range(nkg[i]):
                mx += 1
                max_done[(i, kg)] = mx
            mx += 1
            maxred_done[i] = mx
        if 1 <= i <= N:
            pe += 1
            scale_done[i - 1] = pe
        if 2 <= i < N + 2:
            j = i - 2
            for gq in range((nkt[j] + 7) // 8):
                pt += 1
                ptev_done[(j, gq)] = pt
            ptev_all[j] = pt
            if strips[j][1] % 4 == 3:
                ev += 1
                ot_done[gidx] = ev
                gidx += 1
    exp_done = {}
    sc = sc0
    for i in range(N):
        for kg in range(nkg[i]):
            sc += 1
            exp_done[(i, kg)] = sc
    exp_all = {i: exp_done[(i, nkg[i] - 1)] for i in range(N)}
    mask_done = {i: pl0 + i + 1 for i in range(N)}

    def qk_ap(hl, col, is_q, width):
        base = A_QT if is_q else A_KT
        if not l2_ext:
            tile = hl // 2
            row0 = (hl % 2) * 64
            return c.scrA[row0:row0 + 64, base + tile * T + col:
                          base + tile * T + col + width]
        return c.scrA[0:66, base + hl * T + col:base + hl * T + col + width]

    csum_kg = [int(v) for v in np.cumsum([0] + nkg)]

    def gof(i, kg):
        return csum_kg[i] + kg

    def prev_same_bank_s1(i, kg):
        g = gof(i, kg) - 2
        if g < 0:
            return None
        s = int(np.searchsorted(csum_kg, g, side="right")) - 1
        return (s, g - csum_kg[s])

    def tens(t):
        for i in range(-1, N + 2):
            if 0 <= i - 2 < N:
                j = i - 2
                hl, qt, kl = strips[j]
                t.wait_ge(c.s_pt.h, ptev_all[j])
                gj = j // 4
                if gj >= 2:
                    t.wait_ge(c.s_ev.h, ot_done[gj - 2])
                obank = c.pp[4 + gj % 2]
                tbuf = B_T0 if j % 2 == 0 else B_T1
                for kt in range(nkt[j]):
                    c.s_mm.inc(t.matmul(
                        obank[0:64, (qt % 4) * 128:(qt % 4) * 128 + 128],
                        c.scrA[:, A_VT + kt * vstep + hl * 64:
                               A_VT + kt * vstep + hl * 64 + 64],
                        c.scrB[:, tbuf + kt * 128:tbuf + (kt + 1) * 128],
                        start=(kt == 0), stop=(kt == nkt[j] - 1)))
            if 0 <= i + 1 < N:
                hl, qt, kl = strips[i + 1]
                for kg in range(nkg[i + 1]):
                    wdt = min(512, kl - kg * 512)
                    pb = prev_same_bank_s1(i + 1, kg)
                    if pb is not None:
                        t.wait_ge(c.s_mx.h, max_done[pb])
                    c.s_mm.inc(t.matmul(
                        c.pp[gof(i + 1, kg) % 2][:, 0:wdt],
                        qk_ap(hl, qt * 128, True, 128),
                        qk_ap(hl, kg * 512, False, wdt),
                        start=True, stop=True))
            if 0 <= i < N:
                hl, qt, kl = strips[i]
                for kg in range(nkg[i]):
                    wdt = min(512, kl - kg * 512)
                    pb = prev_same_bank_s1(i, kg)
                    if pb is not None:
                        t.wait_ge(c.s_sc.h, exp_done[pb])
                    c.s_mm.inc(t.matmul(
                        c.pp[2 + gof(i, kg) % 2][:, 0:wdt],
                        qk_ap(hl, qt * 128, True, 128),
                        qk_ap(hl, kg * 512, False, wdt),
                        start=True, stop=True))
            if 0 <= i - 1 < N:
                j = i - 1
                t.wait_ge(c.s_pe.h, scale_done[j])
                if j >= 2:
                    t.wait_ge(c.s_pt.h, ptev_all[j - 1])
                pbuf = B_P0 if j % 2 == 0 else B_P1
                for kt in range(nkt[j]):
                    bank = c.tp[kt // 8]
                    c.s_tpos.inc(t.transpose(
                        bank[:, (kt % 8) * 128:(kt % 8 + 1) * 128],
                        c.scrB[:, pbuf + kt * 128:pbuf + (kt + 1) * 128],
                        c.ident[:]))

    def vec(v):
        for i in range(N + 2):
            if i < N:
                if i >= 2:
                    v.wait_ge(c.s_sc.h, exp_all[i - 2])
                for kg in range(nkg[i]):
                    wdt = min(512, strips[i][2] - kg * 512)
                    v.wait_ge(c.s_mm.h, s1_done[(i, kg)])
                    c.s_mx.inc(v.reduce_max(
                        c.stats[:, (i % 2) * 4 + kg:(i % 2) * 4 + kg + 1],
                        c.pp[gof(i, kg) % 2][:, 0:wdt], axis=AX.X))
                v.wait_ge(c.s_mx.h, c.s_mx.n)
                c.s_mx.inc(v.reduce_max(
                    c.stats[:, 8 + i % 2:9 + i % 2],
                    c.stats[:, (i % 2) * 4:(i % 2) * 4 + nkg[i]],
                    axis=AX.X, negate=True))
            if 1 <= i <= N:
                j = i - 1
                hl, qt, kl = strips[j]
                pbuf = B_P0 if j % 2 == 0 else B_P1
                v.wait_ge(c.s_sc.h, exp_all[j])
                if causal:
                    v.wait_ge(c.s_pl.h, mask_done[j])
                    dkg = nkg[j] - 1
                    c.vser(v, v.reduce_sum(
                        c.stats[:, 10 + (j % 2) * 4 + dkg:11 + (j % 2) * 4 + dkg],
                        c.scrB[:, pbuf + dkg * 512:pbuf + kl], axis=AX.X))
                c.vser(v, v.reduce_sum(
                    c.stats[:, 18 + j % 2:19 + j % 2],
                    c.stats[:, 10 + (j % 2) * 4:10 + (j % 2) * 4 + nkg[j]],
                    axis=AX.X))
                c.vser(v, v.reciprocal(c.stats[:, 20 + j % 2:21 + j % 2],
                                       c.stats[:, 18 + j % 2:19 + j % 2]))
                c.s_pe.inc(v.tensor_scalar(
                    c.scrB[:, pbuf:pbuf + kl], c.scrB[:, pbuf:pbuf + kl],
                    c.stats[:, 20 + j % 2:21 + j % 2], None, ALU.mult))
            if 2 <= i < N + 2:
                j = i - 2
                hl, qt, kl = strips[j]
                tbuf = B_T0 if j % 2 == 0 else B_T1
                if j >= 2:
                    v.wait_ge(c.s_mm.h, pv_done[j - 2])
                for gq in range((nkt[j] + 7) // 8):
                    hi = min(nkt[j], (gq + 1) * 8)
                    v.wait_ge(c.s_tpos.h, trans_done[(j, hi - 1)])
                    wdt = (hi - gq * 8) * 128
                    bank = c.tp[gq]
                    c.s_pt.inc(v.tensor_copy(
                        c.scrB[:, tbuf + gq * 1024:tbuf + gq * 1024 + wdt],
                        bank[:, 0:wdt]))
                if qt % 4 == 3:
                    v.wait_ge(c.s_mm.h, pv_done[j])
                    gj = j // 4
                    obank = c.pp[4 + gj % 2]
                    ft = hl // 2
                    r0 = (hl % 2) * 64
                    c.s_ev.inc(v.tensor_copy(
                        c.scrA[r0:r0 + 64,
                               A_OT + ft * T + (qt - 3) * 128:
                               A_OT + ft * T + (qt + 1) * 128],
                        obank[0:64, 0:512]))

    def scal(s):
        for i in range(N):
            hl, qt, kl = strips[i]
            pbuf = B_P0 if i % 2 == 0 else B_P1
            s.wait_ge(c.s_mx.h, maxred_done[i])
            if i >= 2:
                s.wait_ge(c.s_tpos.h, trans_done[(i - 2, nkt[i - 2] - 1)])
            for kg in range(nkg[i]):
                wdt = min(512, kl - kg * 512)
                s.wait_ge(c.s_mm.h, s2_done[(i, kg)])
                c.s_sc.inc(s.activation(
                    c.scrB[:, pbuf + kg * 512:pbuf + kg * 512 + wdt],
                    c.pp[2 + gof(i, kg) % 2][:, 0:wdt],
                    AF.Exp,
                    bias=c.stats[:, 8 + i % 2:9 + i % 2],
                    accum_out=c.stats[:, 10 + (i % 2) * 4 + kg:
                                      11 + (i % 2) * 4 + kg]))

    def pool(g):
        if not causal:
            return
        for i in range(N):
            hl, qt, kl = strips[i]
            pbuf = B_P0 if i % 2 == 0 else B_P1
            g.wait_ge(c.s_sc.h, exp_all[i])
            c.s_pl.inc(g.affine_select(
                c.scrB[:, pbuf + qt * 128:pbuf + kl],
                c.scrB[:, pbuf + qt * 128:pbuf + kl],
                [[-1, 128]], ALU.is_ge, 0.0, base=0, channel_multiplier=1))
    if causal:
        phase(nc, tensor=tens, vector=vec, scalar=scal, gpsimd=pool)
    else:
        phase(nc, tensor=tens, vector=vec, scalar=scal)
    assert c.s_mm.n == mm, (c.s_mm.n, mm)
    assert c.s_tpos.n == tpn
    assert c.s_mx.n >= mx - mx0 + mx0
    assert c.s_pe.n == pe
    assert c.s_pt.n == pt
    assert c.s_ev.n == ev


def emit_attn_out_wave(c, wave, Wp_ap, brow_ap, nft=4):
    nc = c.nc
    ld0 = c.s_ld.n
    mm0 = c.s_mm.n
    ev0 = c.s_ev.n
    use_bias = (wave == 0 and brow_ap is not None)

    def g(g):
        c.s_ld.inc(g.dma_start(
            out=c.wD[:, 0:nft * 1024].rearrange("p (fc cw) -> p fc cw", cw=1024),
            in_=Wp_ap.rearrange("(fc p) cw -> p fc cw", p=128)[
                :, wave * nft:wave * nft + nft, :]), 16)
        if use_bias:
            c.s_ld.inc(g.dma_start(out=c.rowb[0:1, 0:1024], in_=brow_ap), 16)

    def tens(t):
        t.wait_ge(c.s_ld.h, ld0 + (32 if use_bias else 16))
        for tt in range(NTT):
            pr = (c.pp[0], c.pp[1]) if tt % 2 == 0 else (c.pp[2], c.pp[3])
            if tt >= 2:
                t.wait_ge(c.s_ev.h, ev0 + (tt - 2) * 2 + 2)
            for fc in range(nft):
                lhs = c.scrA[:, A_OT + fc * T + tt * 128:
                             A_OT + fc * T + tt * 128 + 128]
                for ch in range(2):
                    mm = t.matmul(pr[ch][:], lhs,
                                  c.wD[:, fc * 1024 + ch * 512:
                                       fc * 1024 + ch * 512 + 512],
                                  start=(fc == 0),
                                  stop=(fc == nft - 1 and not use_bias))
                    if fc == nft - 1 and not use_bias:
                        c.s_mm.inc(mm)
            if use_bias:
                for ch in range(2):
                    c.s_mm.inc(t.matmul(pr[ch][:], c.ones1[:],
                                        c.rowb[0:1, ch * 512:ch * 512 + 512],
                                        start=False, stop=True))

    def vec(v):
        for tt in range(NTT):
            pr = (c.pp[0], c.pp[1]) if tt % 2 == 0 else (c.pp[2], c.pp[3])
            v.wait_ge(c.s_mm.h, mm0 + (tt + 1) * 2)
            for ch in range(2):
                c.s_ev.inc(v.tensor_tensor(
                    c.stream[:, tt * C + ch * 512:tt * C + ch * 512 + 512],
                    c.stream[:, tt * C + ch * 512:tt * C + ch * 512 + 512],
                    pr[ch][:], ALU.add))
    phase(nc, gpsimd=g, tensor=tens, vector=vec)


def emit_l1_attn(c, w):
    import os
    emit_ln(c)
    nwaves = int(os.environ.get("K_ATTN_WAVES", "2"))
    for wave in range(nwaves):
        def bq(ft, wave=wave):
            return c.bfeat[:, wave * 4 + ft:wave * 4 + ft + 1]

        def bk(ft, wave=wave):
            return c.bfeat[:, 8 + wave * 4 + ft:8 + wave * 4 + ft + 1]
        if wave == 0:
            ld0 = c.s_ld.n

            def gb(g):
                c.s_ld.inc(g.dma_start(out=c.bfeat[:, 0:16],
                                       in_=w["L1_battn_feat"]), 16)
                g.wait_ge(c.s_ld.h, ld0 + 16)

            def gv(v):
                v.wait_ge(c.s_ld.h, ld0 + 16)
                v.tensor_scalar(c.bfeat[:, 0:8], c.bfeat[:, 0:8], 0.125,
                                None, ALU.mult)
            phase(c.nc, gpsimd=gb, vector=gv)
        emit_attn_qkv_wave(
            c, wave, w["L1_W_attn"], w["L1_W_attn"], w["L1_W_attn"],
            bq, bk, w["L1_bv_row"][:, wave * 512:wave * 512 + 512],
            fq0=wave * 512, fk0=1024 + wave * 512, fv0=2048 + wave * 512)
        if os.environ.get("K_ATTN_QKV_ONLY"):
            continue
        emit_attn_core_wave(c, causal=True,
                            nheads=int(os.environ.get("K_ATTN_HEADS", "8")))
        if os.environ.get("K_ATTN_NO_OUT"):
            continue
        emit_attn_out_wave(c, wave, w["L1_W_proj"], w["L1_bproj_row"])


def build_l1(c, w):
    import os
    parts = os.environ.get("K_PARTS", "fa f")
    for p in parts.split():
        for ch in p:
            if ch == "f":
                emit_l1_ffn(c, w)
            elif ch == "a":
                emit_l1_attn(c, w)
            elif ch == "n":
                emit_ln(c)
            elif ch == "r":
                emit_ln(c)
                emit_router(c, w["L1_gate_w"], w["L1_gate_b"], renorm=False)





def emit_moe_up_glu(c, e, dram_gw, dram_uw, dram_dw, FH2, fh, t0, tw):
    nc = c.nc
    ld0 = c.s_ld.n
    mm0 = c.s_mm.n
    sc0 = c.s_sc.n
    ev0 = c.s_ev.n
    pl0 = c.s_pl.n
    f0 = fh * 2048
    NFB = 2048 // 256
    ntg = tw // 512

    def dma_chunk(g, fb):
        for which, dw in ((0, dram_gw), (1, dram_uw)):
            c.s_ld.inc(g.dma_start(
                out=c.wU[:, which * 4096 + (fb % 2) * 2048:
                         which * 4096 + (fb % 2) * 2048 + 2048].rearrange(
                    "p (cc f) -> p cc f", f=256),
                in_=dw[e].rearrange("(cc p) f -> p cc f", p=128)[
                    :, :, f0 + fb * 256:f0 + (fb + 1) * 256]), 16)

    def g(g):
        for fb in range(2):
            dma_chunk(g, fb)
        for j in range(4):
            c.s_ld.inc(g.dma_start(
                out=c.scrA[:, A_DW + j * 4096:A_DW + (j + 1) * 4096].rearrange(
                    "p (fc cw) -> p fc cw", cw=1024),
                in_=dram_dw[e].rearrange("(fc p) cw -> p fc cw", p=128)[
                    :, fh * 16 + j * 4:fh * 16 + (j + 1) * 4, :]), 16)

    def tens(t):
        for fb in range(NFB):
            t.wait_ge(c.s_ld.h, ld0 + (fb + 1) * 32 + (64 if fb >= 2 else 0))
            for fs in range(2):
                for tg in range(ntg):
                    k = (fb * 2 + fs) * ntg + tg
                    pg = c.pp[2 * (k % 2)]
                    pu = c.pp[2 * (k % 2) + 1]
                    if k >= 2:
                        t.wait_ge(c.s_pl.h, pl0 + k - 2 + 1)
                    for which, bank in ((0, pg), (1, pu)):
                        for cc in range(NCC):
                            mm = t.matmul(
                                bank[:],
                                c.wU[:, which * 4096 + (fb % 2) * 2048 +
                                     cc * 256 + fs * 128:
                                     which * 4096 + (fb % 2) * 2048 +
                                     cc * 256 + fs * 128 + 128],
                                c.xnT[:, cc * T + t0 + tg * 512:
                                      cc * T + t0 + tg * 512 + 512],
                                start=(cc == 0), stop=(cc == NCC - 1))
                            if which == 1 and cc == NCC - 1:
                                c.s_mm.inc(mm)

    def scal(s):
        for k in range(NFB * 2 * ntg):
            pg = c.pp[2 * (k % 2)]
            s.wait_ge(c.s_mm.h, mm0 + k + 1)
            if k >= 2:
                s.wait_ge(c.s_pl.h, pl0 + k - 2 + 1)
            c.s_sc.inc(s.activation(
                c.scrB[:, (k % 2) * 512:(k % 2) * 512 + 512],
                pg[:], AF.Silu))

    def vec(v):
        for k in range(NFB * 2 * ntg):
            pu = c.pp[2 * (k % 2) + 1]
            v.wait_ge(c.s_mm.h, mm0 + k + 1)
            if k >= 2:
                v.wait_ge(c.s_pl.h, pl0 + k - 2 + 1)
            c.s_ev.inc(v.tensor_copy(
                c.scrB[:, 1024 + (k % 2) * 512:1024 + (k % 2) * 512 + 512],
                pu[:]))

    def pool(g):
        upf = 2 * ntg
        for k in range(NFB * 2 * ntg):
            fc = k // ntg
            tg = k % ntg
            g.wait_ge(c.s_sc.h, sc0 + k + 1)
            g.wait_ge(c.s_ev.h, ev0 + k + 1)
            c.s_pl.inc(g.tensor_tensor(
                c.scrA[:, A_HT + fc * tw + tg * 512:
                       A_HT + fc * tw + tg * 512 + 512],
                c.scrB[:, (k % 2) * 512:(k % 2) * 512 + 512],
                c.scrB[:, 1024 + (k % 2) * 512:1024 + (k % 2) * 512 + 512],
                ALU.mult))
            fb = k // upf
            if k % upf == upf - 1 and fb + 2 < NFB:
                dma_chunk(g, fb + 2)
    with nc.Block() as blk:
        blk.gpsimd(lambda gg: (g(gg), pool(gg)))
        blk.tensor(tens)
        blk.scalar(scal)
        blk.vector(vec)
    return ld0, 2 * NFB + 4


def emit_l2_ffn(c, w):
    import os
    emit_ln(c)
    emit_router(c, w["L2_router_w"], w["L2_router_b"], renorm=True)
    for e in range(int(os.environ.get("K_EXPERTS", "8"))):
        for fh in range(2):
            for tg in range(2):
                ld0, nd = emit_moe_up_glu(c, e, w["L2_gate_w"], w["L2_up_w"],
                                          w["L2_down_w"], 4096, fh,
                                          t0=tg * 1024, tw=1024)
                emit_moe_down(c, e, 2048, ld0, nd, tw=1024, t0=tg * 1024,
                              add_bias=False)


def emit_attn_qkv_wave_l2(c, w, wave):
    nc = c.nc
    ld0 = c.s_ld.n
    mm0 = c.s_mm.n
    sc0 = c.s_sc.n
    ev0 = c.s_ev.n
    f0 = wave * 256

    def g(g):
        c.s_ld.inc(g.dma_start(
            out=c.wU[:, 0:2048].rearrange("p (cc f) -> p cc f", f=256),
            in_=w["L2_Wq"].rearrange("(cc p) f -> p cc f", p=128)[
                :, :, f0:f0 + 256]), 16)
        c.s_ld.inc(g.dma_start(
            out=c.wU[:, 2048:4096].rearrange("p (cc f) -> p cc f", f=256),
            in_=w["L2_Wk"].rearrange("(cc p) f -> p cc f", p=128)[
                :, :, f0:f0 + 256]), 16)
        c.s_ld.inc(g.dma_start(
            out=c.wD[:, 0:2048].rearrange("p (cc f) -> p cc f", f=256),
            in_=w["L2_Wv"].rearrange("(cc p) f -> p cc f", p=128)[
                :, :, f0:f0 + 256]), 16)
        for hl in range(4):
            h = wave * 4 + hl
            c.s_ld.inc(g.dma_start(
                out=c.scrA[64:66, A_QT + hl * T:A_QT + (hl + 1) * T],
                in_=w["L2_aq"][h]), 16)
            c.s_ld.inc(g.dma_start(
                out=c.scrA[64:66, A_KT + hl * T:A_KT + (hl + 1) * T],
                in_=w["L2_ak"][h]), 16)

    def tens(t):
        t.wait_ge(c.s_ld.h, ld0 + 176)
        k = 0
        for which in range(2):
            for ft in range(2):
                for tg in range(4):
                    bank = c.pp[k % 4]
                    if k >= 4:
                        t.wait_ge(c.s_sc.h, sc0 + (k - 4 + 1) * 2)
                    for cc in range(NCC):
                        mm = t.matmul(
                            bank[:],
                            c.wU[:, which * 2048 + cc * 256 + ft * 128:
                                 which * 2048 + cc * 256 + ft * 128 + 128],
                            c.xnT[:, cc * T + tg * 512:cc * T + tg * 512 + 512],
                            start=(cc == 0), stop=(cc == NCC - 1))
                        if cc == NCC - 1:
                            c.s_mm.inc(mm)
                    k += 1
        for tt in range(NTT):
            bank = c.pp[4 + tt % 2]
            if tt >= 2:
                t.wait_ge(c.s_ev.h, ev0 + (tt - 2) + 1)
            for cc in range(NCC):
                mm = t.matmul(bank[:, 0:256],
                              c.xnT[:, cc * T + tt * 128:cc * T + tt * 128 + 128],
                              c.wD[:, cc * 256:(cc + 1) * 256],
                              start=(cc == 0), stop=(cc == NCC - 1))
                if cc == NCC - 1:
                    c.s_mm.inc(mm)

    def scal(s):
        k = 0
        for which in range(2):
            base = A_QT if which == 0 else A_KT
            for ft in range(2):
                for tg in range(4):
                    s.wait_ge(c.s_mm.h, mm0 + k + 1)
                    for sub in range(2):
                        hl = ft * 2 + sub
                        c.s_sc.inc(s.activation(
                            c.scrA[0:64, base + hl * T + tg * 512:
                                   base + hl * T + tg * 512 + 512],
                            c.pp[k % 4][sub * 64:sub * 64 + 64, :],
                            AF.Identity,
                            scale=(0.125 if which == 0 else 1.0)))
                    k += 1

    def vec(v):
        for tt in range(NTT):
            v.wait_ge(c.s_mm.h, mm0 + 16 + tt + 1)
            c.s_ev.inc(v.tensor_copy(
                c.scrA[:, A_VT + tt * 256:A_VT + (tt + 1) * 256],
                c.pp[4 + tt % 2][:, 0:256]))
    phase(nc, gpsimd=g, tensor=tens, scalar=scal, vector=vec)


def emit_l2_attn(c, w):
    import os
    emit_ln(c)
    nwaves = int(os.environ.get("K_ATTN_WAVES4", "4"))
    for wave in range(nwaves):
        emit_attn_qkv_wave_l2(c, w, wave)
        emit_attn_core_wave(c, causal=True, nheads=4, l2_ext=True, vstep=256)
        emit_attn_out_wave(c, wave, w["L2_Wo"],
                           w["L2_bo_row"] if wave == 0 else None, nft=2)


def build_l2(c, w):
    import os
    parts = os.environ.get("K_PARTS", "fa f")
    for p in parts.split():
        for ch in p:
            if ch == "f":
                emit_l2_ffn(c, w)
            elif ch == "a":
                emit_l2_attn(c, w)
            elif ch == "n":
                emit_ln(c)




def emit_l3_rope(c, which):
    nc = c.nc
    base = A_QT if which == 0 else A_KT
    COS = c.scrBf[:, 2048:4096]
    SIN = c.scrBf[:, 0:2048]

    def vec_rope(v):
        for ft in range(4):
            X = c.scrA
            x0 = base + ft * T
            v.tensor_tensor(c.scrB[0:64, 4096:6144], X[0:64, x0:x0 + T],
                            COS[0:64, :], ALU.mult)
            v.tensor_tensor(c.scrB[0:64, 6144:8192], X[64:128, x0:x0 + T],
                            SIN[64:128, :], ALU.mult)
            v.tensor_tensor(c.scrB[64:128, 4096:6144], X[0:64, x0:x0 + T],
                            SIN[0:64, :], ALU.mult)
            v.tensor_tensor(c.scrB[64:128, 6144:8192], X[64:128, x0:x0 + T],
                            COS[64:128, :], ALU.mult)
            v.tensor_tensor(X[0:64, x0:x0 + T], c.scrB[0:64, 4096:6144],
                            c.scrB[0:64, 6144:8192], ALU.subtract)
            v.tensor_tensor(X[64:128, x0:x0 + T], c.scrB[64:128, 4096:6144],
                            c.scrB[64:128, 6144:8192], ALU.add)
    phase(nc, vector=vec_rope)


def emit_l3_rms(c, w, which, wave):
    nc = c.nc
    base = A_QT if which == 0 else A_KT
    for half in range(2):
        fts = (2 * half, 2 * half + 1)

        def pA(s, fts=fts):
            for j, ft in enumerate(fts):
                s.activation(c.scrB[:, 4096 + j * 2048:4096 + j * 2048 + 2048],
                             c.scrA[:, base + ft * T:base + (ft + 1) * T],
                             AF.Square)
        phase(nc, scalar=pA)

        mm0 = c.s_mm.n
        ev0 = c.s_ev.n

        def pB_t(t, fts=fts, mm0=mm0, ev0=ev0):
            k = 0
            for j, ft in enumerate(fts):
                for tg in range(4):
                    if k >= 2:
                        t.wait_ge(c.s_ev.h, ev0 + k - 2 + 1)
                    c.s_mm.inc(t.matmul(
                        c.pp[4 + k % 2][0:2, 0:512], c.csel[:, 4:6],
                        c.scrB[:, 4096 + j * 2048 + tg * 512:
                               4096 + j * 2048 + tg * 512 + 512],
                        start=True, stop=True))
                    k += 1

        def pB_v(v, fts=fts, mm0=mm0):
            k = 0
            for j, ft in enumerate(fts):
                for tg in range(4):
                    v.wait_ge(c.s_mm.h, mm0 + k + 1)
                    c.s_ev.inc(v.tensor_copy(
                        c.scrBf[0:2, j * 2048 + tg * 512:
                                j * 2048 + tg * 512 + 512],
                        c.pp[4 + k % 2][0:2, 0:512]))
                    k += 1
        phase(nc, tensor=pB_t, vector=pB_v)

        def pC(v):
            v.tensor_scalar(c.scrBf[0:2, 0:4096], c.scrBf[0:2, 0:4096],
                            1.0 / 64, 1e-6, ALU.mult, ALU.add)
        phase(nc, vector=pC)

        def pD(s):
            s.activation(c.scrBf[0:2, 0:4096], c.scrBf[0:2, 0:4096], AF.Sqrt)
        phase(nc, scalar=pD)

        def pE(v):
            with nc.allow_low_precision(reason="rstd feeds bf16 matmul"):
                v.reciprocal(c.rowb[0:2, 0:2048], c.scrBf[0:2, 0:2048])
                v.reciprocal(c.scrB[0:2, 4096:6144], c.scrBf[0:2, 2048:4096])
        phase(nc, vector=pE)

        mm1 = c.s_mm.n
        ev1 = c.s_ev.n

        def pF_t(t, fts=fts, mm1=mm1, ev1=ev1):
            k = 0
            for j, ft in enumerate(fts):
                rstd = c.rowb[0:2, 0:2048] if j == 0 else c.scrB[0:2, 4096:6144]
                for tg in range(4):
                    if k >= 2:
                        t.wait_ge(c.s_ev.h, ev1 + k - 2 + 1)
                    c.s_mm.inc(t.matmul(
                        c.pp[4 + k % 2][:, 0:512], c.rsel[:],
                        rstd[:, tg * 512:tg * 512 + 512],
                        start=True, stop=True))
                    k += 1

        def pF_v(v, fts=fts, mm1=mm1):
            k = 0
            for j, ft in enumerate(fts):
                x0 = base + ft * T
                for tg in range(4):
                    v.wait_ge(c.s_mm.h, mm1 + k + 1)
                    c.s_ev.inc(v.tensor_tensor(
                        c.scrA[:, x0 + tg * 512:x0 + tg * 512 + 512],
                        c.scrA[:, x0 + tg * 512:x0 + tg * 512 + 512],
                        c.pp[4 + k % 2][:, 0:512], ALU.mult))
                    k += 1
                gc = c.bfeat[:, 16 + which * 8 + wave * 4 + ft:
                             17 + which * 8 + wave * 4 + ft]
                v.tensor_scalar(c.scrA[:, x0:x0 + T], c.scrA[:, x0:x0 + T],
                                gc, None, ALU.mult)
        phase(nc, tensor=pF_t, vector=pF_v)


def emit_l3_attn(c, w):
    import os
    emit_ln(c)
    ld0 = c.s_ld.n

    def gb(g):
        c.s_ld.inc(g.dma_start(out=c.bfeat[:, 16:24], in_=w["L3_qg_col"]), 16)
        c.s_ld.inc(g.dma_start(out=c.bfeat[:, 24:32], in_=w["L3_kg_col"]), 16)
        c.s_ld.inc(g.dma_start(out=c.bfeat[:, 32:48], in_=w["L3_b_qkv_feat"]), 16)
        g.wait_ge(c.s_ld.h, ld0 + 48)

    def gv(v):
        v.wait_ge(c.s_ld.h, ld0 + 48)
        v.tensor_scalar(c.bfeat[:, 16:24], c.bfeat[:, 16:24], 0.125,
                        None, ALU.mult)
    phase(c.nc, gpsimd=gb, vector=gv)
    nwaves = int(os.environ.get("K_ATTN_WAVES", "2"))
    for wave in range(nwaves):
        def bq(ft, wave=wave):
            return c.bfeat[:, 32 + wave * 4 + ft:33 + wave * 4 + ft]

        def bk(ft, wave=wave):
            return c.bfeat[:, 40 + wave * 4 + ft:41 + wave * 4 + ft]
        emit_attn_qkv_wave(
            c, wave, w["L3_w_qkv"], w["L3_w_qkv"], w["L3_w_qkv"],
            bq, bk, w["L3_bv_row"][:, wave * 512:wave * 512 + 512],
            fq0=wave * 512, fk0=1024 + wave * 512, fv0=2048 + wave * 512,
            qk_scale=1.0)
        ld1 = c.s_ld.n

        def gcs(g, ld1=ld1):
            c.s_ld.inc(g.dma_start(out=c.scrBf[:, 2048:4096], in_=w["L3_cos"]), 16)
            c.s_ld.inc(g.dma_start(out=c.scrBf[:, 0:2048], in_=w["L3_sin"]), 16)
            g.wait_ge(c.s_ld.h, ld1 + 32)
        phase(c.nc, gpsimd=gcs)
        emit_l3_rope(c, 0)
        emit_l3_rope(c, 1)
        emit_l3_rms(c, w, 0, wave)
        emit_l3_rms(c, w, 1, wave)
        if os.environ.get("K_ATTN_QKV_ONLY"):
            continue
        emit_attn_core_wave(c, causal=False,
                            nheads=int(os.environ.get("K_ATTN_HEADS", "8")))
        if os.environ.get("K_ATTN_NO_OUT"):
            continue
        emit_attn_out_wave(c, wave, w["L3_w_out"],
                           w["L3_bout_row"] if wave == 0 else None)


def emit_l3_prenorm(c, w):
    nc = c.nc
    ld0 = c.s_ld.n
    mm0 = c.s_mm.n

    def g(g):
        c.s_ld.inc(g.dma_start(out=c.rowb[0:1, 0:1024],
                               in_=w["L3_norm_scale_row"]), 16)

    def p1s(s):
        for tt in range(NTT):
            s.activation(c.scrBf[:, (tt % 4) * 1024:(tt % 4) * 1024 + 1024],
                         c.scrA[:, A_XN + tt * C:A_XN + (tt + 1) * C],
                         AF.Square, accum_out=c.stats[:, 16 + tt:17 + tt])

    def p1v(v):
        for tt in range(NTT):
            v.scalar_tensor_tensor(
                c.stream[:, tt * C:(tt + 1) * C],
                c.scrA[:, A_XN + tt * C:A_XN + (tt + 1) * C], 0.5,
                c.stream[:, tt * C:(tt + 1) * C], ALU.mult, ALU.add)

    def p1t(t):
        t.wait_ge(c.s_ld.h, ld0 + 16)
        for ch in range(2):
            c.s_mm.inc(t.matmul(c.pp[4 + ch][:], c.ones1[:],
                                c.rowb[0:1, ch * 512:ch * 512 + 512],
                                start=True, stop=True))
    phase(nc, gpsimd=g, scalar=p1s, vector=p1v, tensor=p1t)

    def p2v(v):
        c.vser(v, v.tensor_scalar(c.stats[:, 16:32], c.stats[:, 16:32],
                                  1.0 / C, 1e-5, ALU.mult, ALU.add))
    phase(nc, vector=p2v)

    def p3s(s):
        s.activation(c.stats[:, 16:32], c.stats[:, 16:32], AF.Sqrt)
    phase(nc, scalar=p3s)

    def p4v(v):
        c.vser(v, v.reciprocal(c.stats[:, 48:64], c.stats[:, 16:32]))
        c.vser(v, v.tensor_copy(c.scrB[:, 0:512], c.pp[4][:]))
        c.vser(v, v.tensor_copy(c.scrB[:, 512:1024], c.pp[5][:]))
        for tt in range(NTT):
            v.tensor_scalar(
                c.scrA[:, A_XN + tt * C:A_XN + (tt + 1) * C],
                c.scrA[:, A_XN + tt * C:A_XN + (tt + 1) * C],
                c.stats[:, 48 + tt:49 + tt], None, ALU.mult)
        for tt in range(NTT):
            v.tensor_tensor(
                c.scrA[:, A_XN + tt * C:A_XN + (tt + 1) * C],
                c.scrA[:, A_XN + tt * C:A_XN + (tt + 1) * C],
                c.scrB[:, 0:1024], ALU.mult)
    phase(nc, vector=p4v)
    emit_transpose_to(c, c.scrA, A_XN, c.xnT, 0)


def emit_moe_up_swiglu(c, e, dram_w1, dram_b1, dram_w2, dram_b2, t0, tw,
                       load_b=True):
    nc = c.nc
    ld0 = c.s_ld.n
    mm0 = c.s_mm.n
    sc0 = c.s_sc.n
    ev0 = c.s_ev.n
    pl0 = c.s_pl.n
    NFB = 2048 // 256
    ntg = tw // 512
    nb = 1 if load_b else 0

    def dma_chunk(g, fb):
        for which in range(2):
            c.s_ld.inc(g.dma_start(
                out=c.wU[:, which * 4096 + (fb % 2) * 2048:
                         which * 4096 + (fb % 2) * 2048 + 2048].rearrange(
                    "p (cc f) -> p cc f", f=256),
                in_=dram_w1[e].rearrange("(cc p) f -> p cc f", p=128)[
                    :, :, which * 2048 + fb * 256:
                    which * 2048 + (fb + 1) * 256]), 16)

    def g(g):
        if load_b:
            c.s_ld.inc(g.dma_start(out=c.bfeat[:, 0:32], in_=dram_b1[e]), 16)
        for fb in range(2):
            dma_chunk(g, fb)
        for j in range(4):
            c.s_ld.inc(g.dma_start(
                out=c.scrA[:, A_DW + j * 4096:A_DW + (j + 1) * 4096].rearrange(
                    "p (fc cw) -> p fc cw", cw=1024),
                in_=dram_w2[e].rearrange("(fc p) cw -> p fc cw", p=128)[
                    :, j * 4:(j + 1) * 4, :]), 16)
        c.s_ld.inc(g.dma_start(out=c.rowb[0:1, 1024:2048],
                               in_=dram_b2[e:e + 1, :]), 16)

    def tens(t):
        for fb in range(NFB):
            t.wait_ge(c.s_ld.h,
                      ld0 + (nb + 2 * (fb + 1) + (5 if fb >= 2 else 0)) * 16)
            for fs in range(2):
                for tg in range(ntg):
                    k = (fb * 2 + fs) * ntg + tg
                    pg = c.pp[2 * (k % 2)]
                    pl_ = c.pp[2 * (k % 2) + 1]
                    if k >= 2:
                        t.wait_ge(c.s_pl.h, pl0 + k - 2 + 1)
                    for which, bank in ((0, pg), (1, pl_)):
                        for cc in range(NCC):
                            mm = t.matmul(
                                bank[:],
                                c.wU[:, which * 4096 + (fb % 2) * 2048 +
                                     cc * 256 + fs * 128:
                                     which * 4096 + (fb % 2) * 2048 +
                                     cc * 256 + fs * 128 + 128],
                                c.xnT[:, cc * T + t0 + tg * 512:
                                      cc * T + t0 + tg * 512 + 512],
                                start=(cc == 0), stop=(cc == NCC - 1))
                            if which == 1 and cc == NCC - 1:
                                c.s_mm.inc(mm)

    def vec(v):
        for k in range(NFB * 2 * ntg):
            fc = k // ntg
            pg = c.pp[2 * (k % 2)]
            pl_ = c.pp[2 * (k % 2) + 1]
            v.wait_ge(c.s_mm.h, mm0 + k + 1)
            if k >= 2:
                v.wait_ge(c.s_pl.h, pl0 + k - 2 + 1)
            v.tensor_scalar(c.scrB[:, (k % 2) * 512:(k % 2) * 512 + 512],
                            pg[:], c.bfeat[:, fc:fc + 1], 7.0,
                            ALU.add, ALU.min)
            c.s_ev.inc(v.tensor_scalar(
                c.scrB[:, 1024 + (k % 2) * 512:1024 + (k % 2) * 512 + 512],
                pl_[:], c.bfeat[:, 16 + fc:17 + fc], 7.0,
                ALU.add, ALU.min))

    def scal(s):
        s.wait_ge(c.s_ld.h, ld0 + 16)
        for k in range(NFB * 2 * ntg):
            s.wait_ge(c.s_ev.h, ev0 + k + 1)
            if k >= 2:
                s.wait_ge(c.s_pl.h, pl0 + k - 2 + 1)
            c.s_sc.inc(s.activation(
                c.scrB[:, 2048 + (k % 2) * 512:2048 + (k % 2) * 512 + 512],
                c.scrB[:, (k % 2) * 512:(k % 2) * 512 + 512],
                AF.Gelu_apprx_sigmoid))

    def pool(g):
        for k in range(NFB * 2 * ntg):
            fc = k // ntg
            tg = k % ntg
            g.wait_ge(c.s_sc.h, sc0 + k + 1)
            g.tensor_scalar(
                c.scrB[:, 1024 + (k % 2) * 512:1024 + (k % 2) * 512 + 512],
                c.scrB[:, 1024 + (k % 2) * 512:1024 + (k % 2) * 512 + 512],
                -7.0, 1.0, ALU.max, ALU.add)
            c.s_pl.inc(g.tensor_tensor(
                c.scrA[:, A_HT + fc * tw + tg * 512:
                       A_HT + fc * tw + tg * 512 + 512],
                c.scrB[:, 2048 + (k % 2) * 512:2048 + (k % 2) * 512 + 512],
                c.scrB[:, 1024 + (k % 2) * 512:1024 + (k % 2) * 512 + 512],
                ALU.mult))
            fb = k // (2 * ntg)
            if k % (2 * ntg) == 2 * ntg - 1 and fb + 2 < NFB:
                dma_chunk(g, fb + 2)

    with nc.Block() as blk:
        blk.gpsimd(lambda gg: (g(gg), pool(gg)))
        blk.tensor(tens)
        blk.vector(vec)
        blk.scalar(scal)
    return ld0, nb + 2 * NFB + 4 + 1


def emit_l3_ffn(c, w):
    import os
    emit_ln(c)
    emit_l3_prenorm(c, w)
    emit_router(c, w["L3_gate_w"], w["L3_gate_b"], renorm=True)
    for e in range(int(os.environ.get("K_EXPERTS", "8"))):
        for tg in range(2):
            ld0, nd = emit_moe_up_swiglu(c, e, w["L3_mlp1_w"], w["L3_mlp1_b"],
                                         w["L3_mlp2_w"], w["L3_mlp2_b"],
                                         t0=tg * 1024, tw=1024,
                                         load_b=(tg == 0))
            emit_moe_down(c, e, 2048, ld0, nd, tw=1024, t0=tg * 1024)


def build_l3(c, w):
    import os
    parts = os.environ.get("K_PARTS", "fa f")
    for p in parts.split():
        for ch in p:
            if ch == "f":
                emit_l3_ffn(c, w)
            elif ch == "a":
                emit_l3_attn(c, w)
            elif ch == "n":
                emit_ln(c)





def emit_l0_ffn(c, w):
    nc = c.nc
    emit_ln(c)
    ld0 = c.s_ld.n

    def g0(g):
        c.s_ld.inc(g.dma_start(out=c.bfeat[:, 0:8], in_=w["L0_mu_r"]), 16)
        c.s_ld.inc(g.dma_start(out=c.bfeat[:, 8:16], in_=w["L0_mu_k"]), 16)
        g.wait_ge(c.s_ld.h, ld0 + 32)

    def v0(v):
        v.wait_ge(c.s_ld.h, ld0 + 32)
        c.vser(v, v.tensor_scalar(c.bfeat[:, 16:32], c.bfeat[:, 0:16],
                                  -1.0, 1.0, ALU.mult, ALU.add))
    phase(nc, gpsimd=g0, vector=v0)

    XR, XK = 0, 16384

    def v1(v):
        for which, base, mu0, om0 in ((0, XR, 0, 16), (1, XK, 8, 24)):
            for cc in range(NCC):
                v.tensor_scalar(
                    c.scrA[:, base + cc * T:base + (cc + 1) * T],
                    c.xnT[:, cc * T:(cc + 1) * T],
                    c.bfeat[:, mu0 + cc:mu0 + cc + 1], None, ALU.mult)
            for cc in range(NCC):
                v.scalar_tensor_tensor(
                    c.scrA[:, base + cc * T + 1:base + (cc + 1) * T],
                    c.xnT[:, cc * T:(cc + 1) * T - 1],
                    c.bfeat[:, om0 + cc:om0 + cc + 1],
                    c.scrA[:, base + cc * T + 1:base + (cc + 1) * T],
                    ALU.mult, ALU.add)
    phase(nc, vector=v1)

    for which in range(2):
        src_base = XR if which == 0 else XK
        dram_w = w["L0_W_r"] if which == 0 else w["L0_W_k"]
        ld1 = c.s_ld.n
        mm0 = c.s_mm.n
        sc0 = c.s_sc.n
        pl0 = c.s_pl.n

        def gw_(g, dram_w=dram_w, ld1=ld1, mm0=mm0):
            for fb in range(2):
                c.s_ld.inc(g.dma_start(
                    out=c.wU[:, fb * 4096:fb * 4096 + 4096].rearrange(
                        "p (cc f) -> p cc f", f=512),
                    in_=dram_w.rearrange("(cc p) f -> p cc f", p=128)[
                        :, :, fb * 512:(fb + 1) * 512]), 16)

        def tns(t, src_base=src_base, ld1=ld1, mm0=mm0, sc0=sc0):
            for fb in range(2):
                t.wait_ge(c.s_ld.h, ld1 + (fb + 1) * 16)
                for fs in range(4):
                    for tg in range(4):
                        k = (fb * 4 + fs) * 4 + tg
                        bank = c.pp[k % 4]
                        if k >= 4:
                            t.wait_ge(c.s_sc.h, sc0 + k - 4 + 1)
                        for cc in range(NCC):
                            mm = t.matmul(
                                bank[:],
                                c.wU[:, fb * 4096 + cc * 512 + fs * 128:
                                     fb * 4096 + cc * 512 + fs * 128 + 128],
                                c.scrA[:, src_base + cc * T + tg * 512:
                                       src_base + cc * T + tg * 512 + 512],
                                start=(cc == 0), stop=(cc == NCC - 1))
                            if cc == NCC - 1:
                                c.s_mm.inc(mm)

        if which == 0:
            def ev_(s, mm0=mm0):
                for k in range(32):
                    fc = k // 4
                    tg = k % 4
                    s.wait_ge(c.s_mm.h, mm0 + k + 1)
                    c.s_sc.inc(s.activation(
                        c.xnT[:, fc * T + tg * 512:fc * T + tg * 512 + 512],
                        c.pp[k % 4][:], AF.Sigmoid))
            phase(nc, gpsimd=gw_, tensor=tns, scalar=ev_)
        else:
            def ev_(s, mm0=mm0):
                for k in range(32):
                    fc = k // 4
                    tg = k % 4
                    s.wait_ge(c.s_mm.h, mm0 + k + 1)
                    c.s_sc.inc(s.activation(
                        c.scrA[:, XR + fc * T + tg * 512:
                               XR + fc * T + tg * 512 + 512],
                        c.pp[k % 4][:], AF.Relu))

            def sq_(g, sc0=sc0, pl0=pl0):
                for k in range(32):
                    fc = k // 4
                    tg = k % 4
                    g.wait_ge(c.s_sc.h, sc0 + k + 1)
                    sl = c.scrA[:, XR + fc * T + tg * 512:
                                XR + fc * T + tg * 512 + 512]
                    c.s_pl.inc(g.tensor_tensor(sl, sl, sl, ALU.mult))
            phase(nc, gpsimd2=None) if False else \
                phase(nc, gpsimd=lambda gg: (gw_(gg), sq_(gg)), tensor=tns,
                      scalar=ev_)

    ld2 = c.s_ld.n
    mm1 = c.s_mm.n
    ev1 = c.s_ev.n
    pl1 = c.s_pl.n

    def gv_(g):
        for fb in range(2):
            c.s_ld.inc(g.dma_start(
                out=c.wU[:, fb * 4096:fb * 4096 + 4096].rearrange(
                    "p (cc f) -> p cc f", f=512),
                in_=w["L0_W_v"].rearrange("(cc p) f -> p cc f", p=128)[
                    :, :, fb * 512:(fb + 1) * 512]), 16)

    def tv_(t):
        for fb in range(2):
            t.wait_ge(c.s_ld.h, ld2 + (fb + 1) * 16)
            for fs in range(4):
                for tg in range(4):
                    k = (fb * 4 + fs) * 4 + tg
                    bank = c.pp[k % 4]
                    if k >= 4:
                        t.wait_ge(c.s_ev.h, ev1 + k - 4 + 1)
                    for cc in range(NCC):
                        mm = t.matmul(
                            bank[:],
                            c.wU[:, fb * 4096 + cc * 512 + fs * 128:
                                 fb * 4096 + cc * 512 + fs * 128 + 128],
                            c.scrA[:, XR + cc * T + tg * 512:
                                   XR + cc * T + tg * 512 + 512],
                            start=(cc == 0), stop=(cc == NCC - 1))
                        if cc == NCC - 1:
                            c.s_mm.inc(mm)

    def vv_(v):
        for k in range(32):
            fc = k // 4
            tg = k % 4
            v.wait_ge(c.s_mm.h, mm1 + k + 1)
            c.s_ev.inc(v.tensor_tensor(
                c.scrA[:, XK + fc * T + tg * 512:XK + fc * T + tg * 512 + 512],
                c.xnT[:, fc * T + tg * 512:fc * T + tg * 512 + 512],
                c.pp[k % 4][:], ALU.mult))
    phase(nc, gpsimd=gv_, tensor=tv_, vector=vv_)

    tp0 = c.s_tpos.n
    ev2 = c.s_ev.n

    def tt_(t):
        k = 0
        for tt in range(NTT):
            for cc in range(NCC):
                if k >= 16:
                    t.wait_ge(c.s_ev.h, ev2 + (k - 16) // 8 + 1)
                slot = k % 16
                bank = c.tp[slot // 8]
                c.s_tpos.inc(t.transpose(
                    bank[:, (slot % 8) * 128:(slot % 8 + 1) * 128],
                    c.scrA[:, XK + cc * T + tt * 128:XK + cc * T + tt * 128 + 128],
                    c.ident[:]))
                k += 1

    def tv2_(v):
        for tt in range(NTT):
            v.wait_ge(c.s_tpos.h, tp0 + (tt + 1) * 8)
            bank = c.tp[tt % 2]
            c.s_ev.inc(v.scalar_tensor_tensor(
                c.stream[:, tt * C:(tt + 1) * C], bank[:], 0.5,
                c.stream[:, tt * C:(tt + 1) * C], ALU.mult, ALU.add))
    phase(nc, tensor=tt_, vector=tv2_)


def emit_l0_attn(c, w):
    nc = c.nc
    emit_ln(c)
    ld0 = c.s_ld.n
    X2 = 16384

    def g0(g):
        c.s_ld.inc(g.dma_start(out=c.bfeat[:, 0:8], in_=w["L0_norm_w"]), 16)
        c.s_ld.inc(g.dma_start(out=c.bfeat[:, 8:10], in_=w["L0_b_in"]), 16)
        c.s_ld.inc(g.dma_start(out=c.bfeat[0:64, 10:11], in_=w["L0_b_x"]), 16)
        c.s_ld.inc(g.dma_start(out=c.bfeat[0:64, 11:12], in_=w["L0_b_dt"]), 16)
        c.s_ld.inc(g.dma_start(out=c.rowb[0:1, 0:1024], in_=w["L0_b_out"]), 16)
        c.s_ld.inc(g.dma_start(out=c.wD[0:64, 0:64], in_=w["L0_W_x"]), 16)
        c.s_ld.inc(g.dma_start(out=c.wD[64:128, 0:64], in_=w["L0_W_dt"]), 16)
        c.s_ld.inc(g.dma_start(
            out=c.wD[0:64, 1024:2048], in_=w["L0_W_out"]), 16)
        c.s_ld.inc(g.dma_start(
            out=c.wU[:, 0:1536].rearrange("p (cc f) -> p cc f", f=192),
            in_=w["L0_W_in"].rearrange("(cc p) f -> p cc f", p=128)), 16)

    def s0(s):
        for tt in range(NTT):
            s.activation(c.scrBf[:, (tt % 4) * 1024:(tt % 4) * 1024 + 1024],
                         c.scrA[:, A_XN + tt * C:A_XN + (tt + 1) * C],
                         AF.Square, accum_out=c.stats[:, 16 + tt:17 + tt])

    def v0(v):
        for tt in range(NTT):
            v.tensor_tensor(c.stream[:, tt * C:(tt + 1) * C],
                            c.stream[:, tt * C:(tt + 1) * C],
                            c.scrA[:, A_XN + tt * C:A_XN + (tt + 1) * C],
                            ALU.add)
    phase(nc, gpsimd=g0, scalar=s0, vector=v0)

    def v1(v):
        c.vser(v, v.tensor_scalar(c.stats[:, 16:32], c.stats[:, 16:32],
                                  1.0 / C, 1e-8, ALU.mult, ALU.add))
    phase(nc, vector=v1)

    def s1(s):
        s.activation(c.stats[:, 16:32], c.stats[:, 16:32], AF.Sqrt)
    phase(nc, scalar=s1)

    def v2(v):
        c.vser(v, v.reciprocal(c.stats[:, 48:64], c.stats[:, 16:32]))
        for tt in range(NTT):
            v.tensor_scalar(
                c.scrA[:, X2 + tt * C:X2 + (tt + 1) * C],
                c.scrA[:, A_XN + tt * C:A_XN + (tt + 1) * C],
                c.stats[:, 48 + tt:49 + tt], None, ALU.mult)
        v.wait_ge(c.s_ld.h, ld0 + 144)
        for cc in range(NCC):
            v.tensor_scalar(c.wU[:, cc * 192:(cc + 1) * 192],
                            c.wU[:, cc * 192:(cc + 1) * 192],
                            c.bfeat[:, cc:cc + 1], None, ALU.mult)
    phase(nc, vector=v2)
    emit_transpose_to(c, c.scrA, X2, c.xnT, 0)

    mm0 = c.s_mm.n
    sc0 = c.s_sc.n

    def t1(t):
        for ft in range(2):
            M = 128 if ft == 0 else 64
            for tg in range(4):
                bank = c.pp[(ft * 4 + tg) % 4]
                if ft * 4 + tg >= 4:
                    t.wait_ge(c.s_sc.h, sc0 + ft * 4 + tg - 4 + 1)
                for cc in range(NCC):
                    mm = t.matmul(
                        bank[0:M, :],
                        c.wU[:, cc * 192 + ft * 128:cc * 192 + ft * 128 + M],
                        c.xnT[:, cc * T + tg * 512:cc * T + tg * 512 + 512],
                        start=(cc == 0), stop=(cc == NCC - 1))
                    if cc == NCC - 1:
                        c.s_mm.inc(mm)

    def s2(s):
        for ft in range(2):
            M = 128 if ft == 0 else 64
            for tg in range(4):
                k = ft * 4 + tg
                s.wait_ge(c.s_mm.h, mm0 + k + 1)
                c.s_sc.inc(s.activation(
                    c.scrA[0:M, ft * T + tg * 512:ft * T + tg * 512 + 512],
                    c.pp[k % 4][0:M, :], AF.Identity,
                    bias=c.bfeat[0:M, 8 + ft:9 + ft]))
    phase(nc, tensor=t1, scalar=s2)

    mm1 = c.s_mm.n
    sc1 = c.s_sc.n

    def t2(t):
        for which in range(2):
            r0 = which * 64
            for tg in range(4):
                k = which * 4 + tg
                bank = c.pp[k % 4]
                if k >= 4:
                    t.wait_ge(c.s_sc.h, sc1 + k - 4 + 1)
                c.s_mm.inc(t.matmul(
                    bank[0:64, :],
                    c.wD[r0:r0 + 64, 0:64],
                    c.scrA[r0:r0 + 64, tg * 512:tg * 512 + 512],
                    start=True, stop=True))

    def s3(s):
        for which in range(2):
            for tg in range(4):
                k = which * 4 + tg
                s.wait_ge(c.s_mm.h, mm1 + k + 1)
                c.s_sc.inc(s.activation(
                    c.scrBf[0:64, which * 2048 + tg * 512:
                            which * 2048 + tg * 512 + 512],
                    c.pp[k % 4][0:64, :],
                    AF.Tanh if which == 0 else AF.Sigmoid,
                    bias=c.bfeat[0:64, 10 + which:11 + which]))
    phase(nc, tensor=t2, scalar=s3)

    def v3(v):
        v.tensor_scalar(c.scrBf[64:128, 0:2048], c.scrBf[0:64, 2048:4096],
                        -1.0, 1.0, ALU.mult, ALU.add)
        c.vser(v, v.tensor_tensor(c.scrBf[64:128, 2048:4096],
                                  c.scrBf[0:64, 2048:4096],
                                  c.scrBf[0:64, 0:2048], ALU.mult))
        v.tensor_tensor_scan(c.scrA[0:64, 4096:6144],
                             c.scrBf[64:128, 0:2048],
                             c.scrBf[64:128, 2048:4096],
                             0.0, ALU.mult, ALU.add)
    phase(nc, vector=v3)

    def s4(s):
        s.activation(c.scrA[0:64, 6144:8192], c.scrA[0:64, T:T + 2048],
                     AF.Sigmoid)
    phase(nc, scalar=s4)

    def v4(v):
        v.tensor_tensor(c.scrA[0:64, 8192:10240], c.scrA[0:64, 4096:6144],
                        c.scrA[0:64, 6144:8192], ALU.mult)
    phase(nc, vector=v4)

    mm2 = c.s_mm.n
    ev0 = c.s_ev.n

    def t3(t):
        for tt in range(NTT):
            pr = (c.pp[0], c.pp[1]) if tt % 2 == 0 else (c.pp[2], c.pp[3])
            if tt >= 2:
                t.wait_ge(c.s_ev.h, ev0 + (tt - 2) * 2 + 2)
            for ch in range(2):
                t.matmul(pr[ch][:],
                         c.scrA[0:64, 8192 + tt * 128:8192 + tt * 128 + 128],
                         c.wD[0:64, 1024 + ch * 512:1024 + ch * 512 + 512],
                         start=True, stop=False)
                c.s_mm.inc(t.matmul(pr[ch][:], c.ones1[:],
                                    c.rowb[0:1, ch * 512:ch * 512 + 512],
                                    start=False, stop=True))

    def v5(v):
        for tt in range(NTT):
            pr = (c.pp[0], c.pp[1]) if tt % 2 == 0 else (c.pp[2], c.pp[3])
            v.wait_ge(c.s_mm.h, mm2 + (tt + 1) * 2)
            for ch in range(2):
                c.s_ev.inc(v.tensor_tensor(
                    c.stream[:, tt * C + ch * 512:tt * C + ch * 512 + 512],
                    c.stream[:, tt * C + ch * 512:tt * C + ch * 512 + 512],
                    pr[ch][:], ALU.add))
    phase(nc, tensor=t3, vector=v5)


def build_l0(c, w):
    import os
    parts = os.environ.get("K_PARTS", "fa f")
    for p in parts.split():
        for ch in p:
            if ch == "f":
                emit_l0_ffn(c, w)
            elif ch == "a":
                emit_l0_attn(c, w)
            elif ch == "n":
                emit_ln(c)



L1_KEYS = ["L1_W_attn", "L1_battn_feat", "L1_bv_row", "L1_W_proj",
           "L1_bproj_row", "L1_gate_w", "L1_gate_b", "L1_w1", "L1_b1",
           "L1_w2", "L1_b2"]
L0_KEYS = ["L0_W_in", "L0_b_in", "L0_W_x", "L0_b_x", "L0_W_dt", "L0_b_dt",
           "L0_W_out", "L0_b_out", "L0_norm_w", "L0_W_r", "L0_W_k", "L0_W_v",
           "L0_mu_r", "L0_mu_k"]
L2_KEYS = ["L2_Wq", "L2_Wk", "L2_Wv", "L2_Wo", "L2_bo_row", "L2_router_w",
           "L2_router_b", "L2_gate_w", "L2_up_w", "L2_down_w", "L2_aq", "L2_ak"]
L3_KEYS = ["L3_w_qkv", "L3_b_qkv_feat", "L3_bv_row", "L3_w_out", "L3_bout_row",
           "L3_qg_col", "L3_kg_col", "L3_cos", "L3_sin", "L3_norm_scale",
           "L3_norm_scale_row", "L3_gate_w",
           "L3_gate_b", "L3_mlp1_w", "L3_mlp1_b", "L3_mlp2_w", "L3_mlp2_b"]
LAYER_KEYS = {0: L0_KEYS, 1: L1_KEYS, 2: L2_KEYS, 3: L3_KEYS}


def build_program(layers, wdict):
    import os
    nc = bass.Bass("TRN2", target_bir_lowering=False,
                   detect_race_conditions=not os.environ.get("K_NO_RACE_DETECT"))
    keys = []
    for li in layers:
        keys += LAYER_KEYS[li]
    x_in = nc.dram_tensor("x", [T, C], F32, kind="ExternalInput").ap()
    y_out = nc.dram_tensor("y", [T, C], F32, kind="ExternalOutput").ap()
    dram = {}
    for k in keys:
        arr = wdict[k]
        dram[k] = nc.dram_tensor(k, list(arr.shape), F32, kind="ExternalInput").ap()
    with ExitStack() as es:
        c = alloc_ctx(nc, es)
        emit_setup(c, x_in)
        for li in layers:
            if li == 0:
                build_l0(c, dram)
            elif li == 1:
                build_l1(c, dram)
            elif li == 2:
                build_l2(c, dram)
            elif li == 3:
                build_l3(c, dram)
        emit_store(c, y_out)
        import os
        dumps = os.environ.get("K_DUMP", "")
        if dumps:
            for nm in dumps.split(","):
                tn = getattr(c, nm)
                dt = nc.dram_tensor("dump_" + nm, list(tn.shape),
                                    tn.dtype, kind="ExternalOutput").ap()
                st0 = c.s_st.n

                def sdump(s, tn=tn, dt=dt, st0=st0):
                    c.s_st.inc(s.dma_start(out=dt, in_=tn[:]), 16)
                    s.wait_ge(c.s_st.h, st0 + 16)
                phase(nc, sync=sdump)
    return nc, keys


def kernel(**inputs):
    w = host_prep(inputs)
    x = _as_f32(inputs["L0_attn_x"])
    import os
    gs = os.environ.get("K_GROUPS", "0123")
    groups = [[int(ch) for ch in g] for g in gs.split(",")]
    cur = [np.ascontiguousarray(x[b]) for b in range(8)]
    for layers in groups:
        nc, keys = build_program(layers, w)
        in_maps = []
        for b in range(8):
            m = {"x": cur[b]}
            for k in keys:
                m[k] = w[k]
            in_maps.append(m)
        res = run_bass_kernel_spmd(nc, in_maps, core_ids=list(range(8)))
        cur = [np.asarray(res.results[b]["y"]) for b in range(8)]
    return np.stack(cur, axis=0)


# version 03
# speedup vs baseline: 1.0361x; 1.0361x over previous
from contextlib import ExitStack
import numpy as np
import concourse.bass as bass
import concourse.mybir as mybir
from concourse.bass_utils import run_bass_kernel_spmd

F32 = mybir.dt.float32
BF16 = mybir.dt.bfloat16
AF = mybir.ActivationFunctionType
ALU = mybir.AluOpType
AX = mybir.AxisListType

T = 2048
C = 1024
NTT = T // 128
NCC = C // 128

A_XN = 0
A_HT = 0
A_DW = 16384
A_QT = 0
A_KT = 8192
A_VT = 16384
A_OT = 24576
B_P0, B_P1, B_T0, B_T1 = 0, 2048, 4096, 6144


def _as_f32(x):
    return np.ascontiguousarray(np.asarray(x), dtype=np.float32)


def host_prep(inp):
    d = {}
    d["L0_W_in"] = _as_f32(inp["L0_attn_W_in"])
    b = _as_f32(inp["L0_attn_b_in"])
    bi = np.zeros((128, 2), np.float32)
    bi[:, 0] = b[0:128]
    bi[0:64, 1] = b[128:192]
    d["L0_b_in"] = bi
    d["L0_W_x"] = _as_f32(inp["L0_attn_W_x"])
    d["L0_b_x"] = _as_f32(inp["L0_attn_b_x"]).reshape(64, 1)
    d["L0_W_dt"] = _as_f32(inp["L0_attn_W_dt"])
    d["L0_b_dt"] = _as_f32(inp["L0_attn_b_dt"]).reshape(64, 1)
    d["L0_W_out"] = _as_f32(inp["L0_attn_W_out"])
    d["L0_b_out"] = _as_f32(inp["L0_attn_b_out"]).reshape(1, C)
    d["L0_norm_w"] = _as_f32(inp["L0_attn_norm_w"]).reshape(NCC, 128).T.copy()
    d["L0_W_r"] = _as_f32(inp["L0_ffn_W_r"]).T.copy()
    d["L0_W_k"] = _as_f32(inp["L0_ffn_W_k"]).T.copy()
    d["L0_W_v"] = _as_f32(inp["L0_ffn_W_v"]).T.copy()
    d["L0_mu_r"] = _as_f32(inp["L0_ffn_mu_r"]).reshape(NCC, 128).T.copy()
    d["L0_mu_k"] = _as_f32(inp["L0_ffn_mu_k"]).reshape(NCC, 128).T.copy()

    d["L1_W_attn"] = _as_f32(inp["L1_attn_W_attn"])
    ba = _as_f32(inp["L1_attn_b_attn"])
    d["L1_battn_feat"] = ba[:2048].reshape(16, 128).T.copy()
    d["L1_bv_row"] = ba[2048:].reshape(1, C)
    d["L1_W_proj"] = _as_f32(inp["L1_attn_W_proj"])
    d["L1_bproj_row"] = _as_f32(inp["L1_attn_b_proj"]).reshape(1, C)
    d["L1_gate_w"] = _as_f32(inp["L1_ffn_gate_w"])
    d["L1_gate_b"] = _as_f32(inp["L1_ffn_gate_b"]).reshape(1, 8)
    d["L1_w1"] = _as_f32(inp["L1_ffn_w1"])
    d["L1_b1"] = _as_f32(inp["L1_ffn_b1"]).reshape(8, 16, 128).transpose(0, 2, 1).copy()
    d["L1_w2"] = _as_f32(inp["L1_ffn_w2"])
    d["L1_b2"] = _as_f32(inp["L1_ffn_b2"])

    d["L2_Wq"] = _as_f32(inp["L2_attn_Wq"])
    d["L2_Wk"] = _as_f32(inp["L2_attn_Wk"])
    d["L2_Wv"] = _as_f32(inp["L2_attn_Wv"])
    d["L2_Wo"] = _as_f32(inp["L2_attn_Wo"])
    d["L2_bo_row"] = _as_f32(inp["L2_attn_bo"]).reshape(1, C)
    d["L2_router_w"] = _as_f32(inp["L2_ffn_router_w"]).T.copy()
    d["L2_router_b"] = _as_f32(inp["L2_ffn_router_b"]).reshape(1, 8)
    d["L2_gate_w"] = _as_f32(inp["L2_ffn_gate_w"]).transpose(0, 2, 1).copy()
    d["L2_up_w"] = _as_f32(inp["L2_ffn_up_w"]).transpose(0, 2, 1).copy()
    d["L2_down_w"] = _as_f32(inp["L2_ffn_down_w"]).transpose(0, 2, 1).copy()
    H = 16
    slope = (1.0 / ((2.0 ** 8) ** (1.0 / H)) ** np.arange(1, H + 1)).astype(np.float32)
    pos = np.arange(T, dtype=np.float32)
    d["L2_aq"] = np.ascontiguousarray(
        np.stack([np.outer(slope, pos), np.ones((H, T), np.float32)], axis=1))
    d["L2_ak"] = np.ascontiguousarray(
        np.stack([np.ones((H, T), np.float32), -np.outer(slope, pos)], axis=1))

    DH = 64
    perm = np.zeros(C, dtype=np.int64)
    for hp in range(H // 2):
        base = hp * 128
        perm[base + 0:base + 32] = (2 * hp) * DH + np.arange(0, DH, 2)
        perm[base + 32:base + 64] = (2 * hp + 1) * DH + np.arange(0, DH, 2)
        perm[base + 64:base + 96] = (2 * hp) * DH + np.arange(1, DH, 2)
        perm[base + 96:base + 128] = (2 * hp + 1) * DH + np.arange(1, DH, 2)
    wq = _as_f32(inp["L3_attn_w_qkv"]).T.copy()
    wq3 = np.concatenate([wq[:, perm], wq[:, C + perm], wq[:, 2 * C:]], axis=1)
    d["L3_w_qkv"] = np.ascontiguousarray(wq3)
    bq = _as_f32(inp["L3_attn_b_qkv"])
    bq3 = np.concatenate([bq[perm], bq[C + perm], bq[2 * C:]])
    d["L3_b_qkv_feat"] = bq3[:2048].reshape(16, 128).T.copy()
    d["L3_bv_row"] = bq3[2048:].reshape(1, C)
    d["L3_w_out"] = _as_f32(inp["L3_attn_w_out"]).T.copy()
    d["L3_bout_row"] = _as_f32(inp["L3_attn_b_out"]).reshape(1, C)

    def gcol(g):
        out = np.zeros((128, 8), np.float32)
        for hp in range(8):
            out[0:32, hp] = g[2 * hp, 0::2]
            out[32:64, hp] = g[2 * hp + 1, 0::2]
            out[64:96, hp] = g[2 * hp, 1::2]
            out[96:128, hp] = g[2 * hp + 1, 1::2]
        return out
    d["L3_qg_col"] = gcol(_as_f32(inp["L3_attn_q_gamma"]))
    d["L3_kg_col"] = gcol(_as_f32(inp["L3_attn_k_gamma"]))
    half = DH // 2
    invf = (1.0 / (10000.0 ** (np.arange(half, dtype=np.float32) / half))).astype(np.float32)
    ang = pos[None, :] * invf[:, None]
    d["L3_cos"] = np.ascontiguousarray(np.concatenate(
        [np.cos(ang)] * 4, axis=0).astype(np.float32))
    d["L3_sin"] = np.ascontiguousarray(np.concatenate(
        [np.sin(ang)] * 4, axis=0).astype(np.float32))
    d["L3_norm_scale_row"] = _as_f32(inp["L3_ffn_norm_scale"]).reshape(1, C)
    d["L3_norm_scale"] = _as_f32(inp["L3_ffn_norm_scale"]).reshape(NCC, 128).T.copy()
    d["L3_gate_w"] = _as_f32(inp["L3_ffn_gate_w"])
    d["L3_gate_b"] = _as_f32(inp["L3_ffn_gate_b"]).reshape(1, 8)
    m1 = _as_f32(inp["L3_ffn_mlp1_w"])
    I2 = 4096
    permh = np.concatenate([np.arange(0, I2, 2), np.arange(1, I2, 2)])
    d["L3_mlp1_w"] = np.ascontiguousarray(m1[:, :, permh])
    b1 = _as_f32(inp["L3_ffn_mlp1_b"])[:, permh]
    d["L3_mlp1_b"] = b1.reshape(8, 32, 128).transpose(0, 2, 1).copy()
    d["L3_mlp2_w"] = _as_f32(inp["L3_ffn_mlp2_w"])
    d["L3_mlp2_b"] = _as_f32(inp["L3_ffn_mlp2_b"])
    return d


class Sem:
    def __init__(self, es, nc, name):
        self.h = es.enter_context(nc.semaphore(name))
        self.n = 0

    def inc(self, instr, k=1):
        instr.then_inc(self.h, k)
        self.n += k
        return instr


class Ctx:
    pass


def alloc_ctx(nc, es):
    c = Ctx()
    c.nc = nc

    def sbt(name, shape, dt):
        return es.enter_context(nc.sbuf_tensor(name, shape, dt))
    c.stream = sbt("stream", [128, NTT * C], F32)
    c.xnT = sbt("xnT", [128, NCC * T], BF16)
    c.scrA = sbt("scrA", [128, 32768], BF16)
    c.scrB = sbt("scrB", [128, 8192], BF16)
    c.scrBf = c.scrB.bitcast(F32)
    c.wU = sbt("wU", [128, 8192], BF16)
    c.wD = sbt("wD", [128, 4096], BF16)
    c.rowb = sbt("rowb", [128, 2048], BF16)
    c.stats = sbt("stats", [128, 64], F32)
    c.gates = sbt("gates", [128, 256], F32)
    c.bfeat = sbt("bfeat", [128, 64], F32)
    c.cvec = sbt("cvec", [128, 24], F32)
    c.csel = sbt("csel", [128, 8], BF16)
    c.rsel = sbt("rsel", [2, 128], BF16)
    c.ones1 = sbt("ones1", [1, 128], BF16)
    c.ident = sbt("ident", [128, 128], BF16)
    c.pp = [es.enter_context(nc.psum_tensor(f"p{i}", [128, 512], F32)) for i in range(6)]
    c.tp = [es.enter_context(nc.psum_tensor(f"tp{i}", [128, 1024], BF16)) for i in range(2)]
    for nm in ["ld", "mm", "ev", "sc", "pl", "tpos", "mx", "pe", "pt", "st",
               "vs"]:
        setattr(c, "s_" + nm, Sem(es, nc, "s_" + nm))

    def vser(eng, instr):
        instr.then_inc(c.s_vs.h, 1)
        c.s_vs.n += 1
        eng.wait_ge(c.s_vs.h, c.s_vs.n)
        return instr
    c.vser = vser
    return c


def phase(nc, **fns):
    with nc.Block() as blk:
        for name, f in fns.items():
            getattr(blk, name)(f)




def emit_setup(c, x_in):
    nc = c.nc
    ld0 = c.s_ld.n

    def g(g):
        c.s_ld.inc(g.dma_start(
            out=c.stream[:].rearrange("p (tt c) -> p tt c", c=C),
            in_=x_in.rearrange("(tt p) c -> p tt c", p=128)), 16)
        g.memset(c.ones1[:], 1.0)
        c.vser(g, g.memset(c.ident[:], 1.0))
        g.affine_select(c.ident[:], c.ident[:], [[-1, 128]], ALU.is_equal, 0.0,
                        base=0, channel_multiplier=1)
        g.memset(c.cvec[:, 16:17], 1e-5)
        g.memset(c.cvec[:, 17:18], 1e-6)
        g.memset(c.cvec[:, 18:19], 1e-8)
        c.vser(g, g.memset(c.csel[:], 1.0))
        c.vser(g, g.affine_select(c.csel[:, 0:4], c.csel[:, 0:4], [[-32, 4]],
                                  ALU.is_ge, 0.0, base=0, channel_multiplier=1))
        g.affine_select(c.csel[:, 0:4], c.csel[:, 0:4], [[32, 4]], ALU.is_ge,
                        0.0, base=31, channel_multiplier=-1)
        c.vser(g, g.memset(c.rsel[:], 1.0))
        g.affine_select(
            c.rsel[:].rearrange("p (a h j) -> p a h j", a=2, h=2, j=32),
            c.rsel[:].rearrange("p (a h j) -> p a h j", a=2, h=2, j=32),
            [[0, 2], [-1, 2], [0, 32]], ALU.is_equal, 0.0,
            base=0, channel_multiplier=1)
        g.wait_ge(c.s_ld.h, ld0 + 16)
    phase(nc, gpsimd=g)

    def v(v):
        v.tensor_add(c.csel[:, 4:5], c.csel[:, 0:1], c.csel[:, 2:3])
        v.tensor_add(c.csel[:, 5:6], c.csel[:, 1:2], c.csel[:, 3:4])
    phase(nc, vector=v)


def emit_store(c, y_out):
    nc = c.nc
    st0 = c.s_st.n

    def s(s):
        c.s_st.inc(s.dma_start(
            out=y_out.rearrange("(tt p) c -> p tt c", p=128),
            in_=c.stream[:].rearrange("p (tt c) -> p tt c", c=C)), 16)
        s.wait_ge(c.s_st.h, st0 + 16)
    phase(nc, sync=s)


def emit_ln(c):
    nc = c.nc
    xn = c.scrA

    def p1s(s):
        for tt in range(NTT):
            s.activation(c.scrBf[:, (tt % 4) * 1024:(tt % 4) * 1024 + 1024],
                         c.stream[:, tt * C:(tt + 1) * C],
                         AF.Square, accum_out=c.stats[:, 16 + tt:17 + tt])

    def p1v(v):
        for tt in range(NTT):
            v.reduce_sum(c.stats[:, tt:tt + 1],
                         c.stream[:, tt * C:(tt + 1) * C], axis=AX.X)
    phase(nc, scalar=p1s, vector=p1v)

    def p2v(v):
        st = c.stats
        c.vser(v, v.tensor_scalar(c.cvec[:, 0:16], st[:, 0:16], 1.0 / C, None,
                                  ALU.mult))
        c.vser(v, v.tensor_scalar(st[:, 16:32], st[:, 16:32], 1.0 / C, None,
                                  ALU.mult))
        c.vser(v, v.tensor_tensor(st[:, 32:48], c.cvec[:, 0:16],
                                  c.cvec[:, 0:16], ALU.mult))
        v.tensor_tensor(st[:, 16:32], st[:, 16:32], st[:, 32:48], ALU.subtract)
    phase(nc, vector=p2v)

    def p3s(s):
        s.activation(c.stats[:, 16:32], c.stats[:, 16:32], AF.Sqrt, bias=c.cvec[:, 16:17])
    phase(nc, scalar=p3s)

    def p4v(v):
        c.vser(v, v.reciprocal(c.stats[:, 48:64], c.stats[:, 16:32]))
        for tt in range(NTT):
            v.tensor_scalar(
                xn[:, A_XN + tt * C:A_XN + (tt + 1) * C],
                c.stream[:, tt * C:(tt + 1) * C],
                c.cvec[:, tt:tt + 1], c.stats[:, 48 + tt:49 + tt],
                ALU.subtract, ALU.mult)
    phase(nc, vector=p4v)
    emit_transpose_to(c, xn, A_XN, c.xnT, 0)


def emit_transpose_to(c, src, src_col0, dstT, dst_col0, ncc=NCC, ntt=NTT):
    nc = c.nc
    tp0 = c.s_tpos.n
    xe0 = c.s_ev.n

    def tens(t):
        k = 0
        for cc in range(ncc):
            for tt in range(ntt):
                if k >= 16:
                    t.wait_ge(c.s_ev.h, xe0 + (k - 16) // 8 + 1)
                slot = k % 16
                bank = c.tp[slot // 8]
                c.s_tpos.inc(t.transpose(
                    bank[:, (slot % 8) * 128:(slot % 8 + 1) * 128],
                    src[:, src_col0 + tt * ncc * 128 + cc * 128:
                        src_col0 + tt * ncc * 128 + (cc + 1) * 128],
                    c.ident[:]))
                k += 1

    def vec(v):
        k = 0
        for cc in range(ncc):
            for tt in range(ntt):
                k += 1
                if k % 8 == 0:
                    v.wait_ge(c.s_tpos.h, tp0 + k)
                    bank = c.tp[((k - 8) % 16) // 8]
                    c.s_ev.inc(v.tensor_copy(
                        dstT[:, dst_col0 + cc * ntt * 128 + (tt - 7) * 128:
                             dst_col0 + cc * ntt * 128 + (tt + 1) * 128],
                        bank[:]))
    phase(nc, tensor=tens, vector=vec)


def emit_router(c, dram_gw, dram_gb, renorm, scale_col=None):
    nc = c.nc
    ld0 = c.s_ld.n
    mm0 = c.s_mm.n
    ev0 = c.s_ev.n
    LG = c.scrBf

    def g(g):
        c.s_ld.inc(g.dma_start(
            out=c.wD[:, 0:64].rearrange("p (cc e) -> p cc e", e=8),
            in_=dram_gw.rearrange("(cc p) e -> p cc e", p=128)), 16)
        c.s_ld.inc(g.dma_start(out=c.rowb[0:1, 0:8], in_=dram_gb), 16)

    def pre_v(v):
        if scale_col is not None:
            v.wait_ge(c.s_ld.h, ld0 + 32)
            for cc in range(NCC):
                v.tensor_scalar(c.wD[:, cc * 8:cc * 8 + 8],
                                c.wD[:, cc * 8:cc * 8 + 8],
                                scale_col(cc), None, ALU.mult)
    phase(nc, gpsimd=g, vector=pre_v)

    def tens(t):
        t.wait_ge(c.s_ld.h, ld0 + 32)
        for tt in range(NTT):
            bank = c.pp[tt % 4]
            if tt >= 4:
                t.wait_ge(c.s_ev.h, ev0 + (tt - 4) + 1)
            for cc in range(NCC):
                t.matmul(bank[:, 0:8],
                         c.xnT[:, cc * T + tt * 128: cc * T + tt * 128 + 128],
                         c.wD[:, cc * 8:cc * 8 + 8],
                         start=(cc == 0), stop=False)
            c.s_mm.inc(t.matmul(bank[:, 0:8], c.ones1[:], c.rowb[0:1, 0:8],
                                start=False, stop=True))

    def vec0(v):
        for tt in range(NTT):
            v.wait_ge(c.s_mm.h, mm0 + tt + 1)
            c.s_ev.inc(v.tensor_copy(LG[:, tt * 8:tt * 8 + 8],
                                     c.pp[tt % 4][:, 0:8]))
    phase(nc, tensor=tens, vector=vec0)

    import os
    if os.environ.get("K_GATE_STOP"):
        return
    st = c.stats
    if not renorm:
        def v1(v):
            for tt in range(NTT):
                v.reduce_max(st[:, 32 + tt:33 + tt], LG[:, tt * 8:tt * 8 + 8],
                             axis=AX.X, negate=True)
        phase(nc, vector=v1)

        def sc(s):
            for tt in range(NTT):
                s.activation(LG[:, 256 + tt * 8:264 + tt * 8],
                             LG[:, tt * 8:tt * 8 + 8], AF.Exp,
                             bias=st[:, 32 + tt:33 + tt],
                             accum_out=st[:, 48 + tt:49 + tt])
        phase(nc, scalar=sc)

        def v2(v):
            E = LG[:, 256:384]
            P = LG[:, 640:768]
            c.vser(v, v.reciprocal(st[:, 0:16], st[:, 48:64]))
            for tt in range(NTT):
                v.tensor_scalar(P[:, tt * 8:tt * 8 + 8],
                                E[:, tt * 8:tt * 8 + 8],
                                st[:, tt:tt + 1], None, ALU.mult)
            for tt in range(NTT):
                v.reduce_max(st[:, 16 + tt:17 + tt], P[:, tt * 8:tt * 8 + 8],
                             axis=AX.X)
            for tt in range(NTT):
                v.tensor_scalar(LG[:, 384 + tt * 8:392 + tt * 8],
                                P[:, tt * 8:tt * 8 + 8],
                                st[:, 16 + tt:17 + tt], None, ALU.is_ge)
            for tt in range(NTT):
                v.scalar_tensor_tensor(LG[:, 512 + tt * 8:520 + tt * 8],
                                       LG[:, 384 + tt * 8:392 + tt * 8],
                                       -2.0, P[:, tt * 8:tt * 8 + 8],
                                       ALU.mult, ALU.add)
            for tt in range(NTT):
                v.reduce_max(st[:, 32 + tt:33 + tt],
                             LG[:, 512 + tt * 8:520 + tt * 8], axis=AX.X)
            for tt in range(NTT):
                v.tensor_scalar(LG[:, 384 + tt * 8:392 + tt * 8],
                                P[:, tt * 8:tt * 8 + 8],
                                st[:, 32 + tt:33 + tt], None, ALU.is_ge)
            for tt in range(NTT):
                v.tensor_tensor(c.gates[:, tt * 8:tt * 8 + 8],
                                P[:, tt * 8:tt * 8 + 8],
                                LG[:, 384 + tt * 8:392 + tt * 8], ALU.mult)
            v.tensor_scalar(c.gates[:, 128:256], c.gates[:, 0:128], 0.5,
                            None, ALU.mult)
        phase(nc, vector=v2)
    else:
        def v1(v):
            for tt in range(NTT):
                v.reduce_max(st[:, tt:tt + 1], LG[:, tt * 8:tt * 8 + 8],
                             axis=AX.X)
            for tt in range(NTT):
                v.tensor_scalar(LG[:, 256 + tt * 8:264 + tt * 8],
                                LG[:, tt * 8:tt * 8 + 8],
                                st[:, tt:tt + 1], None, ALU.is_ge)
            for tt in range(NTT):
                v.scalar_tensor_tensor(LG[:, 384 + tt * 8:392 + tt * 8],
                                       LG[:, 256 + tt * 8:264 + tt * 8],
                                       -1e30, LG[:, tt * 8:tt * 8 + 8],
                                       ALU.mult, ALU.add)
            for tt in range(NTT):
                v.reduce_max(st[:, 16 + tt:17 + tt],
                             LG[:, 384 + tt * 8:392 + tt * 8], axis=AX.X)
            for tt in range(NTT):
                v.tensor_tensor(st[:, 32 + tt:33 + tt], st[:, tt:tt + 1],
                                st[:, 16 + tt:17 + tt], ALU.subtract)
        phase(nc, vector=v1)

        def sc(s):
            s.activation(st[:, 48:64], st[:, 32:48], AF.Sigmoid)
        phase(nc, scalar=sc)

        def v2(v):
            os_ = c.cvec[:, 0:16]
            c.vser(v, v.tensor_scalar(os_, st[:, 48:64], -1.0, 1.0, ALU.mult,
                                      ALU.add))
            for tt in range(NTT):
                v.tensor_scalar(LG[:, 512 + tt * 8:520 + tt * 8],
                                LG[:, 384 + tt * 8:392 + tt * 8],
                                st[:, 16 + tt:17 + tt], None, ALU.is_ge)
            for tt in range(NTT):
                v.tensor_scalar(LG[:, 256 + tt * 8:264 + tt * 8],
                                LG[:, 256 + tt * 8:264 + tt * 8],
                                st[:, 48 + tt:49 + tt], None, ALU.mult)
            for tt in range(NTT):
                v.tensor_scalar(LG[:, 512 + tt * 8:520 + tt * 8],
                                LG[:, 512 + tt * 8:520 + tt * 8],
                                os_[:, tt:tt + 1], None, ALU.mult)
            for tt in range(NTT):
                v.tensor_tensor(c.gates[:, tt * 8:tt * 8 + 8],
                                LG[:, 256 + tt * 8:264 + tt * 8],
                                LG[:, 512 + tt * 8:520 + tt * 8], ALU.add)
            v.tensor_scalar(c.gates[:, 128:256], c.gates[:, 0:128], 0.5,
                            None, ALU.mult)
        phase(nc, vector=v2)




def emit_moe_up_relu(c, e, dram_w1, dram_b1, dram_w2, dram_b2, FH, t0, tw,
                     load_b=True):
    nc = c.nc
    ld0 = c.s_ld.n
    mm0 = c.s_mm.n
    sc0 = c.s_sc.n
    NFB = FH // 512
    ntg = tw // 512
    NF = FH // 128
    nb = 1 if load_b else 0

    def g(g):
        if load_b:
            c.s_ld.inc(g.dma_start(out=c.bfeat[:, 0:NF], in_=dram_b1[e]), 16)
        for fb in range(NFB):
            if fb >= 2:
                g.wait_ge(c.s_mm.h, mm0 + (fb - 1) * 4 * ntg)
            c.s_ld.inc(g.dma_start(
                out=c.wU[:, (fb % 2) * 4096:(fb % 2) * 4096 + 4096].rearrange(
                    "p (cc f) -> p cc f", f=512),
                in_=dram_w1[e].rearrange("(cc p) f -> p cc f", p=128)[
                    :, :, fb * 512:(fb + 1) * 512]), 16)
        for j in range(NF // 4):
            c.s_ld.inc(g.dma_start(
                out=c.scrA[:, A_DW + j * 4096:A_DW + (j + 1) * 4096].rearrange(
                    "p (fc cw) -> p fc cw", cw=1024),
                in_=dram_w2[e].rearrange("(fc p) cw -> p fc cw", p=128)[
                    :, j * 4:(j + 1) * 4, :]), 16)
        if dram_b2 is not None:
            c.s_ld.inc(g.dma_start(out=c.rowb[0:1, 1024:2048],
                                   in_=dram_b2[e:e + 1, :]), 16)

    def tens(t):
        for fb in range(NFB):
            t.wait_ge(c.s_ld.h, ld0 + (nb + fb + 1) * 16)
            slot = (fb % 2) * 4096
            for fs in range(4):
                for tg in range(ntg):
                    k = (fb * 4 + fs) * ntg + tg
                    bank = c.pp[k % 4]
                    if k >= 4:
                        t.wait_ge(c.s_sc.h, sc0 + k - 4 + 1)
                    for cc in range(NCC):
                        mm = t.matmul(
                            bank[:],
                            c.wU[:, slot + cc * 512 + fs * 128:
                                 slot + cc * 512 + fs * 128 + 128],
                            c.xnT[:, cc * T + t0 + tg * 512:
                                  cc * T + t0 + tg * 512 + 512],
                            start=(cc == 0), stop=(cc == NCC - 1))
                        if cc == NCC - 1:
                            c.s_mm.inc(mm)

    def sup(s):
        if load_b:
            s.wait_ge(c.s_ld.h, ld0 + 16)
        for k in range(NF * ntg):
            fc = k // ntg
            tg = k % ntg
            s.wait_ge(c.s_mm.h, mm0 + k + 1)
            c.s_sc.inc(s.activation(
                c.scrA[:, A_HT + fc * tw + tg * 512:
                       A_HT + fc * tw + tg * 512 + 512],
                c.pp[k % 4][:], AF.Relu, bias=c.bfeat[:, fc:fc + 1]))
    phase(nc, gpsimd=g, tensor=tens, scalar=sup)
    return ld0, nb + NFB + NF // 4 + (1 if dram_b2 is not None else 0)


def emit_moe_down(c, e, FH, ld0, n_dma, tw=T, t0=0, dw_col0=A_DW, fg_tiles=None,
                  add_bias=True):
    nc = c.nc
    mm0 = c.s_mm.n
    ev0 = c.s_ev.n
    ntt_g = tw // 128
    NF = FH // 128 if fg_tiles is None else fg_tiles
    f_off = 0

    def tens(t):
        t.wait_ge(c.s_ld.h, ld0 + n_dma * 16)
        for tt in range(ntt_g):
            pr = (c.pp[0], c.pp[1]) if tt % 2 == 0 else (c.pp[2], c.pp[3])
            if tt >= 2:
                t.wait_ge(c.s_ev.h, ev0 + (tt - 2) * 2 + 2)
            for fc in range(NF):
                lhs = c.scrA[:, A_HT + (f_off + fc) * tw + tt * 128:
                             A_HT + (f_off + fc) * tw + tt * 128 + 128]
                for ch in range(2):
                    mm = t.matmul(pr[ch][:], lhs,
                                  c.scrA[:, dw_col0 + fc * 1024 + ch * 512:
                                         dw_col0 + fc * 1024 + ch * 512 + 512],
                                  start=(fc == 0),
                                  stop=(fc == NF - 1 and not add_bias))
                    if fc == NF - 1 and not add_bias:
                        c.s_mm.inc(mm)
            if add_bias:
                for ch in range(2):
                    c.s_mm.inc(t.matmul(
                        pr[ch][:], c.ones1[:],
                        c.rowb[0:1, 1024 + ch * 512:1536 + ch * 512],
                        start=False, stop=True))

    def vec(v):
        for tt in range(ntt_g):
            pr = (c.pp[0], c.pp[1]) if tt % 2 == 0 else (c.pp[2], c.pp[3])
            v.wait_ge(c.s_mm.h, mm0 + (tt + 1) * 2)
            gcol = c.gates[:, 128 + (t0 // 128 + tt) * 8 + e:
                           129 + (t0 // 128 + tt) * 8 + e]
            col0 = (t0 // 128 + tt) * C
            for ch in range(2):
                c.s_ev.inc(v.scalar_tensor_tensor(
                    c.stream[:, col0 + ch * 512:col0 + ch * 512 + 512],
                    pr[ch][:], gcol,
                    c.stream[:, col0 + ch * 512:col0 + ch * 512 + 512],
                    ALU.mult, ALU.add))
    phase(nc, tensor=tens, vector=vec)


def emit_l1_ffn(c, w):
    import os
    emit_ln(c)
    emit_router(c, w["L1_gate_w"], w["L1_gate_b"], renorm=False)
    for e in range(int(os.environ.get("K_EXPERTS", "8"))):
        for tg in range(2):
            ld0, nd = emit_moe_up_relu(c, e, w["L1_w1"], w["L1_b1"],
                                       w["L1_w2"], w["L1_b2"], 2048,
                                       t0=tg * 1024, tw=1024, load_b=(tg == 0))
            emit_moe_down(c, e, 2048, ld0, nd, tw=1024, t0=tg * 1024)




def emit_attn_qkv_wave(c, wave, Wq_ap, Wk_ap, Wv_ap, bq_col, bk_col, bv_row_ap,
                      fq0, fk0, fv0, qk_scale=0.125):
    nc = c.nc
    ld0 = c.s_ld.n
    mm0 = c.s_mm.n
    sc0 = c.s_sc.n
    ev0 = c.s_ev.n

    def g(g):
        c.s_ld.inc(g.dma_start(
            out=c.wU[:, 0:4096].rearrange("p (cc f) -> p cc f", f=512),
            in_=Wq_ap.rearrange("(cc p) f -> p cc f", p=128)[:, :, fq0:fq0 + 512]), 16)
        c.s_ld.inc(g.dma_start(
            out=c.wU[:, 4096:8192].rearrange("p (cc f) -> p cc f", f=512),
            in_=Wk_ap.rearrange("(cc p) f -> p cc f", p=128)[:, :, fk0:fk0 + 512]), 16)
        c.s_ld.inc(g.dma_start(
            out=c.wD[:, 0:4096].rearrange("p (cc f) -> p cc f", f=512),
            in_=Wv_ap.rearrange("(cc p) f -> p cc f", p=128)[:, :, fv0:fv0 + 512]), 16)
        c.s_ld.inc(g.dma_start(out=c.rowb[0:1, 0:512], in_=bv_row_ap), 16)

    def tens(t):
        t.wait_ge(c.s_ld.h, ld0 + 64)
        k = 0
        for which in range(2):
            for ft in range(4):
                for tg in range(4):
                    bank = c.pp[k % 4]
                    if k >= 4:
                        t.wait_ge(c.s_sc.h, sc0 + k - 4 + 1)
                    for cc in range(NCC):
                        mm = t.matmul(
                            bank[:],
                            c.wU[:, which * 4096 + cc * 512 + ft * 128:
                                 which * 4096 + cc * 512 + ft * 128 + 128],
                            c.xnT[:, cc * T + tg * 512:cc * T + tg * 512 + 512],
                            start=(cc == 0), stop=(cc == NCC - 1))
                        if cc == NCC - 1:
                            c.s_mm.inc(mm)
                    k += 1
        for tt in range(NTT):
            bank = c.pp[4 + tt % 2]
            if tt >= 2:
                t.wait_ge(c.s_ev.h, ev0 + (tt - 2) + 1)
            for cc in range(NCC):
                t.matmul(bank[:],
                         c.xnT[:, cc * T + tt * 128:cc * T + tt * 128 + 128],
                         c.wD[:, cc * 512:(cc + 1) * 512],
                         start=(cc == 0), stop=False)
            c.s_mm.inc(t.matmul(bank[:], c.ones1[:], c.rowb[0:1, 0:512],
                                start=False, stop=True))

    def sc(s):
        s.wait_ge(c.s_ld.h, ld0 + 64)
        k = 0
        for which in range(2):
            base = A_QT if which == 0 else A_KT
            bcol = bq_col if which == 0 else bk_col
            for ft in range(4):
                for tg in range(4):
                    s.wait_ge(c.s_mm.h, mm0 + k + 1)
                    kw = {}
                    if bcol is not None:
                        kw["bias"] = bcol(ft)
                    c.s_sc.inc(s.activation(
                        c.scrA[:, base + ft * T + tg * 512:
                               base + ft * T + tg * 512 + 512],
                        c.pp[k % 4][:], AF.Identity,
                        scale=(qk_scale if which == 0 else 1.0), **kw))
                    k += 1

    def vec(v):
        for tt in range(NTT):
            v.wait_ge(c.s_mm.h, mm0 + 32 + tt + 1)
            c.s_ev.inc(v.tensor_copy(
                c.scrA[:, A_VT + tt * 512:A_VT + (tt + 1) * 512],
                c.pp[4 + tt % 2][:]))
    phase(nc, gpsimd=g, tensor=tens, scalar=sc, vector=vec)


def emit_attn_core_wave(c, causal, nheads=8, l2_ext=False, vstep=512):
    nc = c.nc
    strips = [(hl, qt, (qt + 1) * 128 if causal else T)
              for hl in range(nheads) for qt in range(NTT)]
    N = len(strips)
    nkg = [(kl + 511) // 512 for (_, _, kl) in strips]
    nkt = [(kl + 127) // 128 for (_, _, kl) in strips]
    mm0 = c.s_mm.n
    mx0 = c.s_mx.n
    sc0 = c.s_sc.n
    pe0 = c.s_pe.n
    pt0 = c.s_pt.n
    pl0 = c.s_pl.n
    tp0 = c.s_tpos.n
    ev0 = c.s_ev.n
    s_done = {}
    pv_done = {}
    trans_done = {}
    mm = mm0
    tpn = tp0
    for i in range(-1, N + 2):
        if 0 <= i - 2 < N:
            mm += nkt[i - 2]
            pv_done[i - 2] = mm
        if 0 <= i + 1 < N:
            for kg in range(nkg[i + 1]):
                mm += 1
                s_done[(i + 1, kg)] = mm
        if 0 <= i - 1 < N:
            for kt in range(nkt[i - 1]):
                tpn += 1
                trans_done[(i - 1, kt)] = tpn
    maxred_done = {}
    scale_done = {}
    ptev_all = {}
    ot_done = {}
    mx = mx0
    pe = pe0
    pt = pt0
    ev = ev0
    gidx = 0
    for i in range(N + 2):
        if i < N:
            mx += nkg[i] + 1
            maxred_done[i] = mx
        if 1 <= i <= N:
            pe += 1
            scale_done[i - 1] = pe
        if 2 <= i < N + 2:
            j = i - 2
            pt += (nkt[j] + 7) // 8
            ptev_all[j] = pt
            if strips[j][1] % 4 == 3:
                ev += 1
                ot_done[gidx] = ev
                gidx += 1
    exp_done = {}
    sc = sc0
    for i in range(N):
        for kg in range(nkg[i]):
            sc += 1
            exp_done[(i, kg)] = sc
    exp_all = {i: exp_done[(i, nkg[i] - 1)] for i in range(N)}
    bank_user = {}
    s_bank_wait = {}
    for i in range(N):
        for kg in range(nkg[i]):
            if kg in bank_user:
                s_bank_wait[(i, kg)] = exp_done[(bank_user[kg], kg)]
            bank_user[kg] = i
    mask_done = {i: pl0 + i + 1 for i in range(N)}

    def qk_ap(hl, col, is_q, width):
        base = A_QT if is_q else A_KT
        if not l2_ext:
            tile = hl // 2
            row0 = (hl % 2) * 64
            return c.scrA[row0:row0 + 64, base + tile * T + col:
                          base + tile * T + col + width]
        return c.scrA[0:66, base + hl * T + col:base + hl * T + col + width]

    def tens(t):
        for i in range(-1, N + 2):
            if 0 <= i - 2 < N:
                j = i - 2
                hl, qt, kl = strips[j]
                t.wait_ge(c.s_pt.h, ptev_all[j])
                gj = j // 4
                if gj >= 2:
                    t.wait_ge(c.s_ev.h, ot_done[gj - 2])
                obank = c.pp[4 + gj % 2]
                tbuf = B_T0 if j % 2 == 0 else B_T1
                for kt in range(nkt[j]):
                    c.s_mm.inc(t.matmul(
                        obank[0:64, (qt % 4) * 128:(qt % 4) * 128 + 128],
                        c.scrA[:, A_VT + kt * vstep + hl * 64:
                               A_VT + kt * vstep + hl * 64 + 64],
                        c.scrB[:, tbuf + kt * 128:tbuf + (kt + 1) * 128],
                        start=(kt == 0), stop=(kt == nkt[j] - 1)))
            if 0 <= i + 1 < N:
                hl, qt, kl = strips[i + 1]
                for kg in range(nkg[i + 1]):
                    wdt = min(512, kl - kg * 512)
                    if (i + 1, kg) in s_bank_wait:
                        t.wait_ge(c.s_sc.h, s_bank_wait[(i + 1, kg)])
                    c.s_mm.inc(t.matmul(
                        c.pp[kg][:, 0:wdt],
                        qk_ap(hl, qt * 128, True, 128),
                        qk_ap(hl, kg * 512, False, wdt),
                        start=True, stop=True))
            if 0 <= i - 1 < N:
                j = i - 1
                t.wait_ge(c.s_pe.h, scale_done[j])
                if j >= 1:
                    t.wait_ge(c.s_pt.h, ptev_all[j - 1])
                pbuf = B_P0 if j % 2 == 0 else B_P1
                for kt in range(nkt[j]):
                    bank = c.tp[kt // 8]
                    c.s_tpos.inc(t.transpose(
                        bank[:, (kt % 8) * 128:(kt % 8 + 1) * 128],
                        c.scrB[:, pbuf + kt * 128:pbuf + (kt + 1) * 128],
                        c.ident[:]))

    def vec(v):
        for i in range(N + 2):
            if i < N:
                hl, qt, kl = strips[i]
                if i >= 2:
                    v.wait_ge(c.s_sc.h, exp_all[i - 2])
                for kg in range(nkg[i]):
                    wdt = min(512, kl - kg * 512)
                    v.wait_ge(c.s_mm.h, s_done[(i, kg)])
                    c.s_mx.inc(v.reduce_max(
                        c.stats[:, (i % 2) * 4 + kg:(i % 2) * 4 + kg + 1],
                        c.pp[kg][:, 0:wdt], axis=AX.X))
                v.wait_ge(c.s_mx.h, c.s_mx.n)
                c.s_mx.inc(v.reduce_max(
                    c.stats[:, 8 + i % 2:9 + i % 2],
                    c.stats[:, (i % 2) * 4:(i % 2) * 4 + nkg[i]],
                    axis=AX.X, negate=True))
            if 1 <= i <= N:
                j = i - 1
                hl, qt, kl = strips[j]
                pbuf = B_P0 if j % 2 == 0 else B_P1
                v.wait_ge(c.s_sc.h, exp_all[j])
                if causal:
                    v.wait_ge(c.s_pl.h, mask_done[j])
                    dkg = nkg[j] - 1
                    c.vser(v, v.reduce_sum(
                        c.stats[:, 10 + (j % 2) * 4 + dkg:11 + (j % 2) * 4 + dkg],
                        c.scrB[:, pbuf + dkg * 512:pbuf + kl], axis=AX.X))
                c.vser(v, v.reduce_sum(
                    c.stats[:, 18 + j % 2:19 + j % 2],
                    c.stats[:, 10 + (j % 2) * 4:10 + (j % 2) * 4 + nkg[j]],
                    axis=AX.X))
                c.vser(v, v.reciprocal(c.stats[:, 20 + j % 2:21 + j % 2],
                                       c.stats[:, 18 + j % 2:19 + j % 2]))
                c.s_pe.inc(v.tensor_scalar(
                    c.scrB[:, pbuf:pbuf + kl], c.scrB[:, pbuf:pbuf + kl],
                    c.stats[:, 20 + j % 2:21 + j % 2], None, ALU.mult))
            if 2 <= i < N + 2:
                j = i - 2
                hl, qt, kl = strips[j]
                tbuf = B_T0 if j % 2 == 0 else B_T1
                if j >= 2:
                    v.wait_ge(c.s_mm.h, pv_done[j - 2])
                for gq in range((nkt[j] + 7) // 8):
                    hi = min(nkt[j], (gq + 1) * 8)
                    v.wait_ge(c.s_tpos.h, trans_done[(j, hi - 1)])
                    wdt = (hi - gq * 8) * 128
                    bank = c.tp[gq]
                    c.s_pt.inc(v.tensor_copy(
                        c.scrB[:, tbuf + gq * 1024:tbuf + gq * 1024 + wdt],
                        bank[:, 0:wdt]))
                if qt % 4 == 3:
                    v.wait_ge(c.s_mm.h, pv_done[j])
                    gj = j // 4
                    obank = c.pp[4 + gj % 2]
                    ft = hl // 2
                    r0 = (hl % 2) * 64
                    c.s_ev.inc(v.tensor_copy(
                        c.scrA[r0:r0 + 64,
                               A_OT + ft * T + (qt - 3) * 128:
                               A_OT + ft * T + (qt + 1) * 128],
                        obank[0:64, 0:512]))

    def scal(s):
        for i in range(N):
            hl, qt, kl = strips[i]
            pbuf = B_P0 if i % 2 == 0 else B_P1
            s.wait_ge(c.s_mx.h, maxred_done[i])
            if i >= 2:
                s.wait_ge(c.s_tpos.h, trans_done[(i - 2, nkt[i - 2] - 1)])
            for kg in range(nkg[i]):
                wdt = min(512, kl - kg * 512)
                c.s_sc.inc(s.activation(
                    c.scrB[:, pbuf + kg * 512:pbuf + kg * 512 + wdt],
                    c.pp[kg][:, 0:wdt], AF.Exp,
                    bias=c.stats[:, 8 + i % 2:9 + i % 2],
                    accum_out=c.stats[:, 10 + (i % 2) * 4 + kg:
                                      11 + (i % 2) * 4 + kg]))

    def pool(g):
        if not causal:
            return
        for i in range(N):
            hl, qt, kl = strips[i]
            pbuf = B_P0 if i % 2 == 0 else B_P1
            g.wait_ge(c.s_sc.h, exp_all[i])
            c.s_pl.inc(g.affine_select(
                c.scrB[:, pbuf + qt * 128:pbuf + kl],
                c.scrB[:, pbuf + qt * 128:pbuf + kl],
                [[-1, 128]], ALU.is_ge, 0.0, base=0, channel_multiplier=1))
    if causal:
        phase(nc, tensor=tens, vector=vec, scalar=scal, gpsimd=pool)
    else:
        phase(nc, tensor=tens, vector=vec, scalar=scal)


def emit_attn_out_wave(c, wave, Wp_ap, brow_ap, nft=4):
    nc = c.nc
    ld0 = c.s_ld.n
    mm0 = c.s_mm.n
    ev0 = c.s_ev.n
    use_bias = (wave == 0 and brow_ap is not None)

    def g(g):
        c.s_ld.inc(g.dma_start(
            out=c.wD[:, 0:nft * 1024].rearrange("p (fc cw) -> p fc cw", cw=1024),
            in_=Wp_ap.rearrange("(fc p) cw -> p fc cw", p=128)[
                :, wave * nft:wave * nft + nft, :]), 16)
        if use_bias:
            c.s_ld.inc(g.dma_start(out=c.rowb[0:1, 0:1024], in_=brow_ap), 16)

    def tens(t):
        t.wait_ge(c.s_ld.h, ld0 + (32 if use_bias else 16))
        for tt in range(NTT):
            pr = (c.pp[0], c.pp[1]) if tt % 2 == 0 else (c.pp[2], c.pp[3])
            if tt >= 2:
                t.wait_ge(c.s_ev.h, ev0 + (tt - 2) * 2 + 2)
            for fc in range(nft):
                lhs = c.scrA[:, A_OT + fc * T + tt * 128:
                             A_OT + fc * T + tt * 128 + 128]
                for ch in range(2):
                    mm = t.matmul(pr[ch][:], lhs,
                                  c.wD[:, fc * 1024 + ch * 512:
                                       fc * 1024 + ch * 512 + 512],
                                  start=(fc == 0),
                                  stop=(fc == nft - 1 and not use_bias))
                    if fc == nft - 1 and not use_bias:
                        c.s_mm.inc(mm)
            if use_bias:
                for ch in range(2):
                    c.s_mm.inc(t.matmul(pr[ch][:], c.ones1[:],
                                        c.rowb[0:1, ch * 512:ch * 512 + 512],
                                        start=False, stop=True))

    def vec(v):
        for tt in range(NTT):
            pr = (c.pp[0], c.pp[1]) if tt % 2 == 0 else (c.pp[2], c.pp[3])
            v.wait_ge(c.s_mm.h, mm0 + (tt + 1) * 2)
            for ch in range(2):
                c.s_ev.inc(v.tensor_tensor(
                    c.stream[:, tt * C + ch * 512:tt * C + ch * 512 + 512],
                    c.stream[:, tt * C + ch * 512:tt * C + ch * 512 + 512],
                    pr[ch][:], ALU.add))
    phase(nc, gpsimd=g, tensor=tens, vector=vec)


def emit_l1_attn(c, w):
    import os
    emit_ln(c)
    nwaves = int(os.environ.get("K_ATTN_WAVES", "2"))
    for wave in range(nwaves):
        def bq(ft, wave=wave):
            return c.bfeat[:, wave * 4 + ft:wave * 4 + ft + 1]

        def bk(ft, wave=wave):
            return c.bfeat[:, 8 + wave * 4 + ft:8 + wave * 4 + ft + 1]
        if wave == 0:
            ld0 = c.s_ld.n

            def gb(g):
                c.s_ld.inc(g.dma_start(out=c.bfeat[:, 0:16],
                                       in_=w["L1_battn_feat"]), 16)
                g.wait_ge(c.s_ld.h, ld0 + 16)

            def gv(v):
                v.wait_ge(c.s_ld.h, ld0 + 16)
                v.tensor_scalar(c.bfeat[:, 0:8], c.bfeat[:, 0:8], 0.125,
                                None, ALU.mult)
            phase(c.nc, gpsimd=gb, vector=gv)
        emit_attn_qkv_wave(
            c, wave, w["L1_W_attn"], w["L1_W_attn"], w["L1_W_attn"],
            bq, bk, w["L1_bv_row"][:, wave * 512:wave * 512 + 512],
            fq0=wave * 512, fk0=1024 + wave * 512, fv0=2048 + wave * 512)
        if os.environ.get("K_ATTN_QKV_ONLY"):
            continue
        emit_attn_core_wave(c, causal=True,
                            nheads=int(os.environ.get("K_ATTN_HEADS", "8")))
        if os.environ.get("K_ATTN_NO_OUT"):
            continue
        emit_attn_out_wave(c, wave, w["L1_W_proj"], w["L1_bproj_row"])


def build_l1(c, w):
    import os
    parts = os.environ.get("K_PARTS", "fa f")
    for p in parts.split():
        for ch in p:
            if ch == "f":
                emit_l1_ffn(c, w)
            elif ch == "a":
                emit_l1_attn(c, w)
            elif ch == "n":
                emit_ln(c)
            elif ch == "r":
                emit_ln(c)
                emit_router(c, w["L1_gate_w"], w["L1_gate_b"], renorm=False)





def emit_moe_up_glu(c, e, dram_gw, dram_uw, dram_dw, FH2, fh, t0, tw):
    nc = c.nc
    ld0 = c.s_ld.n
    mm0 = c.s_mm.n
    sc0 = c.s_sc.n
    ev0 = c.s_ev.n
    pl0 = c.s_pl.n
    f0 = fh * 2048
    NFB = 2048 // 256
    ntg = tw // 512

    def dma_chunk(g, fb):
        for which, dw in ((0, dram_gw), (1, dram_uw)):
            c.s_ld.inc(g.dma_start(
                out=c.wU[:, which * 4096 + (fb % 2) * 2048:
                         which * 4096 + (fb % 2) * 2048 + 2048].rearrange(
                    "p (cc f) -> p cc f", f=256),
                in_=dw[e].rearrange("(cc p) f -> p cc f", p=128)[
                    :, :, f0 + fb * 256:f0 + (fb + 1) * 256]), 16)

    def g(g):
        for fb in range(2):
            dma_chunk(g, fb)
        for j in range(4):
            c.s_ld.inc(g.dma_start(
                out=c.scrA[:, A_DW + j * 4096:A_DW + (j + 1) * 4096].rearrange(
                    "p (fc cw) -> p fc cw", cw=1024),
                in_=dram_dw[e].rearrange("(fc p) cw -> p fc cw", p=128)[
                    :, fh * 16 + j * 4:fh * 16 + (j + 1) * 4, :]), 16)

    def tens(t):
        for fb in range(NFB):
            t.wait_ge(c.s_ld.h, ld0 + (fb + 1) * 32 + (64 if fb >= 2 else 0))
            for fs in range(2):
                for tg in range(ntg):
                    k = (fb * 2 + fs) * ntg + tg
                    pg = c.pp[2 * (k % 2)]
                    pu = c.pp[2 * (k % 2) + 1]
                    if k >= 2:
                        t.wait_ge(c.s_pl.h, pl0 + k - 2 + 1)
                    for which, bank in ((0, pg), (1, pu)):
                        for cc in range(NCC):
                            mm = t.matmul(
                                bank[:],
                                c.wU[:, which * 4096 + (fb % 2) * 2048 +
                                     cc * 256 + fs * 128:
                                     which * 4096 + (fb % 2) * 2048 +
                                     cc * 256 + fs * 128 + 128],
                                c.xnT[:, cc * T + t0 + tg * 512:
                                      cc * T + t0 + tg * 512 + 512],
                                start=(cc == 0), stop=(cc == NCC - 1))
                            if which == 1 and cc == NCC - 1:
                                c.s_mm.inc(mm)

    def scal(s):
        for k in range(NFB * 2 * ntg):
            pg = c.pp[2 * (k % 2)]
            s.wait_ge(c.s_mm.h, mm0 + k + 1)
            if k >= 2:
                s.wait_ge(c.s_pl.h, pl0 + k - 2 + 1)
            c.s_sc.inc(s.activation(
                c.scrB[:, (k % 2) * 512:(k % 2) * 512 + 512],
                pg[:], AF.Silu))

    def vec(v):
        for k in range(NFB * 2 * ntg):
            pu = c.pp[2 * (k % 2) + 1]
            v.wait_ge(c.s_mm.h, mm0 + k + 1)
            if k >= 2:
                v.wait_ge(c.s_pl.h, pl0 + k - 2 + 1)
            c.s_ev.inc(v.tensor_copy(
                c.scrB[:, 1024 + (k % 2) * 512:1024 + (k % 2) * 512 + 512],
                pu[:]))

    def pool(g):
        upf = 2 * ntg
        for k in range(NFB * 2 * ntg):
            fc = k // ntg
            tg = k % ntg
            g.wait_ge(c.s_sc.h, sc0 + k + 1)
            g.wait_ge(c.s_ev.h, ev0 + k + 1)
            c.s_pl.inc(g.tensor_tensor(
                c.scrA[:, A_HT + fc * tw + tg * 512:
                       A_HT + fc * tw + tg * 512 + 512],
                c.scrB[:, (k % 2) * 512:(k % 2) * 512 + 512],
                c.scrB[:, 1024 + (k % 2) * 512:1024 + (k % 2) * 512 + 512],
                ALU.mult))
            fb = k // upf
            if k % upf == upf - 1 and fb + 2 < NFB:
                dma_chunk(g, fb + 2)
    with nc.Block() as blk:
        blk.gpsimd(lambda gg: (g(gg), pool(gg)))
        blk.tensor(tens)
        blk.scalar(scal)
        blk.vector(vec)
    return ld0, 2 * NFB + 4


def emit_l2_ffn(c, w):
    import os
    emit_ln(c)
    emit_router(c, w["L2_router_w"], w["L2_router_b"], renorm=True)
    for e in range(int(os.environ.get("K_EXPERTS", "8"))):
        for fh in range(2):
            for tg in range(2):
                ld0, nd = emit_moe_up_glu(c, e, w["L2_gate_w"], w["L2_up_w"],
                                          w["L2_down_w"], 4096, fh,
                                          t0=tg * 1024, tw=1024)
                emit_moe_down(c, e, 2048, ld0, nd, tw=1024, t0=tg * 1024,
                              add_bias=False)


def emit_attn_qkv_wave_l2(c, w, wave):
    nc = c.nc
    ld0 = c.s_ld.n
    mm0 = c.s_mm.n
    sc0 = c.s_sc.n
    ev0 = c.s_ev.n
    f0 = wave * 256

    def g(g):
        c.s_ld.inc(g.dma_start(
            out=c.wU[:, 0:2048].rearrange("p (cc f) -> p cc f", f=256),
            in_=w["L2_Wq"].rearrange("(cc p) f -> p cc f", p=128)[
                :, :, f0:f0 + 256]), 16)
        c.s_ld.inc(g.dma_start(
            out=c.wU[:, 2048:4096].rearrange("p (cc f) -> p cc f", f=256),
            in_=w["L2_Wk"].rearrange("(cc p) f -> p cc f", p=128)[
                :, :, f0:f0 + 256]), 16)
        c.s_ld.inc(g.dma_start(
            out=c.wD[:, 0:2048].rearrange("p (cc f) -> p cc f", f=256),
            in_=w["L2_Wv"].rearrange("(cc p) f -> p cc f", p=128)[
                :, :, f0:f0 + 256]), 16)
        for hl in range(4):
            h = wave * 4 + hl
            c.s_ld.inc(g.dma_start(
                out=c.scrA[64:66, A_QT + hl * T:A_QT + (hl + 1) * T],
                in_=w["L2_aq"][h]), 16)
            c.s_ld.inc(g.dma_start(
                out=c.scrA[64:66, A_KT + hl * T:A_KT + (hl + 1) * T],
                in_=w["L2_ak"][h]), 16)

    def tens(t):
        t.wait_ge(c.s_ld.h, ld0 + 176)
        k = 0
        for which in range(2):
            for ft in range(2):
                for tg in range(4):
                    bank = c.pp[k % 4]
                    if k >= 4:
                        t.wait_ge(c.s_sc.h, sc0 + (k - 4 + 1) * 2)
                    for cc in range(NCC):
                        mm = t.matmul(
                            bank[:],
                            c.wU[:, which * 2048 + cc * 256 + ft * 128:
                                 which * 2048 + cc * 256 + ft * 128 + 128],
                            c.xnT[:, cc * T + tg * 512:cc * T + tg * 512 + 512],
                            start=(cc == 0), stop=(cc == NCC - 1))
                        if cc == NCC - 1:
                            c.s_mm.inc(mm)
                    k += 1
        for tt in range(NTT):
            bank = c.pp[4 + tt % 2]
            if tt >= 2:
                t.wait_ge(c.s_ev.h, ev0 + (tt - 2) + 1)
            for cc in range(NCC):
                mm = t.matmul(bank[:, 0:256],
                              c.xnT[:, cc * T + tt * 128:cc * T + tt * 128 + 128],
                              c.wD[:, cc * 256:(cc + 1) * 256],
                              start=(cc == 0), stop=(cc == NCC - 1))
                if cc == NCC - 1:
                    c.s_mm.inc(mm)

    def scal(s):
        k = 0
        for which in range(2):
            base = A_QT if which == 0 else A_KT
            for ft in range(2):
                for tg in range(4):
                    s.wait_ge(c.s_mm.h, mm0 + k + 1)
                    for sub in range(2):
                        hl = ft * 2 + sub
                        c.s_sc.inc(s.activation(
                            c.scrA[0:64, base + hl * T + tg * 512:
                                   base + hl * T + tg * 512 + 512],
                            c.pp[k % 4][sub * 64:sub * 64 + 64, :],
                            AF.Identity,
                            scale=(0.125 if which == 0 else 1.0)))
                    k += 1

    def vec(v):
        for tt in range(NTT):
            v.wait_ge(c.s_mm.h, mm0 + 16 + tt + 1)
            c.s_ev.inc(v.tensor_copy(
                c.scrA[:, A_VT + tt * 256:A_VT + (tt + 1) * 256],
                c.pp[4 + tt % 2][:, 0:256]))
    phase(nc, gpsimd=g, tensor=tens, scalar=scal, vector=vec)


def emit_l2_attn(c, w):
    import os
    emit_ln(c)
    nwaves = int(os.environ.get("K_ATTN_WAVES4", "4"))
    for wave in range(nwaves):
        emit_attn_qkv_wave_l2(c, w, wave)
        emit_attn_core_wave(c, causal=True, nheads=4, l2_ext=True, vstep=256)
        emit_attn_out_wave(c, wave, w["L2_Wo"],
                           w["L2_bo_row"] if wave == 0 else None, nft=2)


def build_l2(c, w):
    import os
    parts = os.environ.get("K_PARTS", "fa f")
    for p in parts.split():
        for ch in p:
            if ch == "f":
                emit_l2_ffn(c, w)
            elif ch == "a":
                emit_l2_attn(c, w)
            elif ch == "n":
                emit_ln(c)




def emit_l3_rope(c, which):
    nc = c.nc
    base = A_QT if which == 0 else A_KT
    COS = c.scrBf[:, 2048:4096]
    SIN = c.scrBf[:, 0:2048]

    def vec_rope(v):
        for ft in range(4):
            X = c.scrA
            x0 = base + ft * T
            v.tensor_tensor(c.scrB[0:64, 4096:6144], X[0:64, x0:x0 + T],
                            COS[0:64, :], ALU.mult)
            v.tensor_tensor(c.scrB[0:64, 6144:8192], X[64:128, x0:x0 + T],
                            SIN[64:128, :], ALU.mult)
            v.tensor_tensor(c.scrB[64:128, 4096:6144], X[0:64, x0:x0 + T],
                            SIN[0:64, :], ALU.mult)
            v.tensor_tensor(c.scrB[64:128, 6144:8192], X[64:128, x0:x0 + T],
                            COS[64:128, :], ALU.mult)
            v.tensor_tensor(X[0:64, x0:x0 + T], c.scrB[0:64, 4096:6144],
                            c.scrB[0:64, 6144:8192], ALU.subtract)
            v.tensor_tensor(X[64:128, x0:x0 + T], c.scrB[64:128, 4096:6144],
                            c.scrB[64:128, 6144:8192], ALU.add)
    phase(nc, vector=vec_rope)


def emit_l3_rms(c, w, which, wave):
    nc = c.nc
    base = A_QT if which == 0 else A_KT
    for half in range(2):
        fts = (2 * half, 2 * half + 1)

        def pA(s, fts=fts):
            for j, ft in enumerate(fts):
                s.activation(c.scrB[:, 4096 + j * 2048:4096 + j * 2048 + 2048],
                             c.scrA[:, base + ft * T:base + (ft + 1) * T],
                             AF.Square)
        phase(nc, scalar=pA)

        mm0 = c.s_mm.n
        ev0 = c.s_ev.n

        def pB_t(t, fts=fts, mm0=mm0, ev0=ev0):
            k = 0
            for j, ft in enumerate(fts):
                for tg in range(4):
                    if k >= 2:
                        t.wait_ge(c.s_ev.h, ev0 + k - 2 + 1)
                    c.s_mm.inc(t.matmul(
                        c.pp[4 + k % 2][0:2, 0:512], c.csel[:, 4:6],
                        c.scrB[:, 4096 + j * 2048 + tg * 512:
                               4096 + j * 2048 + tg * 512 + 512],
                        start=True, stop=True))
                    k += 1

        def pB_v(v, fts=fts, mm0=mm0):
            k = 0
            for j, ft in enumerate(fts):
                for tg in range(4):
                    v.wait_ge(c.s_mm.h, mm0 + k + 1)
                    c.s_ev.inc(v.tensor_copy(
                        c.scrBf[0:2, j * 2048 + tg * 512:
                                j * 2048 + tg * 512 + 512],
                        c.pp[4 + k % 2][0:2, 0:512]))
                    k += 1
        phase(nc, tensor=pB_t, vector=pB_v)

        def pC(v):
            v.tensor_scalar(c.scrBf[0:2, 0:4096], c.scrBf[0:2, 0:4096],
                            1.0 / 64, 1e-6, ALU.mult, ALU.add)
        phase(nc, vector=pC)

        def pD(s):
            s.activation(c.scrBf[0:2, 0:4096], c.scrBf[0:2, 0:4096], AF.Sqrt)
        phase(nc, scalar=pD)

        def pE(v):
            with nc.allow_low_precision(reason="rstd feeds bf16 matmul"):
                v.reciprocal(c.rowb[0:2, 0:2048], c.scrBf[0:2, 0:2048])
                v.reciprocal(c.scrB[0:2, 4096:6144], c.scrBf[0:2, 2048:4096])
        phase(nc, vector=pE)

        mm1 = c.s_mm.n
        ev1 = c.s_ev.n

        def pF_t(t, fts=fts, mm1=mm1, ev1=ev1):
            k = 0
            for j, ft in enumerate(fts):
                rstd = c.rowb[0:2, 0:2048] if j == 0 else c.scrB[0:2, 4096:6144]
                for tg in range(4):
                    if k >= 2:
                        t.wait_ge(c.s_ev.h, ev1 + k - 2 + 1)
                    c.s_mm.inc(t.matmul(
                        c.pp[4 + k % 2][:, 0:512], c.rsel[:],
                        rstd[:, tg * 512:tg * 512 + 512],
                        start=True, stop=True))
                    k += 1

        def pF_v(v, fts=fts, mm1=mm1):
            k = 0
            for j, ft in enumerate(fts):
                x0 = base + ft * T
                for tg in range(4):
                    v.wait_ge(c.s_mm.h, mm1 + k + 1)
                    c.s_ev.inc(v.tensor_tensor(
                        c.scrA[:, x0 + tg * 512:x0 + tg * 512 + 512],
                        c.scrA[:, x0 + tg * 512:x0 + tg * 512 + 512],
                        c.pp[4 + k % 2][:, 0:512], ALU.mult))
                    k += 1
                gc = c.bfeat[:, 16 + which * 8 + wave * 4 + ft:
                             17 + which * 8 + wave * 4 + ft]
                v.tensor_scalar(c.scrA[:, x0:x0 + T], c.scrA[:, x0:x0 + T],
                                gc, None, ALU.mult)
        phase(nc, tensor=pF_t, vector=pF_v)


def emit_l3_attn(c, w):
    import os
    emit_ln(c)
    ld0 = c.s_ld.n

    def gb(g):
        c.s_ld.inc(g.dma_start(out=c.bfeat[:, 16:24], in_=w["L3_qg_col"]), 16)
        c.s_ld.inc(g.dma_start(out=c.bfeat[:, 24:32], in_=w["L3_kg_col"]), 16)
        c.s_ld.inc(g.dma_start(out=c.bfeat[:, 32:48], in_=w["L3_b_qkv_feat"]), 16)
        g.wait_ge(c.s_ld.h, ld0 + 48)

    def gv(v):
        v.wait_ge(c.s_ld.h, ld0 + 48)
        v.tensor_scalar(c.bfeat[:, 16:24], c.bfeat[:, 16:24], 0.125,
                        None, ALU.mult)
    phase(c.nc, gpsimd=gb, vector=gv)
    nwaves = int(os.environ.get("K_ATTN_WAVES", "2"))
    for wave in range(nwaves):
        def bq(ft, wave=wave):
            return c.bfeat[:, 32 + wave * 4 + ft:33 + wave * 4 + ft]

        def bk(ft, wave=wave):
            return c.bfeat[:, 40 + wave * 4 + ft:41 + wave * 4 + ft]
        emit_attn_qkv_wave(
            c, wave, w["L3_w_qkv"], w["L3_w_qkv"], w["L3_w_qkv"],
            bq, bk, w["L3_bv_row"][:, wave * 512:wave * 512 + 512],
            fq0=wave * 512, fk0=1024 + wave * 512, fv0=2048 + wave * 512,
            qk_scale=1.0)
        ld1 = c.s_ld.n

        def gcs(g, ld1=ld1):
            c.s_ld.inc(g.dma_start(out=c.scrBf[:, 2048:4096], in_=w["L3_cos"]), 16)
            c.s_ld.inc(g.dma_start(out=c.scrBf[:, 0:2048], in_=w["L3_sin"]), 16)
            g.wait_ge(c.s_ld.h, ld1 + 32)
        phase(c.nc, gpsimd=gcs)
        emit_l3_rope(c, 0)
        emit_l3_rope(c, 1)
        emit_l3_rms(c, w, 0, wave)
        emit_l3_rms(c, w, 1, wave)
        if os.environ.get("K_ATTN_QKV_ONLY"):
            continue
        emit_attn_core_wave(c, causal=False,
                            nheads=int(os.environ.get("K_ATTN_HEADS", "8")))
        if os.environ.get("K_ATTN_NO_OUT"):
            continue
        emit_attn_out_wave(c, wave, w["L3_w_out"],
                           w["L3_bout_row"] if wave == 0 else None)


def emit_l3_prenorm(c, w):
    nc = c.nc
    ld0 = c.s_ld.n
    mm0 = c.s_mm.n

    def g(g):
        c.s_ld.inc(g.dma_start(out=c.rowb[0:1, 0:1024],
                               in_=w["L3_norm_scale_row"]), 16)

    def p1s(s):
        for tt in range(NTT):
            s.activation(c.scrBf[:, (tt % 4) * 1024:(tt % 4) * 1024 + 1024],
                         c.scrA[:, A_XN + tt * C:A_XN + (tt + 1) * C],
                         AF.Square, accum_out=c.stats[:, 16 + tt:17 + tt])

    def p1v(v):
        for tt in range(NTT):
            v.scalar_tensor_tensor(
                c.stream[:, tt * C:(tt + 1) * C],
                c.scrA[:, A_XN + tt * C:A_XN + (tt + 1) * C], 0.5,
                c.stream[:, tt * C:(tt + 1) * C], ALU.mult, ALU.add)

    def p1t(t):
        t.wait_ge(c.s_ld.h, ld0 + 16)
        for ch in range(2):
            c.s_mm.inc(t.matmul(c.pp[4 + ch][:], c.ones1[:],
                                c.rowb[0:1, ch * 512:ch * 512 + 512],
                                start=True, stop=True))
    phase(nc, gpsimd=g, scalar=p1s, vector=p1v, tensor=p1t)

    def p2v(v):
        c.vser(v, v.tensor_scalar(c.stats[:, 16:32], c.stats[:, 16:32],
                                  1.0 / C, 1e-5, ALU.mult, ALU.add))
    phase(nc, vector=p2v)

    def p3s(s):
        s.activation(c.stats[:, 16:32], c.stats[:, 16:32], AF.Sqrt)
    phase(nc, scalar=p3s)

    def p4v(v):
        c.vser(v, v.reciprocal(c.stats[:, 48:64], c.stats[:, 16:32]))
        c.vser(v, v.tensor_copy(c.scrB[:, 0:512], c.pp[4][:]))
        c.vser(v, v.tensor_copy(c.scrB[:, 512:1024], c.pp[5][:]))
        for tt in range(NTT):
            v.tensor_scalar(
                c.scrA[:, A_XN + tt * C:A_XN + (tt + 1) * C],
                c.scrA[:, A_XN + tt * C:A_XN + (tt + 1) * C],
                c.stats[:, 48 + tt:49 + tt], None, ALU.mult)
        for tt in range(NTT):
            v.tensor_tensor(
                c.scrA[:, A_XN + tt * C:A_XN + (tt + 1) * C],
                c.scrA[:, A_XN + tt * C:A_XN + (tt + 1) * C],
                c.scrB[:, 0:1024], ALU.mult)
    phase(nc, vector=p4v)
    emit_transpose_to(c, c.scrA, A_XN, c.xnT, 0)


def emit_moe_up_swiglu(c, e, dram_w1, dram_b1, dram_w2, dram_b2, t0, tw,
                       load_b=True):
    nc = c.nc
    ld0 = c.s_ld.n
    mm0 = c.s_mm.n
    sc0 = c.s_sc.n
    ev0 = c.s_ev.n
    pl0 = c.s_pl.n
    NFB = 2048 // 256
    ntg = tw // 512
    nb = 1 if load_b else 0

    def dma_chunk(g, fb):
        for which in range(2):
            c.s_ld.inc(g.dma_start(
                out=c.wU[:, which * 4096 + (fb % 2) * 2048:
                         which * 4096 + (fb % 2) * 2048 + 2048].rearrange(
                    "p (cc f) -> p cc f", f=256),
                in_=dram_w1[e].rearrange("(cc p) f -> p cc f", p=128)[
                    :, :, which * 2048 + fb * 256:
                    which * 2048 + (fb + 1) * 256]), 16)

    def g(g):
        if load_b:
            c.s_ld.inc(g.dma_start(out=c.bfeat[:, 0:32], in_=dram_b1[e]), 16)
        for fb in range(2):
            dma_chunk(g, fb)
        for j in range(4):
            c.s_ld.inc(g.dma_start(
                out=c.scrA[:, A_DW + j * 4096:A_DW + (j + 1) * 4096].rearrange(
                    "p (fc cw) -> p fc cw", cw=1024),
                in_=dram_w2[e].rearrange("(fc p) cw -> p fc cw", p=128)[
                    :, j * 4:(j + 1) * 4, :]), 16)
        c.s_ld.inc(g.dma_start(out=c.rowb[0:1, 1024:2048],
                               in_=dram_b2[e:e + 1, :]), 16)

    def tens(t):
        for fb in range(NFB):
            t.wait_ge(c.s_ld.h,
                      ld0 + (nb + 2 * (fb + 1) + (5 if fb >= 2 else 0)) * 16)
            for fs in range(2):
                for tg in range(ntg):
                    k = (fb * 2 + fs) * ntg + tg
                    pg = c.pp[2 * (k % 2)]
                    pl_ = c.pp[2 * (k % 2) + 1]
                    if k >= 2:
                        t.wait_ge(c.s_pl.h, pl0 + k - 2 + 1)
                    for which, bank in ((0, pg), (1, pl_)):
                        for cc in range(NCC):
                            mm = t.matmul(
                                bank[:],
                                c.wU[:, which * 4096 + (fb % 2) * 2048 +
                                     cc * 256 + fs * 128:
                                     which * 4096 + (fb % 2) * 2048 +
                                     cc * 256 + fs * 128 + 128],
                                c.xnT[:, cc * T + t0 + tg * 512:
                                      cc * T + t0 + tg * 512 + 512],
                                start=(cc == 0), stop=(cc == NCC - 1))
                            if which == 1 and cc == NCC - 1:
                                c.s_mm.inc(mm)

    def vec(v):
        for k in range(NFB * 2 * ntg):
            fc = k // ntg
            pg = c.pp[2 * (k % 2)]
            pl_ = c.pp[2 * (k % 2) + 1]
            v.wait_ge(c.s_mm.h, mm0 + k + 1)
            if k >= 2:
                v.wait_ge(c.s_pl.h, pl0 + k - 2 + 1)
            v.tensor_scalar(c.scrB[:, (k % 2) * 512:(k % 2) * 512 + 512],
                            pg[:], c.bfeat[:, fc:fc + 1], 7.0,
                            ALU.add, ALU.min)
            c.s_ev.inc(v.tensor_scalar(
                c.scrB[:, 1024 + (k % 2) * 512:1024 + (k % 2) * 512 + 512],
                pl_[:], c.bfeat[:, 16 + fc:17 + fc], 7.0,
                ALU.add, ALU.min))

    def scal(s):
        s.wait_ge(c.s_ld.h, ld0 + 16)
        for k in range(NFB * 2 * ntg):
            s.wait_ge(c.s_ev.h, ev0 + k + 1)
            if k >= 2:
                s.wait_ge(c.s_pl.h, pl0 + k - 2 + 1)
            c.s_sc.inc(s.activation(
                c.scrB[:, 2048 + (k % 2) * 512:2048 + (k % 2) * 512 + 512],
                c.scrB[:, (k % 2) * 512:(k % 2) * 512 + 512],
                AF.Gelu_apprx_sigmoid))

    def pool(g):
        for k in range(NFB * 2 * ntg):
            fc = k // ntg
            tg = k % ntg
            g.wait_ge(c.s_sc.h, sc0 + k + 1)
            g.tensor_scalar(
                c.scrB[:, 1024 + (k % 2) * 512:1024 + (k % 2) * 512 + 512],
                c.scrB[:, 1024 + (k % 2) * 512:1024 + (k % 2) * 512 + 512],
                -7.0, 1.0, ALU.max, ALU.add)
            c.s_pl.inc(g.tensor_tensor(
                c.scrA[:, A_HT + fc * tw + tg * 512:
                       A_HT + fc * tw + tg * 512 + 512],
                c.scrB[:, 2048 + (k % 2) * 512:2048 + (k % 2) * 512 + 512],
                c.scrB[:, 1024 + (k % 2) * 512:1024 + (k % 2) * 512 + 512],
                ALU.mult))
            fb = k // (2 * ntg)
            if k % (2 * ntg) == 2 * ntg - 1 and fb + 2 < NFB:
                dma_chunk(g, fb + 2)

    with nc.Block() as blk:
        blk.gpsimd(lambda gg: (g(gg), pool(gg)))
        blk.tensor(tens)
        blk.vector(vec)
        blk.scalar(scal)
    return ld0, nb + 2 * NFB + 4 + 1


def emit_l3_ffn(c, w):
    import os
    emit_ln(c)
    emit_l3_prenorm(c, w)
    emit_router(c, w["L3_gate_w"], w["L3_gate_b"], renorm=True)
    for e in range(int(os.environ.get("K_EXPERTS", "8"))):
        for tg in range(2):
            ld0, nd = emit_moe_up_swiglu(c, e, w["L3_mlp1_w"], w["L3_mlp1_b"],
                                         w["L3_mlp2_w"], w["L3_mlp2_b"],
                                         t0=tg * 1024, tw=1024,
                                         load_b=(tg == 0))
            emit_moe_down(c, e, 2048, ld0, nd, tw=1024, t0=tg * 1024)


def build_l3(c, w):
    import os
    parts = os.environ.get("K_PARTS", "fa f")
    for p in parts.split():
        for ch in p:
            if ch == "f":
                emit_l3_ffn(c, w)
            elif ch == "a":
                emit_l3_attn(c, w)
            elif ch == "n":
                emit_ln(c)





def emit_l0_ffn(c, w):
    nc = c.nc
    emit_ln(c)
    ld0 = c.s_ld.n

    def g0(g):
        c.s_ld.inc(g.dma_start(out=c.bfeat[:, 0:8], in_=w["L0_mu_r"]), 16)
        c.s_ld.inc(g.dma_start(out=c.bfeat[:, 8:16], in_=w["L0_mu_k"]), 16)
        g.wait_ge(c.s_ld.h, ld0 + 32)

    def v0(v):
        v.wait_ge(c.s_ld.h, ld0 + 32)
        c.vser(v, v.tensor_scalar(c.bfeat[:, 16:32], c.bfeat[:, 0:16],
                                  -1.0, 1.0, ALU.mult, ALU.add))
    phase(nc, gpsimd=g0, vector=v0)

    XR, XK = 0, 16384

    def v1(v):
        for which, base, mu0, om0 in ((0, XR, 0, 16), (1, XK, 8, 24)):
            for cc in range(NCC):
                v.tensor_scalar(
                    c.scrA[:, base + cc * T:base + (cc + 1) * T],
                    c.xnT[:, cc * T:(cc + 1) * T],
                    c.bfeat[:, mu0 + cc:mu0 + cc + 1], None, ALU.mult)
            for cc in range(NCC):
                v.scalar_tensor_tensor(
                    c.scrA[:, base + cc * T + 1:base + (cc + 1) * T],
                    c.xnT[:, cc * T:(cc + 1) * T - 1],
                    c.bfeat[:, om0 + cc:om0 + cc + 1],
                    c.scrA[:, base + cc * T + 1:base + (cc + 1) * T],
                    ALU.mult, ALU.add)
    phase(nc, vector=v1)

    for which in range(2):
        src_base = XR if which == 0 else XK
        dram_w = w["L0_W_r"] if which == 0 else w["L0_W_k"]
        ld1 = c.s_ld.n
        mm0 = c.s_mm.n
        sc0 = c.s_sc.n
        pl0 = c.s_pl.n

        def gw_(g, dram_w=dram_w, ld1=ld1, mm0=mm0):
            for fb in range(2):
                c.s_ld.inc(g.dma_start(
                    out=c.wU[:, fb * 4096:fb * 4096 + 4096].rearrange(
                        "p (cc f) -> p cc f", f=512),
                    in_=dram_w.rearrange("(cc p) f -> p cc f", p=128)[
                        :, :, fb * 512:(fb + 1) * 512]), 16)

        def tns(t, src_base=src_base, ld1=ld1, mm0=mm0, sc0=sc0):
            for fb in range(2):
                t.wait_ge(c.s_ld.h, ld1 + (fb + 1) * 16)
                for fs in range(4):
                    for tg in range(4):
                        k = (fb * 4 + fs) * 4 + tg
                        bank = c.pp[k % 4]
                        if k >= 4:
                            t.wait_ge(c.s_sc.h, sc0 + k - 4 + 1)
                        for cc in range(NCC):
                            mm = t.matmul(
                                bank[:],
                                c.wU[:, fb * 4096 + cc * 512 + fs * 128:
                                     fb * 4096 + cc * 512 + fs * 128 + 128],
                                c.scrA[:, src_base + cc * T + tg * 512:
                                       src_base + cc * T + tg * 512 + 512],
                                start=(cc == 0), stop=(cc == NCC - 1))
                            if cc == NCC - 1:
                                c.s_mm.inc(mm)

        if which == 0:
            def ev_(s, mm0=mm0):
                for k in range(32):
                    fc = k // 4
                    tg = k % 4
                    s.wait_ge(c.s_mm.h, mm0 + k + 1)
                    c.s_sc.inc(s.activation(
                        c.xnT[:, fc * T + tg * 512:fc * T + tg * 512 + 512],
                        c.pp[k % 4][:], AF.Sigmoid))
            phase(nc, gpsimd=gw_, tensor=tns, scalar=ev_)
        else:
            def ev_(s, mm0=mm0):
                for k in range(32):
                    fc = k // 4
                    tg = k % 4
                    s.wait_ge(c.s_mm.h, mm0 + k + 1)
                    c.s_sc.inc(s.activation(
                        c.scrA[:, XR + fc * T + tg * 512:
                               XR + fc * T + tg * 512 + 512],
                        c.pp[k % 4][:], AF.Relu))

            def sq_(g, sc0=sc0, pl0=pl0):
                for k in range(32):
                    fc = k // 4
                    tg = k % 4
                    g.wait_ge(c.s_sc.h, sc0 + k + 1)
                    sl = c.scrA[:, XR + fc * T + tg * 512:
                                XR + fc * T + tg * 512 + 512]
                    c.s_pl.inc(g.tensor_tensor(sl, sl, sl, ALU.mult))
            phase(nc, gpsimd2=None) if False else \
                phase(nc, gpsimd=lambda gg: (gw_(gg), sq_(gg)), tensor=tns,
                      scalar=ev_)

    ld2 = c.s_ld.n
    mm1 = c.s_mm.n
    ev1 = c.s_ev.n
    pl1 = c.s_pl.n

    def gv_(g):
        for fb in range(2):
            c.s_ld.inc(g.dma_start(
                out=c.wU[:, fb * 4096:fb * 4096 + 4096].rearrange(
                    "p (cc f) -> p cc f", f=512),
                in_=w["L0_W_v"].rearrange("(cc p) f -> p cc f", p=128)[
                    :, :, fb * 512:(fb + 1) * 512]), 16)

    def tv_(t):
        for fb in range(2):
            t.wait_ge(c.s_ld.h, ld2 + (fb + 1) * 16)
            for fs in range(4):
                for tg in range(4):
                    k = (fb * 4 + fs) * 4 + tg
                    bank = c.pp[k % 4]
                    if k >= 4:
                        t.wait_ge(c.s_ev.h, ev1 + k - 4 + 1)
                    for cc in range(NCC):
                        mm = t.matmul(
                            bank[:],
                            c.wU[:, fb * 4096 + cc * 512 + fs * 128:
                                 fb * 4096 + cc * 512 + fs * 128 + 128],
                            c.scrA[:, XR + cc * T + tg * 512:
                                   XR + cc * T + tg * 512 + 512],
                            start=(cc == 0), stop=(cc == NCC - 1))
                        if cc == NCC - 1:
                            c.s_mm.inc(mm)

    def vv_(v):
        for k in range(32):
            fc = k // 4
            tg = k % 4
            v.wait_ge(c.s_mm.h, mm1 + k + 1)
            c.s_ev.inc(v.tensor_tensor(
                c.scrA[:, XK + fc * T + tg * 512:XK + fc * T + tg * 512 + 512],
                c.xnT[:, fc * T + tg * 512:fc * T + tg * 512 + 512],
                c.pp[k % 4][:], ALU.mult))
    phase(nc, gpsimd=gv_, tensor=tv_, vector=vv_)

    tp0 = c.s_tpos.n
    ev2 = c.s_ev.n

    def tt_(t):
        k = 0
        for tt in range(NTT):
            for cc in range(NCC):
                if k >= 16:
                    t.wait_ge(c.s_ev.h, ev2 + (k - 16) // 8 + 1)
                slot = k % 16
                bank = c.tp[slot // 8]
                c.s_tpos.inc(t.transpose(
                    bank[:, (slot % 8) * 128:(slot % 8 + 1) * 128],
                    c.scrA[:, XK + cc * T + tt * 128:XK + cc * T + tt * 128 + 128],
                    c.ident[:]))
                k += 1

    def tv2_(v):
        for tt in range(NTT):
            v.wait_ge(c.s_tpos.h, tp0 + (tt + 1) * 8)
            bank = c.tp[tt % 2]
            c.s_ev.inc(v.scalar_tensor_tensor(
                c.stream[:, tt * C:(tt + 1) * C], bank[:], 0.5,
                c.stream[:, tt * C:(tt + 1) * C], ALU.mult, ALU.add))
    phase(nc, tensor=tt_, vector=tv2_)


def emit_l0_attn(c, w):
    nc = c.nc
    emit_ln(c)
    ld0 = c.s_ld.n
    X2 = 16384

    def g0(g):
        c.s_ld.inc(g.dma_start(out=c.bfeat[:, 0:8], in_=w["L0_norm_w"]), 16)
        c.s_ld.inc(g.dma_start(out=c.bfeat[:, 8:10], in_=w["L0_b_in"]), 16)
        c.s_ld.inc(g.dma_start(out=c.bfeat[0:64, 10:11], in_=w["L0_b_x"]), 16)
        c.s_ld.inc(g.dma_start(out=c.bfeat[0:64, 11:12], in_=w["L0_b_dt"]), 16)
        c.s_ld.inc(g.dma_start(out=c.rowb[0:1, 0:1024], in_=w["L0_b_out"]), 16)
        c.s_ld.inc(g.dma_start(out=c.wD[0:64, 0:64], in_=w["L0_W_x"]), 16)
        c.s_ld.inc(g.dma_start(out=c.wD[64:128, 0:64], in_=w["L0_W_dt"]), 16)
        c.s_ld.inc(g.dma_start(
            out=c.wD[0:64, 1024:2048], in_=w["L0_W_out"]), 16)
        c.s_ld.inc(g.dma_start(
            out=c.wU[:, 0:1536].rearrange("p (cc f) -> p cc f", f=192),
            in_=w["L0_W_in"].rearrange("(cc p) f -> p cc f", p=128)), 16)

    def s0(s):
        for tt in range(NTT):
            s.activation(c.scrBf[:, (tt % 4) * 1024:(tt % 4) * 1024 + 1024],
                         c.scrA[:, A_XN + tt * C:A_XN + (tt + 1) * C],
                         AF.Square, accum_out=c.stats[:, 16 + tt:17 + tt])

    def v0(v):
        for tt in range(NTT):
            v.tensor_tensor(c.stream[:, tt * C:(tt + 1) * C],
                            c.stream[:, tt * C:(tt + 1) * C],
                            c.scrA[:, A_XN + tt * C:A_XN + (tt + 1) * C],
                            ALU.add)
    phase(nc, gpsimd=g0, scalar=s0, vector=v0)

    def v1(v):
        c.vser(v, v.tensor_scalar(c.stats[:, 16:32], c.stats[:, 16:32],
                                  1.0 / C, 1e-8, ALU.mult, ALU.add))
    phase(nc, vector=v1)

    def s1(s):
        s.activation(c.stats[:, 16:32], c.stats[:, 16:32], AF.Sqrt)
    phase(nc, scalar=s1)

    def v2(v):
        c.vser(v, v.reciprocal(c.stats[:, 48:64], c.stats[:, 16:32]))
        for tt in range(NTT):
            v.tensor_scalar(
                c.scrA[:, X2 + tt * C:X2 + (tt + 1) * C],
                c.scrA[:, A_XN + tt * C:A_XN + (tt + 1) * C],
                c.stats[:, 48 + tt:49 + tt], None, ALU.mult)
        v.wait_ge(c.s_ld.h, ld0 + 144)
        for cc in range(NCC):
            v.tensor_scalar(c.wU[:, cc * 192:(cc + 1) * 192],
                            c.wU[:, cc * 192:(cc + 1) * 192],
                            c.bfeat[:, cc:cc + 1], None, ALU.mult)
    phase(nc, vector=v2)
    emit_transpose_to(c, c.scrA, X2, c.xnT, 0)

    mm0 = c.s_mm.n
    sc0 = c.s_sc.n

    def t1(t):
        for ft in range(2):
            M = 128 if ft == 0 else 64
            for tg in range(4):
                bank = c.pp[(ft * 4 + tg) % 4]
                if ft * 4 + tg >= 4:
                    t.wait_ge(c.s_sc.h, sc0 + ft * 4 + tg - 4 + 1)
                for cc in range(NCC):
                    mm = t.matmul(
                        bank[0:M, :],
                        c.wU[:, cc * 192 + ft * 128:cc * 192 + ft * 128 + M],
                        c.xnT[:, cc * T + tg * 512:cc * T + tg * 512 + 512],
                        start=(cc == 0), stop=(cc == NCC - 1))
                    if cc == NCC - 1:
                        c.s_mm.inc(mm)

    def s2(s):
        for ft in range(2):
            M = 128 if ft == 0 else 64
            for tg in range(4):
                k = ft * 4 + tg
                s.wait_ge(c.s_mm.h, mm0 + k + 1)
                c.s_sc.inc(s.activation(
                    c.scrA[0:M, ft * T + tg * 512:ft * T + tg * 512 + 512],
                    c.pp[k % 4][0:M, :], AF.Identity,
                    bias=c.bfeat[0:M, 8 + ft:9 + ft]))
    phase(nc, tensor=t1, scalar=s2)

    mm1 = c.s_mm.n
    sc1 = c.s_sc.n

    def t2(t):
        for which in range(2):
            r0 = which * 64
            for tg in range(4):
                k = which * 4 + tg
                bank = c.pp[k % 4]
                if k >= 4:
                    t.wait_ge(c.s_sc.h, sc1 + k - 4 + 1)
                c.s_mm.inc(t.matmul(
                    bank[0:64, :],
                    c.wD[r0:r0 + 64, 0:64],
                    c.scrA[r0:r0 + 64, tg * 512:tg * 512 + 512],
                    start=True, stop=True))

    def s3(s):
        for which in range(2):
            for tg in range(4):
                k = which * 4 + tg
                s.wait_ge(c.s_mm.h, mm1 + k + 1)
                c.s_sc.inc(s.activation(
                    c.scrBf[0:64, which * 2048 + tg * 512:
                            which * 2048 + tg * 512 + 512],
                    c.pp[k % 4][0:64, :],
                    AF.Tanh if which == 0 else AF.Sigmoid,
                    bias=c.bfeat[0:64, 10 + which:11 + which]))
    phase(nc, tensor=t2, scalar=s3)

    def v3(v):
        v.tensor_scalar(c.scrBf[64:128, 0:2048], c.scrBf[0:64, 2048:4096],
                        -1.0, 1.0, ALU.mult, ALU.add)
        c.vser(v, v.tensor_tensor(c.scrBf[64:128, 2048:4096],
                                  c.scrBf[0:64, 2048:4096],
                                  c.scrBf[0:64, 0:2048], ALU.mult))
        v.tensor_tensor_scan(c.scrA[0:64, 4096:6144],
                             c.scrBf[64:128, 0:2048],
                             c.scrBf[64:128, 2048:4096],
                             0.0, ALU.mult, ALU.add)
    phase(nc, vector=v3)

    def s4(s):
        s.activation(c.scrA[0:64, 6144:8192], c.scrA[0:64, T:T + 2048],
                     AF.Sigmoid)
    phase(nc, scalar=s4)

    def v4(v):
        v.tensor_tensor(c.scrA[0:64, 8192:10240], c.scrA[0:64, 4096:6144],
                        c.scrA[0:64, 6144:8192], ALU.mult)
    phase(nc, vector=v4)

    mm2 = c.s_mm.n
    ev0 = c.s_ev.n

    def t3(t):
        for tt in range(NTT):
            pr = (c.pp[0], c.pp[1]) if tt % 2 == 0 else (c.pp[2], c.pp[3])
            if tt >= 2:
                t.wait_ge(c.s_ev.h, ev0 + (tt - 2) * 2 + 2)
            for ch in range(2):
                t.matmul(pr[ch][:],
                         c.scrA[0:64, 8192 + tt * 128:8192 + tt * 128 + 128],
                         c.wD[0:64, 1024 + ch * 512:1024 + ch * 512 + 512],
                         start=True, stop=False)
                c.s_mm.inc(t.matmul(pr[ch][:], c.ones1[:],
                                    c.rowb[0:1, ch * 512:ch * 512 + 512],
                                    start=False, stop=True))

    def v5(v):
        for tt in range(NTT):
            pr = (c.pp[0], c.pp[1]) if tt % 2 == 0 else (c.pp[2], c.pp[3])
            v.wait_ge(c.s_mm.h, mm2 + (tt + 1) * 2)
            for ch in range(2):
                c.s_ev.inc(v.tensor_tensor(
                    c.stream[:, tt * C + ch * 512:tt * C + ch * 512 + 512],
                    c.stream[:, tt * C + ch * 512:tt * C + ch * 512 + 512],
                    pr[ch][:], ALU.add))
    phase(nc, tensor=t3, vector=v5)


def build_l0(c, w):
    import os
    parts = os.environ.get("K_PARTS", "fa f")
    for p in parts.split():
        for ch in p:
            if ch == "f":
                emit_l0_ffn(c, w)
            elif ch == "a":
                emit_l0_attn(c, w)
            elif ch == "n":
                emit_ln(c)



L1_KEYS = ["L1_W_attn", "L1_battn_feat", "L1_bv_row", "L1_W_proj",
           "L1_bproj_row", "L1_gate_w", "L1_gate_b", "L1_w1", "L1_b1",
           "L1_w2", "L1_b2"]
L0_KEYS = ["L0_W_in", "L0_b_in", "L0_W_x", "L0_b_x", "L0_W_dt", "L0_b_dt",
           "L0_W_out", "L0_b_out", "L0_norm_w", "L0_W_r", "L0_W_k", "L0_W_v",
           "L0_mu_r", "L0_mu_k"]
L2_KEYS = ["L2_Wq", "L2_Wk", "L2_Wv", "L2_Wo", "L2_bo_row", "L2_router_w",
           "L2_router_b", "L2_gate_w", "L2_up_w", "L2_down_w", "L2_aq", "L2_ak"]
L3_KEYS = ["L3_w_qkv", "L3_b_qkv_feat", "L3_bv_row", "L3_w_out", "L3_bout_row",
           "L3_qg_col", "L3_kg_col", "L3_cos", "L3_sin", "L3_norm_scale",
           "L3_norm_scale_row", "L3_gate_w",
           "L3_gate_b", "L3_mlp1_w", "L3_mlp1_b", "L3_mlp2_w", "L3_mlp2_b"]
LAYER_KEYS = {0: L0_KEYS, 1: L1_KEYS, 2: L2_KEYS, 3: L3_KEYS}


def build_program(layers, wdict):
    import os
    nc = bass.Bass("TRN2", target_bir_lowering=False,
                   detect_race_conditions=not os.environ.get("K_NO_RACE_DETECT"))
    keys = []
    for li in layers:
        keys += LAYER_KEYS[li]
    x_in = nc.dram_tensor("x", [T, C], F32, kind="ExternalInput").ap()
    y_out = nc.dram_tensor("y", [T, C], F32, kind="ExternalOutput").ap()
    dram = {}
    for k in keys:
        arr = wdict[k]
        dram[k] = nc.dram_tensor(k, list(arr.shape), F32, kind="ExternalInput").ap()
    with ExitStack() as es:
        c = alloc_ctx(nc, es)
        emit_setup(c, x_in)
        for li in layers:
            if li == 0:
                build_l0(c, dram)
            elif li == 1:
                build_l1(c, dram)
            elif li == 2:
                build_l2(c, dram)
            elif li == 3:
                build_l3(c, dram)
        emit_store(c, y_out)
        import os
        dumps = os.environ.get("K_DUMP", "")
        if dumps:
            for nm in dumps.split(","):
                tn = getattr(c, nm)
                dt = nc.dram_tensor("dump_" + nm, list(tn.shape),
                                    tn.dtype, kind="ExternalOutput").ap()
                st0 = c.s_st.n

                def sdump(s, tn=tn, dt=dt, st0=st0):
                    c.s_st.inc(s.dma_start(out=dt, in_=tn[:]), 16)
                    s.wait_ge(c.s_st.h, st0 + 16)
                phase(nc, sync=sdump)
    return nc, keys


def kernel(**inputs):
    w = host_prep(inputs)
    x = _as_f32(inputs["L0_attn_x"])
    import os
    gs = os.environ.get("K_GROUPS", "0123")
    groups = [[int(ch) for ch in g] for g in gs.split(",")]
    cur = [np.ascontiguousarray(x[b]) for b in range(8)]
    for layers in groups:
        nc, keys = build_program(layers, w)
        in_maps = []
        for b in range(8):
            m = {"x": cur[b]}
            for k in keys:
                m[k] = w[k]
            in_maps.append(m)
        res = run_bass_kernel_spmd(nc, in_maps, core_ids=list(range(8)))
        cur = [np.asarray(res.results[b]["y"]) for b in range(8)]
    return np.stack(cur, axis=0)


# version 04
# speedup vs baseline: 1.2776x; 1.2331x over previous
from contextlib import ExitStack
import numpy as np
import concourse.bass as bass
import concourse.mybir as mybir
from concourse.bass_utils import run_bass_kernel_spmd

F32 = mybir.dt.float32
BF16 = mybir.dt.bfloat16
AF = mybir.ActivationFunctionType
ALU = mybir.AluOpType
AX = mybir.AxisListType

T = 2048
C = 1024
NTT = T // 128
NCC = C // 128

A_XN = 0
A_HT = 0
A_DW = 16384
A_QT = 0
A_KT = 8192
A_VT = 16384
A_OT = 24576
B_P0, B_P1, B_T0, B_T1 = 0, 2048, 4096, 6144


def _as_f32(x):
    return np.ascontiguousarray(np.asarray(x), dtype=np.float32)


def host_prep(inp):
    d = {}
    d["L0_W_in"] = _as_f32(inp["L0_attn_W_in"])
    b = _as_f32(inp["L0_attn_b_in"])
    bi = np.zeros((128, 2), np.float32)
    bi[:, 0] = b[0:128]
    bi[0:64, 1] = b[128:192]
    d["L0_b_in"] = bi
    d["L0_W_x"] = _as_f32(inp["L0_attn_W_x"])
    d["L0_b_x"] = _as_f32(inp["L0_attn_b_x"]).reshape(64, 1)
    d["L0_W_dt"] = _as_f32(inp["L0_attn_W_dt"])
    d["L0_b_dt"] = _as_f32(inp["L0_attn_b_dt"]).reshape(64, 1)
    d["L0_W_out"] = _as_f32(inp["L0_attn_W_out"])
    d["L0_b_out"] = _as_f32(inp["L0_attn_b_out"]).reshape(1, C)
    d["L0_norm_w"] = _as_f32(inp["L0_attn_norm_w"]).reshape(NCC, 128).T.copy()
    d["L0_W_r"] = _as_f32(inp["L0_ffn_W_r"]).T.copy()
    d["L0_W_k"] = _as_f32(inp["L0_ffn_W_k"]).T.copy()
    d["L0_W_v"] = _as_f32(inp["L0_ffn_W_v"]).T.copy()
    d["L0_mu_r"] = _as_f32(inp["L0_ffn_mu_r"]).reshape(NCC, 128).T.copy()
    d["L0_mu_k"] = _as_f32(inp["L0_ffn_mu_k"]).reshape(NCC, 128).T.copy()

    d["L1_W_attn"] = _as_f32(inp["L1_attn_W_attn"])
    ba = _as_f32(inp["L1_attn_b_attn"])
    d["L1_battn_feat"] = ba[:2048].reshape(16, 128).T.copy()
    d["L1_bv_row"] = ba[2048:].reshape(1, C)
    d["L1_W_proj"] = _as_f32(inp["L1_attn_W_proj"])
    d["L1_bproj_row"] = _as_f32(inp["L1_attn_b_proj"]).reshape(1, C)
    d["L1_gate_w"] = _as_f32(inp["L1_ffn_gate_w"])
    d["L1_gate_b"] = _as_f32(inp["L1_ffn_gate_b"]).reshape(1, 8)
    d["L1_w1"] = _as_f32(inp["L1_ffn_w1"])
    d["L1_b1"] = _as_f32(inp["L1_ffn_b1"]).reshape(8, 16, 128).transpose(0, 2, 1).copy()
    d["L1_w2"] = _as_f32(inp["L1_ffn_w2"])
    d["L1_b2"] = _as_f32(inp["L1_ffn_b2"])

    d["L2_Wq"] = _as_f32(inp["L2_attn_Wq"])
    d["L2_Wk"] = _as_f32(inp["L2_attn_Wk"])
    d["L2_Wv"] = _as_f32(inp["L2_attn_Wv"])
    d["L2_Wo"] = _as_f32(inp["L2_attn_Wo"])
    d["L2_bo_row"] = _as_f32(inp["L2_attn_bo"]).reshape(1, C)
    d["L2_router_w"] = _as_f32(inp["L2_ffn_router_w"]).T.copy()
    d["L2_router_b"] = _as_f32(inp["L2_ffn_router_b"]).reshape(1, 8)
    d["L2_gate_w"] = _as_f32(inp["L2_ffn_gate_w"]).transpose(0, 2, 1).copy()
    d["L2_up_w"] = _as_f32(inp["L2_ffn_up_w"]).transpose(0, 2, 1).copy()
    d["L2_down_w"] = _as_f32(inp["L2_ffn_down_w"]).transpose(0, 2, 1).copy()
    H = 16
    slope = (1.0 / ((2.0 ** 8) ** (1.0 / H)) ** np.arange(1, H + 1)).astype(np.float32)
    pos = np.arange(T, dtype=np.float32)
    d["L2_aq"] = np.ascontiguousarray(
        np.stack([np.outer(slope, pos), np.ones((H, T), np.float32)], axis=1))
    d["L2_ak"] = np.ascontiguousarray(
        np.stack([np.ones((H, T), np.float32), -np.outer(slope, pos)], axis=1))

    DH = 64
    perm = np.zeros(C, dtype=np.int64)
    for hp in range(H // 2):
        base = hp * 128
        perm[base + 0:base + 32] = (2 * hp) * DH + np.arange(0, DH, 2)
        perm[base + 32:base + 64] = (2 * hp + 1) * DH + np.arange(0, DH, 2)
        perm[base + 64:base + 96] = (2 * hp) * DH + np.arange(1, DH, 2)
        perm[base + 96:base + 128] = (2 * hp + 1) * DH + np.arange(1, DH, 2)
    wq = _as_f32(inp["L3_attn_w_qkv"]).T.copy()
    wq3 = np.concatenate([wq[:, perm], wq[:, C + perm], wq[:, 2 * C:]], axis=1)
    d["L3_w_qkv"] = np.ascontiguousarray(wq3)
    bq = _as_f32(inp["L3_attn_b_qkv"])
    bq3 = np.concatenate([bq[perm], bq[C + perm], bq[2 * C:]])
    d["L3_b_qkv_feat"] = bq3[:2048].reshape(16, 128).T.copy()
    d["L3_bv_row"] = bq3[2048:].reshape(1, C)
    d["L3_w_out"] = _as_f32(inp["L3_attn_w_out"]).T.copy()
    d["L3_bout_row"] = _as_f32(inp["L3_attn_b_out"]).reshape(1, C)

    def gcol(g):
        out = np.zeros((128, 8), np.float32)
        for hp in range(8):
            out[0:32, hp] = g[2 * hp, 0::2]
            out[32:64, hp] = g[2 * hp + 1, 0::2]
            out[64:96, hp] = g[2 * hp, 1::2]
            out[96:128, hp] = g[2 * hp + 1, 1::2]
        return out
    d["L3_qg_col"] = gcol(_as_f32(inp["L3_attn_q_gamma"]))
    d["L3_kg_col"] = gcol(_as_f32(inp["L3_attn_k_gamma"]))
    half = DH // 2
    invf = (1.0 / (10000.0 ** (np.arange(half, dtype=np.float32) / half))).astype(np.float32)
    ang = pos[None, :] * invf[:, None]
    d["L3_cos"] = np.ascontiguousarray(np.concatenate(
        [np.cos(ang)] * 4, axis=0).astype(np.float32))
    d["L3_sin"] = np.ascontiguousarray(np.concatenate(
        [np.sin(ang)] * 4, axis=0).astype(np.float32))
    d["L3_norm_scale_row"] = _as_f32(inp["L3_ffn_norm_scale"]).reshape(1, C)
    d["L3_norm_scale"] = _as_f32(inp["L3_ffn_norm_scale"]).reshape(NCC, 128).T.copy()
    d["L3_gate_w"] = _as_f32(inp["L3_ffn_gate_w"])
    d["L3_gate_b"] = _as_f32(inp["L3_ffn_gate_b"]).reshape(1, 8)
    m1 = _as_f32(inp["L3_ffn_mlp1_w"])
    I2 = 4096
    permh = np.concatenate([np.arange(0, I2, 2), np.arange(1, I2, 2)])
    d["L3_mlp1_w"] = np.ascontiguousarray(m1[:, :, permh])
    b1 = _as_f32(inp["L3_ffn_mlp1_b"])[:, permh]
    d["L3_mlp1_b"] = b1.reshape(8, 32, 128).transpose(0, 2, 1).copy()
    d["L3_mlp2_w"] = _as_f32(inp["L3_ffn_mlp2_w"])
    d["L3_mlp2_b"] = _as_f32(inp["L3_ffn_mlp2_b"])
    return d


class Sem:
    def __init__(self, es, nc, name):
        self.h = es.enter_context(nc.semaphore(name))
        self.n = 0

    def inc(self, instr, k=1):
        instr.then_inc(self.h, k)
        self.n += k
        return instr


class Ctx:
    pass


def alloc_ctx(nc, es):
    c = Ctx()
    c.nc = nc

    def sbt(name, shape, dt):
        return es.enter_context(nc.sbuf_tensor(name, shape, dt))
    c.stream = sbt("stream", [128, NTT * C], F32)
    c.xnT = sbt("xnT", [128, NCC * T], BF16)
    c.scrA = sbt("scrA", [128, 32768], BF16)
    c.scrB = sbt("scrB", [128, 8192], BF16)
    c.scrBf = c.scrB.bitcast(F32)
    c.wU = sbt("wU", [128, 8192], BF16)
    c.wD = sbt("wD", [128, 4096], BF16)
    c.rowb = sbt("rowb", [128, 2048], BF16)
    c.stats = sbt("stats", [128, 64], F32)
    c.gates = sbt("gates", [128, 256], F32)
    c.bfeat = sbt("bfeat", [128, 64], F32)
    c.cvec = sbt("cvec", [128, 24], F32)
    c.csel = sbt("csel", [128, 8], BF16)
    c.rsel = sbt("rsel", [2, 128], BF16)
    c.ones1 = sbt("ones1", [1, 128], BF16)
    c.ident = sbt("ident", [128, 128], BF16)
    c.pp = [es.enter_context(nc.psum_tensor(f"p{i}", [128, 512], F32)) for i in range(6)]
    c.tp = [es.enter_context(nc.psum_tensor(f"tp{i}", [128, 1024], BF16)) for i in range(2)]
    for nm in ["ld", "mm", "ev", "sc", "pl", "tpos", "mx", "pe", "pt", "st",
               "vs"]:
        setattr(c, "s_" + nm, Sem(es, nc, "s_" + nm))

    def vser(eng, instr):
        instr.then_inc(c.s_vs.h, 1)
        c.s_vs.n += 1
        eng.wait_ge(c.s_vs.h, c.s_vs.n)
        return instr
    c.vser = vser
    return c


def phase(nc, **fns):
    with nc.Block() as blk:
        for name, f in fns.items():
            getattr(blk, name)(f)




def emit_setup(c, x_in):
    nc = c.nc
    ld0 = c.s_ld.n

    def g(g):
        c.s_ld.inc(g.dma_start(
            out=c.stream[:].rearrange("p (tt c) -> p tt c", c=C),
            in_=x_in.rearrange("(tt p) c -> p tt c", p=128)), 16)
        g.memset(c.ones1[:], 1.0)
        c.vser(g, g.memset(c.ident[:], 1.0))
        g.affine_select(c.ident[:], c.ident[:], [[-1, 128]], ALU.is_equal, 0.0,
                        base=0, channel_multiplier=1)
        g.memset(c.cvec[:, 16:17], 1e-5)
        g.memset(c.cvec[:, 17:18], 1e-6)
        g.memset(c.cvec[:, 18:19], 1e-8)
        c.vser(g, g.memset(c.csel[:], 1.0))
        c.vser(g, g.affine_select(c.csel[:, 0:4], c.csel[:, 0:4], [[-32, 4]],
                                  ALU.is_ge, 0.0, base=0, channel_multiplier=1))
        g.affine_select(c.csel[:, 0:4], c.csel[:, 0:4], [[32, 4]], ALU.is_ge,
                        0.0, base=31, channel_multiplier=-1)
        c.vser(g, g.memset(c.rsel[:], 1.0))
        g.affine_select(
            c.rsel[:].rearrange("p (a h j) -> p a h j", a=2, h=2, j=32),
            c.rsel[:].rearrange("p (a h j) -> p a h j", a=2, h=2, j=32),
            [[0, 2], [-1, 2], [0, 32]], ALU.is_equal, 0.0,
            base=0, channel_multiplier=1)
        g.wait_ge(c.s_ld.h, ld0 + 16)
    phase(nc, gpsimd=g)

    def v(v):
        v.tensor_add(c.csel[:, 4:5], c.csel[:, 0:1], c.csel[:, 2:3])
        v.tensor_add(c.csel[:, 5:6], c.csel[:, 1:2], c.csel[:, 3:4])
    phase(nc, vector=v)


def emit_store(c, y_out):
    nc = c.nc
    st0 = c.s_st.n

    def s(s):
        c.s_st.inc(s.dma_start(
            out=y_out.rearrange("(tt p) c -> p tt c", p=128),
            in_=c.stream[:].rearrange("p (tt c) -> p tt c", c=C)), 16)
        s.wait_ge(c.s_st.h, st0 + 16)
    phase(nc, sync=s)


def emit_ln(c):
    nc = c.nc
    xn = c.scrA

    def p1s(s):
        for tt in range(NTT):
            s.activation(c.scrBf[:, (tt % 4) * 1024:(tt % 4) * 1024 + 1024],
                         c.stream[:, tt * C:(tt + 1) * C],
                         AF.Square, accum_out=c.stats[:, 16 + tt:17 + tt])

    def p1v(v):
        for tt in range(NTT):
            v.reduce_sum(c.stats[:, tt:tt + 1],
                         c.stream[:, tt * C:(tt + 1) * C], axis=AX.X)
    phase(nc, scalar=p1s, vector=p1v)

    def p2v(v):
        st = c.stats
        c.vser(v, v.tensor_scalar(c.cvec[:, 0:16], st[:, 0:16], 1.0 / C, None,
                                  ALU.mult))
        c.vser(v, v.tensor_scalar(st[:, 16:32], st[:, 16:32], 1.0 / C, None,
                                  ALU.mult))
        c.vser(v, v.tensor_tensor(st[:, 32:48], c.cvec[:, 0:16],
                                  c.cvec[:, 0:16], ALU.mult))
        v.tensor_tensor(st[:, 16:32], st[:, 16:32], st[:, 32:48], ALU.subtract)
    phase(nc, vector=p2v)

    def p3s(s):
        s.activation(c.stats[:, 16:32], c.stats[:, 16:32], AF.Sqrt, bias=c.cvec[:, 16:17])
    phase(nc, scalar=p3s)

    def p4v(v):
        c.vser(v, v.reciprocal(c.stats[:, 48:64], c.stats[:, 16:32]))
        for tt in range(NTT):
            v.tensor_scalar(
                xn[:, A_XN + tt * C:A_XN + (tt + 1) * C],
                c.stream[:, tt * C:(tt + 1) * C],
                c.cvec[:, tt:tt + 1], c.stats[:, 48 + tt:49 + tt],
                ALU.subtract, ALU.mult)
    phase(nc, vector=p4v)
    emit_transpose_to(c, xn, A_XN, c.xnT, 0)


def emit_transpose_to(c, src, src_col0, dstT, dst_col0, ncc=NCC, ntt=NTT):
    nc = c.nc
    tp0 = c.s_tpos.n
    xe0 = c.s_ev.n

    def tens(t):
        k = 0
        for cc in range(ncc):
            for tt in range(ntt):
                if k >= 16:
                    t.wait_ge(c.s_ev.h, xe0 + (k - 16) // 8 + 1)
                slot = k % 16
                bank = c.tp[slot // 8]
                c.s_tpos.inc(t.transpose(
                    bank[:, (slot % 8) * 128:(slot % 8 + 1) * 128],
                    src[:, src_col0 + tt * ncc * 128 + cc * 128:
                        src_col0 + tt * ncc * 128 + (cc + 1) * 128],
                    c.ident[:]))
                k += 1

    def vec(v):
        k = 0
        for cc in range(ncc):
            for tt in range(ntt):
                k += 1
                if k % 8 == 0:
                    v.wait_ge(c.s_tpos.h, tp0 + k)
                    bank = c.tp[((k - 8) % 16) // 8]
                    c.s_ev.inc(v.tensor_copy(
                        dstT[:, dst_col0 + cc * ntt * 128 + (tt - 7) * 128:
                             dst_col0 + cc * ntt * 128 + (tt + 1) * 128],
                        bank[:]))
    phase(nc, tensor=tens, vector=vec)


def emit_router(c, dram_gw, dram_gb, renorm, scale_col=None):
    nc = c.nc
    ld0 = c.s_ld.n
    mm0 = c.s_mm.n
    ev0 = c.s_ev.n
    LG = c.scrBf

    def g(g):
        c.s_ld.inc(g.dma_start(
            out=c.wD[:, 0:64].rearrange("p (cc e) -> p cc e", e=8),
            in_=dram_gw.rearrange("(cc p) e -> p cc e", p=128)), 16)
        c.s_ld.inc(g.dma_start(out=c.rowb[0:1, 0:8], in_=dram_gb), 16)

    def pre_v(v):
        if scale_col is not None:
            v.wait_ge(c.s_ld.h, ld0 + 32)
            for cc in range(NCC):
                v.tensor_scalar(c.wD[:, cc * 8:cc * 8 + 8],
                                c.wD[:, cc * 8:cc * 8 + 8],
                                scale_col(cc), None, ALU.mult)
    phase(nc, gpsimd=g, vector=pre_v)

    def tens(t):
        t.wait_ge(c.s_ld.h, ld0 + 32)
        for tt in range(NTT):
            bank = c.pp[tt % 4]
            if tt >= 4:
                t.wait_ge(c.s_ev.h, ev0 + (tt - 4) + 1)
            for cc in range(NCC):
                t.matmul(bank[:, 0:8],
                         c.xnT[:, cc * T + tt * 128: cc * T + tt * 128 + 128],
                         c.wD[:, cc * 8:cc * 8 + 8],
                         start=(cc == 0), stop=False)
            c.s_mm.inc(t.matmul(bank[:, 0:8], c.ones1[:], c.rowb[0:1, 0:8],
                                start=False, stop=True))

    def vec0(v):
        for tt in range(NTT):
            v.wait_ge(c.s_mm.h, mm0 + tt + 1)
            c.s_ev.inc(v.tensor_copy(LG[:, tt * 8:tt * 8 + 8],
                                     c.pp[tt % 4][:, 0:8]))
    phase(nc, tensor=tens, vector=vec0)

    import os
    if os.environ.get("K_GATE_STOP"):
        return
    st = c.stats
    if not renorm:
        def v1(v):
            for tt in range(NTT):
                v.reduce_max(st[:, 32 + tt:33 + tt], LG[:, tt * 8:tt * 8 + 8],
                             axis=AX.X, negate=True)
        phase(nc, vector=v1)

        def sc(s):
            for tt in range(NTT):
                s.activation(LG[:, 256 + tt * 8:264 + tt * 8],
                             LG[:, tt * 8:tt * 8 + 8], AF.Exp,
                             bias=st[:, 32 + tt:33 + tt],
                             accum_out=st[:, 48 + tt:49 + tt])
        phase(nc, scalar=sc)

        def v2(v):
            E = LG[:, 256:384]
            P = LG[:, 640:768]
            c.vser(v, v.reciprocal(st[:, 0:16], st[:, 48:64]))
            for tt in range(NTT):
                v.tensor_scalar(P[:, tt * 8:tt * 8 + 8],
                                E[:, tt * 8:tt * 8 + 8],
                                st[:, tt:tt + 1], None, ALU.mult)
            for tt in range(NTT):
                v.reduce_max(st[:, 16 + tt:17 + tt], P[:, tt * 8:tt * 8 + 8],
                             axis=AX.X)
            for tt in range(NTT):
                v.tensor_scalar(LG[:, 384 + tt * 8:392 + tt * 8],
                                P[:, tt * 8:tt * 8 + 8],
                                st[:, 16 + tt:17 + tt], None, ALU.is_ge)
            for tt in range(NTT):
                v.scalar_tensor_tensor(LG[:, 512 + tt * 8:520 + tt * 8],
                                       LG[:, 384 + tt * 8:392 + tt * 8],
                                       -2.0, P[:, tt * 8:tt * 8 + 8],
                                       ALU.mult, ALU.add)
            for tt in range(NTT):
                v.reduce_max(st[:, 32 + tt:33 + tt],
                             LG[:, 512 + tt * 8:520 + tt * 8], axis=AX.X)
            for tt in range(NTT):
                v.tensor_scalar(LG[:, 384 + tt * 8:392 + tt * 8],
                                P[:, tt * 8:tt * 8 + 8],
                                st[:, 32 + tt:33 + tt], None, ALU.is_ge)
            for tt in range(NTT):
                v.tensor_tensor(c.gates[:, tt * 8:tt * 8 + 8],
                                P[:, tt * 8:tt * 8 + 8],
                                LG[:, 384 + tt * 8:392 + tt * 8], ALU.mult)
            v.tensor_scalar(c.gates[:, 128:256], c.gates[:, 0:128], 0.5,
                            None, ALU.mult)
        phase(nc, vector=v2)
    else:
        def v1(v):
            for tt in range(NTT):
                v.reduce_max(st[:, tt:tt + 1], LG[:, tt * 8:tt * 8 + 8],
                             axis=AX.X)
            for tt in range(NTT):
                v.tensor_scalar(LG[:, 256 + tt * 8:264 + tt * 8],
                                LG[:, tt * 8:tt * 8 + 8],
                                st[:, tt:tt + 1], None, ALU.is_ge)
            for tt in range(NTT):
                v.scalar_tensor_tensor(LG[:, 384 + tt * 8:392 + tt * 8],
                                       LG[:, 256 + tt * 8:264 + tt * 8],
                                       -1e30, LG[:, tt * 8:tt * 8 + 8],
                                       ALU.mult, ALU.add)
            for tt in range(NTT):
                v.reduce_max(st[:, 16 + tt:17 + tt],
                             LG[:, 384 + tt * 8:392 + tt * 8], axis=AX.X)
            for tt in range(NTT):
                v.tensor_tensor(st[:, 32 + tt:33 + tt], st[:, tt:tt + 1],
                                st[:, 16 + tt:17 + tt], ALU.subtract)
        phase(nc, vector=v1)

        def sc(s):
            s.activation(st[:, 48:64], st[:, 32:48], AF.Sigmoid)
        phase(nc, scalar=sc)

        def v2(v):
            os_ = c.cvec[:, 0:16]
            c.vser(v, v.tensor_scalar(os_, st[:, 48:64], -1.0, 1.0, ALU.mult,
                                      ALU.add))
            for tt in range(NTT):
                v.tensor_scalar(LG[:, 512 + tt * 8:520 + tt * 8],
                                LG[:, 384 + tt * 8:392 + tt * 8],
                                st[:, 16 + tt:17 + tt], None, ALU.is_ge)
            for tt in range(NTT):
                v.tensor_scalar(LG[:, 256 + tt * 8:264 + tt * 8],
                                LG[:, 256 + tt * 8:264 + tt * 8],
                                st[:, 48 + tt:49 + tt], None, ALU.mult)
            for tt in range(NTT):
                v.tensor_scalar(LG[:, 512 + tt * 8:520 + tt * 8],
                                LG[:, 512 + tt * 8:520 + tt * 8],
                                os_[:, tt:tt + 1], None, ALU.mult)
            for tt in range(NTT):
                v.tensor_tensor(c.gates[:, tt * 8:tt * 8 + 8],
                                LG[:, 256 + tt * 8:264 + tt * 8],
                                LG[:, 512 + tt * 8:520 + tt * 8], ALU.add)
            v.tensor_scalar(c.gates[:, 128:256], c.gates[:, 0:128], 0.5,
                            None, ALU.mult)
        phase(nc, vector=v2)




def emit_moe_up_relu(c, e, dram_w1, dram_b1, dram_w2, dram_b2, FH, t0, tw,
                     load_b=True):
    nc = c.nc
    ld0 = c.s_ld.n
    mm0 = c.s_mm.n
    sc0 = c.s_sc.n
    NFB = FH // 512
    ntg = tw // 512
    NF = FH // 128
    nb = 1 if load_b else 0

    def g(g):
        if load_b:
            c.s_ld.inc(g.dma_start(out=c.bfeat[:, 0:NF], in_=dram_b1[e]), 16)
        for fb in range(NFB):
            if fb >= 2:
                g.wait_ge(c.s_mm.h, mm0 + (fb - 1) * 4 * ntg)
            c.s_ld.inc(g.dma_start(
                out=c.wU[:, (fb % 2) * 4096:(fb % 2) * 4096 + 4096].rearrange(
                    "p (cc f) -> p cc f", f=512),
                in_=dram_w1[e].rearrange("(cc p) f -> p cc f", p=128)[
                    :, :, fb * 512:(fb + 1) * 512]), 16)
        for j in range(NF // 4):
            c.s_ld.inc(g.dma_start(
                out=c.scrA[:, A_DW + j * 4096:A_DW + (j + 1) * 4096].rearrange(
                    "p (fc cw) -> p fc cw", cw=1024),
                in_=dram_w2[e].rearrange("(fc p) cw -> p fc cw", p=128)[
                    :, j * 4:(j + 1) * 4, :]), 16)
        if dram_b2 is not None:
            c.s_ld.inc(g.dma_start(out=c.rowb[0:1, 1024:2048],
                                   in_=dram_b2[e:e + 1, :]), 16)

    def tens(t):
        for fb in range(NFB):
            t.wait_ge(c.s_ld.h, ld0 + (nb + fb + 1) * 16)
            slot = (fb % 2) * 4096
            for fs in range(4):
                for tg in range(ntg):
                    k = (fb * 4 + fs) * ntg + tg
                    bank = c.pp[k % 4]
                    if k >= 4:
                        t.wait_ge(c.s_sc.h, sc0 + k - 4 + 1)
                    for cc in range(NCC):
                        mm = t.matmul(
                            bank[:],
                            c.wU[:, slot + cc * 512 + fs * 128:
                                 slot + cc * 512 + fs * 128 + 128],
                            c.xnT[:, cc * T + t0 + tg * 512:
                                  cc * T + t0 + tg * 512 + 512],
                            start=(cc == 0), stop=(cc == NCC - 1))
                        if cc == NCC - 1:
                            c.s_mm.inc(mm)

    def sup(s):
        if load_b:
            s.wait_ge(c.s_ld.h, ld0 + 16)
        for k in range(NF * ntg):
            fc = k // ntg
            tg = k % ntg
            s.wait_ge(c.s_mm.h, mm0 + k + 1)
            c.s_sc.inc(s.activation(
                c.scrA[:, A_HT + fc * tw + tg * 512:
                       A_HT + fc * tw + tg * 512 + 512],
                c.pp[k % 4][:], AF.Relu, bias=c.bfeat[:, fc:fc + 1]))
    phase(nc, gpsimd=g, tensor=tens, scalar=sup)
    return ld0, nb + NFB + NF // 4 + (1 if dram_b2 is not None else 0)


def emit_moe_down(c, e, FH, ld0, n_dma, tw=T, t0=0, dw_col0=A_DW, fg_tiles=None,
                  add_bias=True):
    nc = c.nc
    mm0 = c.s_mm.n
    ev0 = c.s_ev.n
    ntt_g = tw // 128
    NF = FH // 128 if fg_tiles is None else fg_tiles
    f_off = 0

    def tens(t):
        t.wait_ge(c.s_ld.h, ld0 + n_dma * 16)
        for tt in range(ntt_g):
            pr = (c.pp[0], c.pp[1]) if tt % 2 == 0 else (c.pp[2], c.pp[3])
            if tt >= 2:
                t.wait_ge(c.s_ev.h, ev0 + (tt - 2) * 2 + 2)
            for fc in range(NF):
                lhs = c.scrA[:, A_HT + (f_off + fc) * tw + tt * 128:
                             A_HT + (f_off + fc) * tw + tt * 128 + 128]
                for ch in range(2):
                    mm = t.matmul(pr[ch][:], lhs,
                                  c.scrA[:, dw_col0 + fc * 1024 + ch * 512:
                                         dw_col0 + fc * 1024 + ch * 512 + 512],
                                  start=(fc == 0),
                                  stop=(fc == NF - 1 and not add_bias))
                    if fc == NF - 1 and not add_bias:
                        c.s_mm.inc(mm)
            if add_bias:
                for ch in range(2):
                    c.s_mm.inc(t.matmul(
                        pr[ch][:], c.ones1[:],
                        c.rowb[0:1, 1024 + ch * 512:1536 + ch * 512],
                        start=False, stop=True))

    def vec(v):
        for tt in range(ntt_g):
            pr = (c.pp[0], c.pp[1]) if tt % 2 == 0 else (c.pp[2], c.pp[3])
            v.wait_ge(c.s_mm.h, mm0 + (tt + 1) * 2)
            gcol = c.gates[:, 128 + (t0 // 128 + tt) * 8 + e:
                           129 + (t0 // 128 + tt) * 8 + e]
            col0 = (t0 // 128 + tt) * C
            for ch in range(2):
                c.s_ev.inc(v.scalar_tensor_tensor(
                    c.stream[:, col0 + ch * 512:col0 + ch * 512 + 512],
                    pr[ch][:], gcol,
                    c.stream[:, col0 + ch * 512:col0 + ch * 512 + 512],
                    ALU.mult, ALU.add))
    phase(nc, tensor=tens, vector=vec)


def emit_l1_ffn(c, w):
    import os
    emit_ln(c)
    emit_router(c, w["L1_gate_w"], w["L1_gate_b"], renorm=False)
    for e in range(int(os.environ.get("K_EXPERTS", "8"))):
        for tg in range(2):
            ld0, nd = emit_moe_up_relu(c, e, w["L1_w1"], w["L1_b1"],
                                       w["L1_w2"], w["L1_b2"], 2048,
                                       t0=tg * 1024, tw=1024, load_b=(tg == 0))
            emit_moe_down(c, e, 2048, ld0, nd, tw=1024, t0=tg * 1024)




def emit_attn_qkv_wave(c, wave, Wq_ap, Wk_ap, Wv_ap, bq_col, bk_col, bv_row_ap,
                      fq0, fk0, fv0, qk_scale=0.125):
    nc = c.nc
    ld0 = c.s_ld.n
    mm0 = c.s_mm.n
    sc0 = c.s_sc.n
    ev0 = c.s_ev.n

    def g(g):
        c.s_ld.inc(g.dma_start(
            out=c.wU[:, 0:4096].rearrange("p (cc f) -> p cc f", f=512),
            in_=Wq_ap.rearrange("(cc p) f -> p cc f", p=128)[:, :, fq0:fq0 + 512]), 16)
        c.s_ld.inc(g.dma_start(
            out=c.wU[:, 4096:8192].rearrange("p (cc f) -> p cc f", f=512),
            in_=Wk_ap.rearrange("(cc p) f -> p cc f", p=128)[:, :, fk0:fk0 + 512]), 16)
        c.s_ld.inc(g.dma_start(
            out=c.wD[:, 0:4096].rearrange("p (cc f) -> p cc f", f=512),
            in_=Wv_ap.rearrange("(cc p) f -> p cc f", p=128)[:, :, fv0:fv0 + 512]), 16)
        c.s_ld.inc(g.dma_start(out=c.rowb[0:1, 0:512], in_=bv_row_ap), 16)

    def tens(t):
        t.wait_ge(c.s_ld.h, ld0 + 64)
        k = 0
        for which in range(2):
            for ft in range(4):
                for tg in range(4):
                    bank = c.pp[k % 4]
                    if k >= 4:
                        t.wait_ge(c.s_sc.h, sc0 + k - 4 + 1)
                    for cc in range(NCC):
                        mm = t.matmul(
                            bank[:],
                            c.wU[:, which * 4096 + cc * 512 + ft * 128:
                                 which * 4096 + cc * 512 + ft * 128 + 128],
                            c.xnT[:, cc * T + tg * 512:cc * T + tg * 512 + 512],
                            start=(cc == 0), stop=(cc == NCC - 1))
                        if cc == NCC - 1:
                            c.s_mm.inc(mm)
                    k += 1
        for tt in range(NTT):
            bank = c.pp[4 + tt % 2]
            if tt >= 2:
                t.wait_ge(c.s_ev.h, ev0 + (tt - 2) + 1)
            for cc in range(NCC):
                t.matmul(bank[:],
                         c.xnT[:, cc * T + tt * 128:cc * T + tt * 128 + 128],
                         c.wD[:, cc * 512:(cc + 1) * 512],
                         start=(cc == 0), stop=False)
            c.s_mm.inc(t.matmul(bank[:], c.ones1[:], c.rowb[0:1, 0:512],
                                start=False, stop=True))

    def sc(s):
        s.wait_ge(c.s_ld.h, ld0 + 64)
        k = 0
        for which in range(2):
            base = A_QT if which == 0 else A_KT
            bcol = bq_col if which == 0 else bk_col
            for ft in range(4):
                for tg in range(4):
                    s.wait_ge(c.s_mm.h, mm0 + k + 1)
                    kw = {}
                    if bcol is not None:
                        kw["bias"] = bcol(ft)
                    c.s_sc.inc(s.activation(
                        c.scrA[:, base + ft * T + tg * 512:
                               base + ft * T + tg * 512 + 512],
                        c.pp[k % 4][:], AF.Identity,
                        scale=(qk_scale if which == 0 else 1.0), **kw))
                    k += 1

    def vec(v):
        for tt in range(NTT):
            v.wait_ge(c.s_mm.h, mm0 + 32 + tt + 1)
            c.s_ev.inc(v.tensor_copy(
                c.scrA[:, A_VT + tt * 512:A_VT + (tt + 1) * 512],
                c.pp[4 + tt % 2][:]))
    phase(nc, gpsimd=g, tensor=tens, scalar=sc, vector=vec)


def emit_attn_core_wave(c, causal, nheads=8, l2_ext=False, vstep=512):
    nc = c.nc
    strips = [(hl, qt, (qt + 1) * 128 if causal else T)
              for hl in range(nheads) for qt in range(NTT)]
    N = len(strips)
    nkg = [(kl + 511) // 512 for (_, _, kl) in strips]
    nkt = [(kl + 127) // 128 for (_, _, kl) in strips]
    mm0 = c.s_mm.n
    mx0 = c.s_mx.n
    sc0 = c.s_sc.n
    pe0 = c.s_pe.n
    pt0 = c.s_pt.n
    pl0 = c.s_pl.n
    tp0 = c.s_tpos.n
    ev0 = c.s_ev.n
    s_done = {}
    pv_done = {}
    trans_done = {}
    mm = mm0
    tpn = tp0
    for i in range(-1, N + 2):
        if 0 <= i - 2 < N:
            mm += nkt[i - 2]
            pv_done[i - 2] = mm
        if 0 <= i + 1 < N:
            for kg in range(nkg[i + 1]):
                mm += 1
                s_done[(i + 1, kg)] = mm
        if 0 <= i - 1 < N:
            for kt in range(nkt[i - 1]):
                tpn += 1
                trans_done[(i - 1, kt)] = tpn
    maxred_done = {}
    scale_done = {}
    ptev_all = {}
    ot_done = {}
    mx = mx0
    pe = pe0
    pt = pt0
    ev = ev0
    gidx = 0
    for i in range(N + 2):
        if i < N:
            mx += nkg[i] + 1
            maxred_done[i] = mx
        if 1 <= i <= N:
            pe += 1
            scale_done[i - 1] = pe
        if 2 <= i < N + 2:
            j = i - 2
            pt += (nkt[j] + 7) // 8
            ptev_all[j] = pt
            if strips[j][1] % 4 == 3:
                ev += 1
                ot_done[gidx] = ev
                gidx += 1
    exp_done = {}
    sc = sc0
    for i in range(N):
        for kg in range(nkg[i]):
            sc += 1
            exp_done[(i, kg)] = sc
    exp_all = {i: exp_done[(i, nkg[i] - 1)] for i in range(N)}
    bank_user = {}
    s_bank_wait = {}
    for i in range(N):
        for kg in range(nkg[i]):
            if kg in bank_user:
                s_bank_wait[(i, kg)] = exp_done[(bank_user[kg], kg)]
            bank_user[kg] = i
    mask_done = {i: pl0 + i + 1 for i in range(N)}

    def qk_ap(hl, col, is_q, width):
        base = A_QT if is_q else A_KT
        if not l2_ext:
            tile = hl // 2
            row0 = (hl % 2) * 64
            return c.scrA[row0:row0 + 64, base + tile * T + col:
                          base + tile * T + col + width]
        return c.scrA[0:66, base + hl * T + col:base + hl * T + col + width]

    def tens(t):
        for i in range(-1, N + 2):
            if 0 <= i - 2 < N:
                j = i - 2
                hl, qt, kl = strips[j]
                t.wait_ge(c.s_pt.h, ptev_all[j])
                gj = j // 4
                if gj >= 2:
                    t.wait_ge(c.s_ev.h, ot_done[gj - 2])
                obank = c.pp[4 + gj % 2]
                tbuf = B_T0 if j % 2 == 0 else B_T1
                for kt in range(nkt[j]):
                    c.s_mm.inc(t.matmul(
                        obank[0:64, (qt % 4) * 128:(qt % 4) * 128 + 128],
                        c.scrA[:, A_VT + kt * vstep + hl * 64:
                               A_VT + kt * vstep + hl * 64 + 64],
                        c.scrB[:, tbuf + kt * 128:tbuf + (kt + 1) * 128],
                        start=(kt == 0), stop=(kt == nkt[j] - 1)))
            if 0 <= i + 1 < N:
                hl, qt, kl = strips[i + 1]
                for kg in range(nkg[i + 1]):
                    wdt = min(512, kl - kg * 512)
                    if (i + 1, kg) in s_bank_wait:
                        t.wait_ge(c.s_sc.h, s_bank_wait[(i + 1, kg)])
                    c.s_mm.inc(t.matmul(
                        c.pp[kg][:, 0:wdt],
                        qk_ap(hl, qt * 128, True, 128),
                        qk_ap(hl, kg * 512, False, wdt),
                        start=True, stop=True))
            if 0 <= i - 1 < N:
                j = i - 1
                t.wait_ge(c.s_pe.h, scale_done[j])
                if j >= 1:
                    t.wait_ge(c.s_pt.h, ptev_all[j - 1])
                pbuf = B_P0 if j % 2 == 0 else B_P1
                for kt in range(nkt[j]):
                    bank = c.tp[kt // 8]
                    c.s_tpos.inc(t.transpose(
                        bank[:, (kt % 8) * 128:(kt % 8 + 1) * 128],
                        c.scrB[:, pbuf + kt * 128:pbuf + (kt + 1) * 128],
                        c.ident[:]))

    def vec(v):
        for i in range(N + 2):
            if i < N:
                hl, qt, kl = strips[i]
                if i >= 2:
                    v.wait_ge(c.s_sc.h, exp_all[i - 2])
                for kg in range(nkg[i]):
                    wdt = min(512, kl - kg * 512)
                    v.wait_ge(c.s_mm.h, s_done[(i, kg)])
                    c.s_mx.inc(v.reduce_max(
                        c.stats[:, (i % 2) * 4 + kg:(i % 2) * 4 + kg + 1],
                        c.pp[kg][:, 0:wdt], axis=AX.X))
                v.wait_ge(c.s_mx.h, c.s_mx.n)
                c.s_mx.inc(v.reduce_max(
                    c.stats[:, 8 + i % 2:9 + i % 2],
                    c.stats[:, (i % 2) * 4:(i % 2) * 4 + nkg[i]],
                    axis=AX.X, negate=True))
            if 1 <= i <= N:
                j = i - 1
                hl, qt, kl = strips[j]
                pbuf = B_P0 if j % 2 == 0 else B_P1
                v.wait_ge(c.s_sc.h, exp_all[j])
                if causal:
                    v.wait_ge(c.s_pl.h, mask_done[j])
                    dkg = nkg[j] - 1
                    c.vser(v, v.reduce_sum(
                        c.stats[:, 10 + (j % 2) * 4 + dkg:11 + (j % 2) * 4 + dkg],
                        c.scrB[:, pbuf + dkg * 512:pbuf + kl], axis=AX.X))
                c.vser(v, v.reduce_sum(
                    c.stats[:, 18 + j % 2:19 + j % 2],
                    c.stats[:, 10 + (j % 2) * 4:10 + (j % 2) * 4 + nkg[j]],
                    axis=AX.X))
                c.vser(v, v.reciprocal(c.stats[:, 20 + j % 2:21 + j % 2],
                                       c.stats[:, 18 + j % 2:19 + j % 2]))
                c.s_pe.inc(v.tensor_scalar(
                    c.scrB[:, pbuf:pbuf + kl], c.scrB[:, pbuf:pbuf + kl],
                    c.stats[:, 20 + j % 2:21 + j % 2], None, ALU.mult))
            if 2 <= i < N + 2:
                j = i - 2
                hl, qt, kl = strips[j]
                tbuf = B_T0 if j % 2 == 0 else B_T1
                if j >= 2:
                    v.wait_ge(c.s_mm.h, pv_done[j - 2])
                for gq in range((nkt[j] + 7) // 8):
                    hi = min(nkt[j], (gq + 1) * 8)
                    v.wait_ge(c.s_tpos.h, trans_done[(j, hi - 1)])
                    wdt = (hi - gq * 8) * 128
                    bank = c.tp[gq]
                    c.s_pt.inc(v.tensor_copy(
                        c.scrB[:, tbuf + gq * 1024:tbuf + gq * 1024 + wdt],
                        bank[:, 0:wdt]))
                if qt % 4 == 3:
                    v.wait_ge(c.s_mm.h, pv_done[j])
                    gj = j // 4
                    obank = c.pp[4 + gj % 2]
                    ft = hl // 2
                    r0 = (hl % 2) * 64
                    c.s_ev.inc(v.tensor_copy(
                        c.scrA[r0:r0 + 64,
                               A_OT + ft * T + (qt - 3) * 128:
                               A_OT + ft * T + (qt + 1) * 128],
                        obank[0:64, 0:512]))

    def scal(s):
        for i in range(N):
            hl, qt, kl = strips[i]
            pbuf = B_P0 if i % 2 == 0 else B_P1
            s.wait_ge(c.s_mx.h, maxred_done[i])
            if i >= 2:
                s.wait_ge(c.s_tpos.h, trans_done[(i - 2, nkt[i - 2] - 1)])
            for kg in range(nkg[i]):
                wdt = min(512, kl - kg * 512)
                c.s_sc.inc(s.activation(
                    c.scrB[:, pbuf + kg * 512:pbuf + kg * 512 + wdt],
                    c.pp[kg][:, 0:wdt], AF.Exp,
                    bias=c.stats[:, 8 + i % 2:9 + i % 2],
                    accum_out=c.stats[:, 10 + (i % 2) * 4 + kg:
                                      11 + (i % 2) * 4 + kg]))

    def pool(g):
        if not causal:
            return
        for i in range(N):
            hl, qt, kl = strips[i]
            pbuf = B_P0 if i % 2 == 0 else B_P1
            g.wait_ge(c.s_sc.h, exp_all[i])
            c.s_pl.inc(g.affine_select(
                c.scrB[:, pbuf + qt * 128:pbuf + kl],
                c.scrB[:, pbuf + qt * 128:pbuf + kl],
                [[-1, 128]], ALU.is_ge, 0.0, base=0, channel_multiplier=1))
    if causal:
        phase(nc, tensor=tens, vector=vec, scalar=scal, gpsimd=pool)
    else:
        phase(nc, tensor=tens, vector=vec, scalar=scal)


def emit_attn_out_wave(c, wave, Wp_ap, brow_ap, nft=4):
    nc = c.nc
    ld0 = c.s_ld.n
    mm0 = c.s_mm.n
    ev0 = c.s_ev.n
    use_bias = (wave == 0 and brow_ap is not None)

    def g(g):
        c.s_ld.inc(g.dma_start(
            out=c.wD[:, 0:nft * 1024].rearrange("p (fc cw) -> p fc cw", cw=1024),
            in_=Wp_ap.rearrange("(fc p) cw -> p fc cw", p=128)[
                :, wave * nft:wave * nft + nft, :]), 16)
        if use_bias:
            c.s_ld.inc(g.dma_start(out=c.rowb[0:1, 0:1024], in_=brow_ap), 16)

    def tens(t):
        t.wait_ge(c.s_ld.h, ld0 + (32 if use_bias else 16))
        for tt in range(NTT):
            pr = (c.pp[0], c.pp[1]) if tt % 2 == 0 else (c.pp[2], c.pp[3])
            if tt >= 2:
                t.wait_ge(c.s_ev.h, ev0 + (tt - 2) * 2 + 2)
            for fc in range(nft):
                lhs = c.scrA[:, A_OT + fc * T + tt * 128:
                             A_OT + fc * T + tt * 128 + 128]
                for ch in range(2):
                    mm = t.matmul(pr[ch][:], lhs,
                                  c.wD[:, fc * 1024 + ch * 512:
                                       fc * 1024 + ch * 512 + 512],
                                  start=(fc == 0),
                                  stop=(fc == nft - 1 and not use_bias))
                    if fc == nft - 1 and not use_bias:
                        c.s_mm.inc(mm)
            if use_bias:
                for ch in range(2):
                    c.s_mm.inc(t.matmul(pr[ch][:], c.ones1[:],
                                        c.rowb[0:1, ch * 512:ch * 512 + 512],
                                        start=False, stop=True))

    def vec(v):
        for tt in range(NTT):
            pr = (c.pp[0], c.pp[1]) if tt % 2 == 0 else (c.pp[2], c.pp[3])
            v.wait_ge(c.s_mm.h, mm0 + (tt + 1) * 2)
            for ch in range(2):
                c.s_ev.inc(v.tensor_tensor(
                    c.stream[:, tt * C + ch * 512:tt * C + ch * 512 + 512],
                    c.stream[:, tt * C + ch * 512:tt * C + ch * 512 + 512],
                    pr[ch][:], ALU.add))
    phase(nc, gpsimd=g, tensor=tens, vector=vec)


def emit_l1_attn(c, w):
    import os
    emit_ln(c)
    nwaves = int(os.environ.get("K_ATTN_WAVES", "2"))
    for wave in range(nwaves):
        def bq(ft, wave=wave):
            return c.bfeat[:, wave * 4 + ft:wave * 4 + ft + 1]

        def bk(ft, wave=wave):
            return c.bfeat[:, 8 + wave * 4 + ft:8 + wave * 4 + ft + 1]
        if wave == 0:
            ld0 = c.s_ld.n

            def gb(g):
                c.s_ld.inc(g.dma_start(out=c.bfeat[:, 0:16],
                                       in_=w["L1_battn_feat"]), 16)
                g.wait_ge(c.s_ld.h, ld0 + 16)

            def gv(v):
                v.wait_ge(c.s_ld.h, ld0 + 16)
                v.tensor_scalar(c.bfeat[:, 0:8], c.bfeat[:, 0:8], 0.125,
                                None, ALU.mult)
            phase(c.nc, gpsimd=gb, vector=gv)
        emit_attn_qkv_wave(
            c, wave, w["L1_W_attn"], w["L1_W_attn"], w["L1_W_attn"],
            bq, bk, w["L1_bv_row"][:, wave * 512:wave * 512 + 512],
            fq0=wave * 512, fk0=1024 + wave * 512, fv0=2048 + wave * 512)
        if os.environ.get("K_ATTN_QKV_ONLY"):
            continue
        emit_attn_core_wave(c, causal=True,
                            nheads=int(os.environ.get("K_ATTN_HEADS", "8")))
        if os.environ.get("K_ATTN_NO_OUT"):
            continue
        emit_attn_out_wave(c, wave, w["L1_W_proj"], w["L1_bproj_row"])


def build_l1(c, w):
    import os
    parts = os.environ.get("K_PARTS", "fa f")
    for p in parts.split():
        for ch in p:
            if ch == "f":
                emit_l1_ffn(c, w)
            elif ch == "a":
                emit_l1_attn(c, w)
            elif ch == "n":
                emit_ln(c)
            elif ch == "r":
                emit_ln(c)
                emit_router(c, w["L1_gate_w"], w["L1_gate_b"], renorm=False)





def emit_moe_up_glu(c, e, dram_gw, dram_uw, dram_dw, FH2, fh, t0, tw):
    nc = c.nc
    ld0 = c.s_ld.n
    mm0 = c.s_mm.n
    sc0 = c.s_sc.n
    ev0 = c.s_ev.n
    pl0 = c.s_pl.n
    f0 = fh * 2048
    NFB = 2048 // 256
    ntg = tw // 512

    def dma_chunk(g, fb):
        for which, dw in ((0, dram_gw), (1, dram_uw)):
            c.s_ld.inc(g.dma_start(
                out=c.wU[:, which * 4096 + (fb % 2) * 2048:
                         which * 4096 + (fb % 2) * 2048 + 2048].rearrange(
                    "p (cc f) -> p cc f", f=256),
                in_=dw[e].rearrange("(cc p) f -> p cc f", p=128)[
                    :, :, f0 + fb * 256:f0 + (fb + 1) * 256]), 16)

    def g(g):
        for fb in range(2):
            dma_chunk(g, fb)
        for j in range(4):
            c.s_ld.inc(g.dma_start(
                out=c.scrA[:, A_DW + j * 4096:A_DW + (j + 1) * 4096].rearrange(
                    "p (fc cw) -> p fc cw", cw=1024),
                in_=dram_dw[e].rearrange("(fc p) cw -> p fc cw", p=128)[
                    :, fh * 16 + j * 4:fh * 16 + (j + 1) * 4, :]), 16)
        for fb in range(2, NFB):
            g.wait_ge(c.s_mm.h, mm0 + (fb - 1) * 2 * ntg)
            dma_chunk(g, fb)

    def tens(t):
        for fb in range(NFB):
            t.wait_ge(c.s_ld.h, ld0 + (fb + 1) * 32 + (64 if fb >= 2 else 0))
            for fs in range(2):
                for tg in range(ntg):
                    k = (fb * 2 + fs) * ntg + tg
                    pg = c.pp[2 * (k % 2)]
                    pu = c.pp[2 * (k % 2) + 1]
                    if k >= 2:
                        t.wait_ge(c.s_pl.h, pl0 + k - 2 + 1)
                        t.wait_ge(c.s_sc.h, sc0 + k - 2 + 1)
                    for which, bank in ((0, pg), (1, pu)):
                        for cc in range(NCC):
                            mm = t.matmul(
                                bank[:],
                                c.wU[:, which * 4096 + (fb % 2) * 2048 +
                                     cc * 256 + fs * 128:
                                     which * 4096 + (fb % 2) * 2048 +
                                     cc * 256 + fs * 128 + 128],
                                c.xnT[:, cc * T + t0 + tg * 512:
                                      cc * T + t0 + tg * 512 + 512],
                                start=(cc == 0), stop=(cc == NCC - 1))
                            if which == 1 and cc == NCC - 1:
                                c.s_mm.inc(mm)

    def scal(s):
        for k in range(NFB * 2 * ntg):
            pg = c.pp[2 * (k % 2)]
            s.wait_ge(c.s_mm.h, mm0 + k + 1)
            if k >= 2:
                s.wait_ge(c.s_pl.h, pl0 + k - 2 + 1)
            c.s_sc.inc(s.activation(
                c.scrB[:, (k % 2) * 512:(k % 2) * 512 + 512],
                pg[:], AF.Silu))

    def vec(v):
        for k in range(NFB * 2 * ntg):
            fc = k // ntg
            tg = k % ntg
            pu = c.pp[2 * (k % 2) + 1]
            v.wait_ge(c.s_mm.h, mm0 + k + 1)
            v.wait_ge(c.s_sc.h, sc0 + k + 1)
            c.s_pl.inc(v.tensor_tensor(
                c.scrA[:, A_HT + fc * tw + tg * 512:
                       A_HT + fc * tw + tg * 512 + 512],
                pu[:],
                c.scrB[:, (k % 2) * 512:(k % 2) * 512 + 512],
                ALU.mult))
    with nc.Block() as blk:
        blk.gpsimd(g)
        blk.tensor(tens)
        blk.scalar(scal)
        blk.vector(vec)
    return ld0, 2 * NFB + 4


def emit_l2_ffn(c, w):
    import os
    emit_ln(c)
    emit_router(c, w["L2_router_w"], w["L2_router_b"], renorm=True)
    for e in range(int(os.environ.get("K_EXPERTS", "8"))):
        for fh in range(2):
            for tg in range(2):
                ld0, nd = emit_moe_up_glu(c, e, w["L2_gate_w"], w["L2_up_w"],
                                          w["L2_down_w"], 4096, fh,
                                          t0=tg * 1024, tw=1024)
                emit_moe_down(c, e, 2048, ld0, nd, tw=1024, t0=tg * 1024,
                              add_bias=False)


def emit_attn_qkv_wave_l2(c, w, wave):
    nc = c.nc
    ld0 = c.s_ld.n
    mm0 = c.s_mm.n
    sc0 = c.s_sc.n
    ev0 = c.s_ev.n
    f0 = wave * 256

    def g(g):
        c.s_ld.inc(g.dma_start(
            out=c.wU[:, 0:2048].rearrange("p (cc f) -> p cc f", f=256),
            in_=w["L2_Wq"].rearrange("(cc p) f -> p cc f", p=128)[
                :, :, f0:f0 + 256]), 16)
        c.s_ld.inc(g.dma_start(
            out=c.wU[:, 2048:4096].rearrange("p (cc f) -> p cc f", f=256),
            in_=w["L2_Wk"].rearrange("(cc p) f -> p cc f", p=128)[
                :, :, f0:f0 + 256]), 16)
        c.s_ld.inc(g.dma_start(
            out=c.wD[:, 0:2048].rearrange("p (cc f) -> p cc f", f=256),
            in_=w["L2_Wv"].rearrange("(cc p) f -> p cc f", p=128)[
                :, :, f0:f0 + 256]), 16)
        for hl in range(4):
            h = wave * 4 + hl
            c.s_ld.inc(g.dma_start(
                out=c.scrA[64:66, A_QT + hl * T:A_QT + (hl + 1) * T],
                in_=w["L2_aq"][h]), 16)
            c.s_ld.inc(g.dma_start(
                out=c.scrA[64:66, A_KT + hl * T:A_KT + (hl + 1) * T],
                in_=w["L2_ak"][h]), 16)

    def tens(t):
        t.wait_ge(c.s_ld.h, ld0 + 176)
        k = 0
        for which in range(2):
            for ft in range(2):
                for tg in range(4):
                    bank = c.pp[k % 4]
                    if k >= 4:
                        t.wait_ge(c.s_sc.h, sc0 + (k - 4 + 1) * 2)
                    for cc in range(NCC):
                        mm = t.matmul(
                            bank[:],
                            c.wU[:, which * 2048 + cc * 256 + ft * 128:
                                 which * 2048 + cc * 256 + ft * 128 + 128],
                            c.xnT[:, cc * T + tg * 512:cc * T + tg * 512 + 512],
                            start=(cc == 0), stop=(cc == NCC - 1))
                        if cc == NCC - 1:
                            c.s_mm.inc(mm)
                    k += 1
        for tt in range(NTT):
            bank = c.pp[4 + tt % 2]
            if tt >= 2:
                t.wait_ge(c.s_ev.h, ev0 + (tt - 2) + 1)
            for cc in range(NCC):
                mm = t.matmul(bank[:, 0:256],
                              c.xnT[:, cc * T + tt * 128:cc * T + tt * 128 + 128],
                              c.wD[:, cc * 256:(cc + 1) * 256],
                              start=(cc == 0), stop=(cc == NCC - 1))
                if cc == NCC - 1:
                    c.s_mm.inc(mm)

    def scal(s):
        k = 0
        for which in range(2):
            base = A_QT if which == 0 else A_KT
            for ft in range(2):
                for tg in range(4):
                    s.wait_ge(c.s_mm.h, mm0 + k + 1)
                    for sub in range(2):
                        hl = ft * 2 + sub
                        c.s_sc.inc(s.activation(
                            c.scrA[0:64, base + hl * T + tg * 512:
                                   base + hl * T + tg * 512 + 512],
                            c.pp[k % 4][sub * 64:sub * 64 + 64, :],
                            AF.Identity,
                            scale=(0.125 if which == 0 else 1.0)))
                    k += 1

    def vec(v):
        for tt in range(NTT):
            v.wait_ge(c.s_mm.h, mm0 + 16 + tt + 1)
            c.s_ev.inc(v.tensor_copy(
                c.scrA[:, A_VT + tt * 256:A_VT + (tt + 1) * 256],
                c.pp[4 + tt % 2][:, 0:256]))
    phase(nc, gpsimd=g, tensor=tens, scalar=scal, vector=vec)


def emit_l2_attn(c, w):
    import os
    emit_ln(c)
    nwaves = int(os.environ.get("K_ATTN_WAVES4", "4"))
    for wave in range(nwaves):
        emit_attn_qkv_wave_l2(c, w, wave)
        emit_attn_core_wave(c, causal=True, nheads=4, l2_ext=True, vstep=256)
        emit_attn_out_wave(c, wave, w["L2_Wo"],
                           w["L2_bo_row"] if wave == 0 else None, nft=2)


def build_l2(c, w):
    import os
    parts = os.environ.get("K_PARTS", "fa f")
    for p in parts.split():
        for ch in p:
            if ch == "f":
                emit_l2_ffn(c, w)
            elif ch == "a":
                emit_l2_attn(c, w)
            elif ch == "n":
                emit_ln(c)




def emit_l3_rope(c, which):
    nc = c.nc
    base = A_QT if which == 0 else A_KT
    COS = c.scrBf[:, 2048:4096]
    SIN = c.scrBf[:, 0:2048]

    def vec_rope(v):
        for ft in range(4):
            X = c.scrA
            x0 = base + ft * T
            v.tensor_tensor(c.scrB[0:64, 4096:6144], X[0:64, x0:x0 + T],
                            COS[0:64, :], ALU.mult)
            v.tensor_tensor(c.scrB[0:64, 6144:8192], X[64:128, x0:x0 + T],
                            SIN[64:128, :], ALU.mult)
            v.tensor_tensor(c.scrB[64:128, 4096:6144], X[0:64, x0:x0 + T],
                            SIN[0:64, :], ALU.mult)
            v.tensor_tensor(c.scrB[64:128, 6144:8192], X[64:128, x0:x0 + T],
                            COS[64:128, :], ALU.mult)
            v.tensor_tensor(X[0:64, x0:x0 + T], c.scrB[0:64, 4096:6144],
                            c.scrB[0:64, 6144:8192], ALU.subtract)
            v.tensor_tensor(X[64:128, x0:x0 + T], c.scrB[64:128, 4096:6144],
                            c.scrB[64:128, 6144:8192], ALU.add)
    phase(nc, vector=vec_rope)


def emit_l3_rms(c, w, which, wave):
    nc = c.nc
    base = A_QT if which == 0 else A_KT
    for half in range(2):
        fts = (2 * half, 2 * half + 1)

        def pA(s, fts=fts):
            for j, ft in enumerate(fts):
                s.activation(c.scrB[:, 4096 + j * 2048:4096 + j * 2048 + 2048],
                             c.scrA[:, base + ft * T:base + (ft + 1) * T],
                             AF.Square)
        phase(nc, scalar=pA)

        mm0 = c.s_mm.n
        ev0 = c.s_ev.n

        def pB_t(t, fts=fts, mm0=mm0, ev0=ev0):
            k = 0
            for j, ft in enumerate(fts):
                for tg in range(4):
                    if k >= 2:
                        t.wait_ge(c.s_ev.h, ev0 + k - 2 + 1)
                    c.s_mm.inc(t.matmul(
                        c.pp[4 + k % 2][0:2, 0:512], c.csel[:, 4:6],
                        c.scrB[:, 4096 + j * 2048 + tg * 512:
                               4096 + j * 2048 + tg * 512 + 512],
                        start=True, stop=True))
                    k += 1

        def pB_v(v, fts=fts, mm0=mm0):
            k = 0
            for j, ft in enumerate(fts):
                for tg in range(4):
                    v.wait_ge(c.s_mm.h, mm0 + k + 1)
                    c.s_ev.inc(v.tensor_copy(
                        c.scrBf[0:2, j * 2048 + tg * 512:
                                j * 2048 + tg * 512 + 512],
                        c.pp[4 + k % 2][0:2, 0:512]))
                    k += 1
        phase(nc, tensor=pB_t, vector=pB_v)

        def pC(v):
            v.tensor_scalar(c.scrBf[0:2, 0:4096], c.scrBf[0:2, 0:4096],
                            1.0 / 64, 1e-6, ALU.mult, ALU.add)
        phase(nc, vector=pC)

        def pD(s):
            s.activation(c.scrBf[0:2, 0:4096], c.scrBf[0:2, 0:4096], AF.Sqrt)
        phase(nc, scalar=pD)

        def pE(v):
            with nc.allow_low_precision(reason="rstd feeds bf16 matmul"):
                v.reciprocal(c.rowb[0:2, 0:2048], c.scrBf[0:2, 0:2048])
                v.reciprocal(c.scrB[0:2, 4096:6144], c.scrBf[0:2, 2048:4096])
        phase(nc, vector=pE)

        mm1 = c.s_mm.n
        ev1 = c.s_ev.n

        def pF_t(t, fts=fts, mm1=mm1, ev1=ev1):
            k = 0
            for j, ft in enumerate(fts):
                rstd = c.rowb[0:2, 0:2048] if j == 0 else c.scrB[0:2, 4096:6144]
                for tg in range(4):
                    if k >= 2:
                        t.wait_ge(c.s_ev.h, ev1 + k - 2 + 1)
                    c.s_mm.inc(t.matmul(
                        c.pp[4 + k % 2][:, 0:512], c.rsel[:],
                        rstd[:, tg * 512:tg * 512 + 512],
                        start=True, stop=True))
                    k += 1

        def pF_v(v, fts=fts, mm1=mm1):
            k = 0
            for j, ft in enumerate(fts):
                x0 = base + ft * T
                for tg in range(4):
                    v.wait_ge(c.s_mm.h, mm1 + k + 1)
                    c.s_ev.inc(v.tensor_tensor(
                        c.scrA[:, x0 + tg * 512:x0 + tg * 512 + 512],
                        c.scrA[:, x0 + tg * 512:x0 + tg * 512 + 512],
                        c.pp[4 + k % 2][:, 0:512], ALU.mult))
                    k += 1
                gc = c.bfeat[:, 16 + which * 8 + wave * 4 + ft:
                             17 + which * 8 + wave * 4 + ft]
                v.tensor_scalar(c.scrA[:, x0:x0 + T], c.scrA[:, x0:x0 + T],
                                gc, None, ALU.mult)
        phase(nc, tensor=pF_t, vector=pF_v)


def emit_l3_attn(c, w):
    import os
    emit_ln(c)
    ld0 = c.s_ld.n

    def gb(g):
        c.s_ld.inc(g.dma_start(out=c.bfeat[:, 16:24], in_=w["L3_qg_col"]), 16)
        c.s_ld.inc(g.dma_start(out=c.bfeat[:, 24:32], in_=w["L3_kg_col"]), 16)
        c.s_ld.inc(g.dma_start(out=c.bfeat[:, 32:48], in_=w["L3_b_qkv_feat"]), 16)
        g.wait_ge(c.s_ld.h, ld0 + 48)

    def gv(v):
        v.wait_ge(c.s_ld.h, ld0 + 48)
        v.tensor_scalar(c.bfeat[:, 16:24], c.bfeat[:, 16:24], 0.125,
                        None, ALU.mult)
    phase(c.nc, gpsimd=gb, vector=gv)
    nwaves = int(os.environ.get("K_ATTN_WAVES", "2"))
    for wave in range(nwaves):
        def bq(ft, wave=wave):
            return c.bfeat[:, 32 + wave * 4 + ft:33 + wave * 4 + ft]

        def bk(ft, wave=wave):
            return c.bfeat[:, 40 + wave * 4 + ft:41 + wave * 4 + ft]
        emit_attn_qkv_wave(
            c, wave, w["L3_w_qkv"], w["L3_w_qkv"], w["L3_w_qkv"],
            bq, bk, w["L3_bv_row"][:, wave * 512:wave * 512 + 512],
            fq0=wave * 512, fk0=1024 + wave * 512, fv0=2048 + wave * 512,
            qk_scale=1.0)
        ld1 = c.s_ld.n

        def gcs(g, ld1=ld1):
            c.s_ld.inc(g.dma_start(out=c.scrBf[:, 2048:4096], in_=w["L3_cos"]), 16)
            c.s_ld.inc(g.dma_start(out=c.scrBf[:, 0:2048], in_=w["L3_sin"]), 16)
            g.wait_ge(c.s_ld.h, ld1 + 32)
        phase(c.nc, gpsimd=gcs)
        emit_l3_rope(c, 0)
        emit_l3_rope(c, 1)
        emit_l3_rms(c, w, 0, wave)
        emit_l3_rms(c, w, 1, wave)
        if os.environ.get("K_ATTN_QKV_ONLY"):
            continue
        emit_attn_core_wave(c, causal=False,
                            nheads=int(os.environ.get("K_ATTN_HEADS", "8")))
        if os.environ.get("K_ATTN_NO_OUT"):
            continue
        emit_attn_out_wave(c, wave, w["L3_w_out"],
                           w["L3_bout_row"] if wave == 0 else None)


def emit_l3_prenorm(c, w):
    nc = c.nc
    ld0 = c.s_ld.n
    mm0 = c.s_mm.n

    def g(g):
        c.s_ld.inc(g.dma_start(out=c.rowb[0:1, 0:1024],
                               in_=w["L3_norm_scale_row"]), 16)

    def p1s(s):
        for tt in range(NTT):
            s.activation(c.scrBf[:, (tt % 4) * 1024:(tt % 4) * 1024 + 1024],
                         c.scrA[:, A_XN + tt * C:A_XN + (tt + 1) * C],
                         AF.Square, accum_out=c.stats[:, 16 + tt:17 + tt])

    def p1v(v):
        for tt in range(NTT):
            v.scalar_tensor_tensor(
                c.stream[:, tt * C:(tt + 1) * C],
                c.scrA[:, A_XN + tt * C:A_XN + (tt + 1) * C], 0.5,
                c.stream[:, tt * C:(tt + 1) * C], ALU.mult, ALU.add)

    def p1t(t):
        t.wait_ge(c.s_ld.h, ld0 + 16)
        for ch in range(2):
            c.s_mm.inc(t.matmul(c.pp[4 + ch][:], c.ones1[:],
                                c.rowb[0:1, ch * 512:ch * 512 + 512],
                                start=True, stop=True))
    phase(nc, gpsimd=g, scalar=p1s, vector=p1v, tensor=p1t)

    def p2v(v):
        c.vser(v, v.tensor_scalar(c.stats[:, 16:32], c.stats[:, 16:32],
                                  1.0 / C, 1e-5, ALU.mult, ALU.add))
    phase(nc, vector=p2v)

    def p3s(s):
        s.activation(c.stats[:, 16:32], c.stats[:, 16:32], AF.Sqrt)
    phase(nc, scalar=p3s)

    def p4v(v):
        c.vser(v, v.reciprocal(c.stats[:, 48:64], c.stats[:, 16:32]))
        c.vser(v, v.tensor_copy(c.scrB[:, 0:512], c.pp[4][:]))
        c.vser(v, v.tensor_copy(c.scrB[:, 512:1024], c.pp[5][:]))
        for tt in range(NTT):
            v.tensor_scalar(
                c.scrA[:, A_XN + tt * C:A_XN + (tt + 1) * C],
                c.scrA[:, A_XN + tt * C:A_XN + (tt + 1) * C],
                c.stats[:, 48 + tt:49 + tt], None, ALU.mult)
        for tt in range(NTT):
            v.tensor_tensor(
                c.scrA[:, A_XN + tt * C:A_XN + (tt + 1) * C],
                c.scrA[:, A_XN + tt * C:A_XN + (tt + 1) * C],
                c.scrB[:, 0:1024], ALU.mult)
    phase(nc, vector=p4v)
    emit_transpose_to(c, c.scrA, A_XN, c.xnT, 0)


def emit_moe_up_swiglu(c, e, dram_w1, dram_b1, dram_w2, dram_b2, t0, tw,
                       load_b=True):
    nc = c.nc
    ld0 = c.s_ld.n
    mm0 = c.s_mm.n
    sc0 = c.s_sc.n
    ev0 = c.s_ev.n
    pl0 = c.s_pl.n
    NFB = 2048 // 256
    ntg = tw // 512
    nb = 1 if load_b else 0

    def dma_chunk(g, fb):
        for which in range(2):
            c.s_ld.inc(g.dma_start(
                out=c.wU[:, which * 4096 + (fb % 2) * 2048:
                         which * 4096 + (fb % 2) * 2048 + 2048].rearrange(
                    "p (cc f) -> p cc f", f=256),
                in_=dram_w1[e].rearrange("(cc p) f -> p cc f", p=128)[
                    :, :, which * 2048 + fb * 256:
                    which * 2048 + (fb + 1) * 256]), 16)

    def g(g):
        if load_b:
            c.s_ld.inc(g.dma_start(out=c.bfeat[:, 0:32], in_=dram_b1[e]), 16)
        for fb in range(2):
            dma_chunk(g, fb)
        for j in range(4):
            c.s_ld.inc(g.dma_start(
                out=c.scrA[:, A_DW + j * 4096:A_DW + (j + 1) * 4096].rearrange(
                    "p (fc cw) -> p fc cw", cw=1024),
                in_=dram_w2[e].rearrange("(fc p) cw -> p fc cw", p=128)[
                    :, j * 4:(j + 1) * 4, :]), 16)
        c.s_ld.inc(g.dma_start(out=c.rowb[0:1, 1024:2048],
                               in_=dram_b2[e:e + 1, :]), 16)
        for fb in range(2, NFB):
            g.wait_ge(c.s_mm.h, mm0 + (fb - 1) * 2 * ntg)
            dma_chunk(g, fb)

    def tens(t):
        for fb in range(NFB):
            t.wait_ge(c.s_ld.h,
                      ld0 + (nb + 2 * (fb + 1) + (5 if fb >= 2 else 0)) * 16)
            for fs in range(2):
                for tg in range(ntg):
                    k = (fb * 2 + fs) * ntg + tg
                    pg = c.pp[2 * (k % 2)]
                    pl_ = c.pp[2 * (k % 2) + 1]
                    if k >= 2:
                        t.wait_ge(c.s_pl.h, pl0 + k - 2 + 1)
                    for which, bank in ((0, pg), (1, pl_)):
                        for cc in range(NCC):
                            mm = t.matmul(
                                bank[:],
                                c.wU[:, which * 4096 + (fb % 2) * 2048 +
                                     cc * 256 + fs * 128:
                                     which * 4096 + (fb % 2) * 2048 +
                                     cc * 256 + fs * 128 + 128],
                                c.xnT[:, cc * T + t0 + tg * 512:
                                      cc * T + t0 + tg * 512 + 512],
                                start=(cc == 0), stop=(cc == NCC - 1))
                            if which == 1 and cc == NCC - 1:
                                c.s_mm.inc(mm)

    def vec(v):
        for k in range(NFB * 2 * ntg):
            fc = k // ntg
            tg = k % ntg
            pg = c.pp[2 * (k % 2)]
            pl_ = c.pp[2 * (k % 2) + 1]
            v.wait_ge(c.s_mm.h, mm0 + k + 1)
            if k >= 2:
                v.wait_ge(c.s_sc.h, sc0 + k - 2 + 1)
            v.tensor_scalar(c.scrB[:, (k % 2) * 512:(k % 2) * 512 + 512],
                            pg[:], c.bfeat[:, fc:fc + 1], 7.0,
                            ALU.add, ALU.min)
            c.s_ev.inc(v.tensor_scalar(
                c.scrB[:, 1024 + (k % 2) * 512:1024 + (k % 2) * 512 + 512],
                pl_[:], c.bfeat[:, 16 + fc:17 + fc], 7.0,
                ALU.add, ALU.min))
            v.tensor_scalar(
                c.scrB[:, 1024 + (k % 2) * 512:1024 + (k % 2) * 512 + 512],
                c.scrB[:, 1024 + (k % 2) * 512:1024 + (k % 2) * 512 + 512],
                -7.0, 1.0, ALU.max, ALU.add)
            v.wait_ge(c.s_sc.h, sc0 + k + 1)
            c.s_pl.inc(v.tensor_tensor(
                c.scrA[:, A_HT + fc * tw + tg * 512:
                       A_HT + fc * tw + tg * 512 + 512],
                c.scrB[:, 2048 + (k % 2) * 512:2048 + (k % 2) * 512 + 512],
                c.scrB[:, 1024 + (k % 2) * 512:1024 + (k % 2) * 512 + 512],
                ALU.mult))

    def scal(s):
        s.wait_ge(c.s_ld.h, ld0 + 16)
        for k in range(NFB * 2 * ntg):
            s.wait_ge(c.s_ev.h, ev0 + k + 1)
            if k >= 2:
                s.wait_ge(c.s_pl.h, pl0 + k - 2 + 1)
            c.s_sc.inc(s.activation(
                c.scrB[:, 2048 + (k % 2) * 512:2048 + (k % 2) * 512 + 512],
                c.scrB[:, (k % 2) * 512:(k % 2) * 512 + 512],
                AF.Gelu_apprx_sigmoid))

    with nc.Block() as blk:
        blk.gpsimd(g)
        blk.tensor(tens)
        blk.vector(vec)
        blk.scalar(scal)
    return ld0, nb + 2 * NFB + 4 + 1


def emit_l3_ffn(c, w):
    import os
    emit_ln(c)
    emit_l3_prenorm(c, w)
    emit_router(c, w["L3_gate_w"], w["L3_gate_b"], renorm=True)
    for e in range(int(os.environ.get("K_EXPERTS", "8"))):
        for tg in range(2):
            ld0, nd = emit_moe_up_swiglu(c, e, w["L3_mlp1_w"], w["L3_mlp1_b"],
                                         w["L3_mlp2_w"], w["L3_mlp2_b"],
                                         t0=tg * 1024, tw=1024,
                                         load_b=(tg == 0))
            emit_moe_down(c, e, 2048, ld0, nd, tw=1024, t0=tg * 1024)


def build_l3(c, w):
    import os
    parts = os.environ.get("K_PARTS", "fa f")
    for p in parts.split():
        for ch in p:
            if ch == "f":
                emit_l3_ffn(c, w)
            elif ch == "a":
                emit_l3_attn(c, w)
            elif ch == "n":
                emit_ln(c)





def emit_l0_ffn(c, w):
    nc = c.nc
    emit_ln(c)
    ld0 = c.s_ld.n

    def g0(g):
        c.s_ld.inc(g.dma_start(out=c.bfeat[:, 0:8], in_=w["L0_mu_r"]), 16)
        c.s_ld.inc(g.dma_start(out=c.bfeat[:, 8:16], in_=w["L0_mu_k"]), 16)
        g.wait_ge(c.s_ld.h, ld0 + 32)

    def v0(v):
        v.wait_ge(c.s_ld.h, ld0 + 32)
        c.vser(v, v.tensor_scalar(c.bfeat[:, 16:32], c.bfeat[:, 0:16],
                                  -1.0, 1.0, ALU.mult, ALU.add))
    phase(nc, gpsimd=g0, vector=v0)

    XR, XK = 0, 16384

    def v1(v):
        for which, base, mu0, om0 in ((0, XR, 0, 16), (1, XK, 8, 24)):
            for cc in range(NCC):
                v.tensor_scalar(
                    c.scrA[:, base + cc * T:base + (cc + 1) * T],
                    c.xnT[:, cc * T:(cc + 1) * T],
                    c.bfeat[:, mu0 + cc:mu0 + cc + 1], None, ALU.mult)
            for cc in range(NCC):
                v.scalar_tensor_tensor(
                    c.scrA[:, base + cc * T + 1:base + (cc + 1) * T],
                    c.xnT[:, cc * T:(cc + 1) * T - 1],
                    c.bfeat[:, om0 + cc:om0 + cc + 1],
                    c.scrA[:, base + cc * T + 1:base + (cc + 1) * T],
                    ALU.mult, ALU.add)
    phase(nc, vector=v1)

    for which in range(2):
        src_base = XR if which == 0 else XK
        dram_w = w["L0_W_r"] if which == 0 else w["L0_W_k"]
        ld1 = c.s_ld.n
        mm0 = c.s_mm.n
        sc0 = c.s_sc.n
        pl0 = c.s_pl.n

        def gw_(g, dram_w=dram_w, ld1=ld1, mm0=mm0):
            for fb in range(2):
                c.s_ld.inc(g.dma_start(
                    out=c.wU[:, fb * 4096:fb * 4096 + 4096].rearrange(
                        "p (cc f) -> p cc f", f=512),
                    in_=dram_w.rearrange("(cc p) f -> p cc f", p=128)[
                        :, :, fb * 512:(fb + 1) * 512]), 16)

        def tns(t, src_base=src_base, ld1=ld1, mm0=mm0, sc0=sc0):
            for fb in range(2):
                t.wait_ge(c.s_ld.h, ld1 + (fb + 1) * 16)
                for fs in range(4):
                    for tg in range(4):
                        k = (fb * 4 + fs) * 4 + tg
                        bank = c.pp[k % 4]
                        if k >= 4:
                            t.wait_ge(c.s_sc.h, sc0 + k - 4 + 1)
                        for cc in range(NCC):
                            mm = t.matmul(
                                bank[:],
                                c.wU[:, fb * 4096 + cc * 512 + fs * 128:
                                     fb * 4096 + cc * 512 + fs * 128 + 128],
                                c.scrA[:, src_base + cc * T + tg * 512:
                                       src_base + cc * T + tg * 512 + 512],
                                start=(cc == 0), stop=(cc == NCC - 1))
                            if cc == NCC - 1:
                                c.s_mm.inc(mm)

        if which == 0:
            def ev_(s, mm0=mm0):
                for k in range(32):
                    fc = k // 4
                    tg = k % 4
                    s.wait_ge(c.s_mm.h, mm0 + k + 1)
                    c.s_sc.inc(s.activation(
                        c.xnT[:, fc * T + tg * 512:fc * T + tg * 512 + 512],
                        c.pp[k % 4][:], AF.Sigmoid))
            phase(nc, gpsimd=gw_, tensor=tns, scalar=ev_)
        else:
            def ev_(s, mm0=mm0):
                for k in range(32):
                    fc = k // 4
                    tg = k % 4
                    s.wait_ge(c.s_mm.h, mm0 + k + 1)
                    c.s_sc.inc(s.activation(
                        c.scrA[:, XR + fc * T + tg * 512:
                               XR + fc * T + tg * 512 + 512],
                        c.pp[k % 4][:], AF.Relu))

            def sq_(g, sc0=sc0, pl0=pl0):
                for k in range(32):
                    fc = k // 4
                    tg = k % 4
                    g.wait_ge(c.s_sc.h, sc0 + k + 1)
                    sl = c.scrA[:, XR + fc * T + tg * 512:
                                XR + fc * T + tg * 512 + 512]
                    c.s_pl.inc(g.tensor_tensor(sl, sl, sl, ALU.mult))
            phase(nc, gpsimd2=None) if False else \
                phase(nc, gpsimd=lambda gg: (gw_(gg), sq_(gg)), tensor=tns,
                      scalar=ev_)

    ld2 = c.s_ld.n
    mm1 = c.s_mm.n
    ev1 = c.s_ev.n
    pl1 = c.s_pl.n

    def gv_(g):
        for fb in range(2):
            c.s_ld.inc(g.dma_start(
                out=c.wU[:, fb * 4096:fb * 4096 + 4096].rearrange(
                    "p (cc f) -> p cc f", f=512),
                in_=w["L0_W_v"].rearrange("(cc p) f -> p cc f", p=128)[
                    :, :, fb * 512:(fb + 1) * 512]), 16)

    def tv_(t):
        for fb in range(2):
            t.wait_ge(c.s_ld.h, ld2 + (fb + 1) * 16)
            for fs in range(4):
                for tg in range(4):
                    k = (fb * 4 + fs) * 4 + tg
                    bank = c.pp[k % 4]
                    if k >= 4:
                        t.wait_ge(c.s_ev.h, ev1 + k - 4 + 1)
                    for cc in range(NCC):
                        mm = t.matmul(
                            bank[:],
                            c.wU[:, fb * 4096 + cc * 512 + fs * 128:
                                 fb * 4096 + cc * 512 + fs * 128 + 128],
                            c.scrA[:, XR + cc * T + tg * 512:
                                   XR + cc * T + tg * 512 + 512],
                            start=(cc == 0), stop=(cc == NCC - 1))
                        if cc == NCC - 1:
                            c.s_mm.inc(mm)

    def vv_(v):
        for k in range(32):
            fc = k // 4
            tg = k % 4
            v.wait_ge(c.s_mm.h, mm1 + k + 1)
            c.s_ev.inc(v.tensor_tensor(
                c.scrA[:, XK + fc * T + tg * 512:XK + fc * T + tg * 512 + 512],
                c.xnT[:, fc * T + tg * 512:fc * T + tg * 512 + 512],
                c.pp[k % 4][:], ALU.mult))
    phase(nc, gpsimd=gv_, tensor=tv_, vector=vv_)

    tp0 = c.s_tpos.n
    ev2 = c.s_ev.n

    def tt_(t):
        k = 0
        for tt in range(NTT):
            for cc in range(NCC):
                if k >= 16:
                    t.wait_ge(c.s_ev.h, ev2 + (k - 16) // 8 + 1)
                slot = k % 16
                bank = c.tp[slot // 8]
                c.s_tpos.inc(t.transpose(
                    bank[:, (slot % 8) * 128:(slot % 8 + 1) * 128],
                    c.scrA[:, XK + cc * T + tt * 128:XK + cc * T + tt * 128 + 128],
                    c.ident[:]))
                k += 1

    def tv2_(v):
        for tt in range(NTT):
            v.wait_ge(c.s_tpos.h, tp0 + (tt + 1) * 8)
            bank = c.tp[tt % 2]
            c.s_ev.inc(v.scalar_tensor_tensor(
                c.stream[:, tt * C:(tt + 1) * C], bank[:], 0.5,
                c.stream[:, tt * C:(tt + 1) * C], ALU.mult, ALU.add))
    phase(nc, tensor=tt_, vector=tv2_)


def emit_l0_attn(c, w):
    nc = c.nc
    emit_ln(c)
    ld0 = c.s_ld.n
    X2 = 16384

    def g0(g):
        c.s_ld.inc(g.dma_start(out=c.bfeat[:, 0:8], in_=w["L0_norm_w"]), 16)
        c.s_ld.inc(g.dma_start(out=c.bfeat[:, 8:10], in_=w["L0_b_in"]), 16)
        c.s_ld.inc(g.dma_start(out=c.bfeat[0:64, 10:11], in_=w["L0_b_x"]), 16)
        c.s_ld.inc(g.dma_start(out=c.bfeat[0:64, 11:12], in_=w["L0_b_dt"]), 16)
        c.s_ld.inc(g.dma_start(out=c.rowb[0:1, 0:1024], in_=w["L0_b_out"]), 16)
        c.s_ld.inc(g.dma_start(out=c.wD[0:64, 0:64], in_=w["L0_W_x"]), 16)
        c.s_ld.inc(g.dma_start(out=c.wD[64:128, 0:64], in_=w["L0_W_dt"]), 16)
        c.s_ld.inc(g.dma_start(
            out=c.wD[0:64, 1024:2048], in_=w["L0_W_out"]), 16)
        c.s_ld.inc(g.dma_start(
            out=c.wU[:, 0:1536].rearrange("p (cc f) -> p cc f", f=192),
            in_=w["L0_W_in"].rearrange("(cc p) f -> p cc f", p=128)), 16)

    def s0(s):
        for tt in range(NTT):
            s.activation(c.scrBf[:, (tt % 4) * 1024:(tt % 4) * 1024 + 1024],
                         c.scrA[:, A_XN + tt * C:A_XN + (tt + 1) * C],
                         AF.Square, accum_out=c.stats[:, 16 + tt:17 + tt])

    def v0(v):
        for tt in range(NTT):
            v.tensor_tensor(c.stream[:, tt * C:(tt + 1) * C],
                            c.stream[:, tt * C:(tt + 1) * C],
                            c.scrA[:, A_XN + tt * C:A_XN + (tt + 1) * C],
                            ALU.add)
    phase(nc, gpsimd=g0, scalar=s0, vector=v0)

    def v1(v):
        c.vser(v, v.tensor_scalar(c.stats[:, 16:32], c.stats[:, 16:32],
                                  1.0 / C, 1e-8, ALU.mult, ALU.add))
    phase(nc, vector=v1)

    def s1(s):
        s.activation(c.stats[:, 16:32], c.stats[:, 16:32], AF.Sqrt)
    phase(nc, scalar=s1)

    def v2(v):
        c.vser(v, v.reciprocal(c.stats[:, 48:64], c.stats[:, 16:32]))
        for tt in range(NTT):
            v.tensor_scalar(
                c.scrA[:, X2 + tt * C:X2 + (tt + 1) * C],
                c.scrA[:, A_XN + tt * C:A_XN + (tt + 1) * C],
                c.stats[:, 48 + tt:49 + tt], None, ALU.mult)
        v.wait_ge(c.s_ld.h, ld0 + 144)
        for cc in range(NCC):
            v.tensor_scalar(c.wU[:, cc * 192:(cc + 1) * 192],
                            c.wU[:, cc * 192:(cc + 1) * 192],
                            c.bfeat[:, cc:cc + 1], None, ALU.mult)
    phase(nc, vector=v2)
    emit_transpose_to(c, c.scrA, X2, c.xnT, 0)

    mm0 = c.s_mm.n
    sc0 = c.s_sc.n

    def t1(t):
        for ft in range(2):
            M = 128 if ft == 0 else 64
            for tg in range(4):
                bank = c.pp[(ft * 4 + tg) % 4]
                if ft * 4 + tg >= 4:
                    t.wait_ge(c.s_sc.h, sc0 + ft * 4 + tg - 4 + 1)
                for cc in range(NCC):
                    mm = t.matmul(
                        bank[0:M, :],
                        c.wU[:, cc * 192 + ft * 128:cc * 192 + ft * 128 + M],
                        c.xnT[:, cc * T + tg * 512:cc * T + tg * 512 + 512],
                        start=(cc == 0), stop=(cc == NCC - 1))
                    if cc == NCC - 1:
                        c.s_mm.inc(mm)

    def s2(s):
        for ft in range(2):
            M = 128 if ft == 0 else 64
            for tg in range(4):
                k = ft * 4 + tg
                s.wait_ge(c.s_mm.h, mm0 + k + 1)
                c.s_sc.inc(s.activation(
                    c.scrA[0:M, ft * T + tg * 512:ft * T + tg * 512 + 512],
                    c.pp[k % 4][0:M, :], AF.Identity,
                    bias=c.bfeat[0:M, 8 + ft:9 + ft]))
    phase(nc, tensor=t1, scalar=s2)

    mm1 = c.s_mm.n
    sc1 = c.s_sc.n

    def t2(t):
        for which in range(2):
            r0 = which * 64
            for tg in range(4):
                k = which * 4 + tg
                bank = c.pp[k % 4]
                if k >= 4:
                    t.wait_ge(c.s_sc.h, sc1 + k - 4 + 1)
                c.s_mm.inc(t.matmul(
                    bank[0:64, :],
                    c.wD[r0:r0 + 64, 0:64],
                    c.scrA[r0:r0 + 64, tg * 512:tg * 512 + 512],
                    start=True, stop=True))

    def s3(s):
        for which in range(2):
            for tg in range(4):
                k = which * 4 + tg
                s.wait_ge(c.s_mm.h, mm1 + k + 1)
                c.s_sc.inc(s.activation(
                    c.scrBf[0:64, which * 2048 + tg * 512:
                            which * 2048 + tg * 512 + 512],
                    c.pp[k % 4][0:64, :],
                    AF.Tanh if which == 0 else AF.Sigmoid,
                    bias=c.bfeat[0:64, 10 + which:11 + which]))
    phase(nc, tensor=t2, scalar=s3)

    def v3(v):
        v.tensor_scalar(c.scrBf[64:128, 0:2048], c.scrBf[0:64, 2048:4096],
                        -1.0, 1.0, ALU.mult, ALU.add)
        c.vser(v, v.tensor_tensor(c.scrBf[64:128, 2048:4096],
                                  c.scrBf[0:64, 2048:4096],
                                  c.scrBf[0:64, 0:2048], ALU.mult))
        v.tensor_tensor_scan(c.scrA[0:64, 4096:6144],
                             c.scrBf[64:128, 0:2048],
                             c.scrBf[64:128, 2048:4096],
                             0.0, ALU.mult, ALU.add)
    phase(nc, vector=v3)

    def s4(s):
        s.activation(c.scrA[0:64, 6144:8192], c.scrA[0:64, T:T + 2048],
                     AF.Sigmoid)
    phase(nc, scalar=s4)

    def v4(v):
        v.tensor_tensor(c.scrA[0:64, 8192:10240], c.scrA[0:64, 4096:6144],
                        c.scrA[0:64, 6144:8192], ALU.mult)
    phase(nc, vector=v4)

    mm2 = c.s_mm.n
    ev0 = c.s_ev.n

    def t3(t):
        for tt in range(NTT):
            pr = (c.pp[0], c.pp[1]) if tt % 2 == 0 else (c.pp[2], c.pp[3])
            if tt >= 2:
                t.wait_ge(c.s_ev.h, ev0 + (tt - 2) * 2 + 2)
            for ch in range(2):
                t.matmul(pr[ch][:],
                         c.scrA[0:64, 8192 + tt * 128:8192 + tt * 128 + 128],
                         c.wD[0:64, 1024 + ch * 512:1024 + ch * 512 + 512],
                         start=True, stop=False)
                c.s_mm.inc(t.matmul(pr[ch][:], c.ones1[:],
                                    c.rowb[0:1, ch * 512:ch * 512 + 512],
                                    start=False, stop=True))

    def v5(v):
        for tt in range(NTT):
            pr = (c.pp[0], c.pp[1]) if tt % 2 == 0 else (c.pp[2], c.pp[3])
            v.wait_ge(c.s_mm.h, mm2 + (tt + 1) * 2)
            for ch in range(2):
                c.s_ev.inc(v.tensor_tensor(
                    c.stream[:, tt * C + ch * 512:tt * C + ch * 512 + 512],
                    c.stream[:, tt * C + ch * 512:tt * C + ch * 512 + 512],
                    pr[ch][:], ALU.add))
    phase(nc, tensor=t3, vector=v5)


def build_l0(c, w):
    import os
    parts = os.environ.get("K_PARTS", "fa f")
    for p in parts.split():
        for ch in p:
            if ch == "f":
                emit_l0_ffn(c, w)
            elif ch == "a":
                emit_l0_attn(c, w)
            elif ch == "n":
                emit_ln(c)



L1_KEYS = ["L1_W_attn", "L1_battn_feat", "L1_bv_row", "L1_W_proj",
           "L1_bproj_row", "L1_gate_w", "L1_gate_b", "L1_w1", "L1_b1",
           "L1_w2", "L1_b2"]
L0_KEYS = ["L0_W_in", "L0_b_in", "L0_W_x", "L0_b_x", "L0_W_dt", "L0_b_dt",
           "L0_W_out", "L0_b_out", "L0_norm_w", "L0_W_r", "L0_W_k", "L0_W_v",
           "L0_mu_r", "L0_mu_k"]
L2_KEYS = ["L2_Wq", "L2_Wk", "L2_Wv", "L2_Wo", "L2_bo_row", "L2_router_w",
           "L2_router_b", "L2_gate_w", "L2_up_w", "L2_down_w", "L2_aq", "L2_ak"]
L3_KEYS = ["L3_w_qkv", "L3_b_qkv_feat", "L3_bv_row", "L3_w_out", "L3_bout_row",
           "L3_qg_col", "L3_kg_col", "L3_cos", "L3_sin", "L3_norm_scale",
           "L3_norm_scale_row", "L3_gate_w",
           "L3_gate_b", "L3_mlp1_w", "L3_mlp1_b", "L3_mlp2_w", "L3_mlp2_b"]
LAYER_KEYS = {0: L0_KEYS, 1: L1_KEYS, 2: L2_KEYS, 3: L3_KEYS}


def build_program(layers, wdict):
    import os
    nc = bass.Bass("TRN2", target_bir_lowering=False,
                   detect_race_conditions=not os.environ.get("K_NO_RACE_DETECT"))
    keys = []
    for li in layers:
        keys += LAYER_KEYS[li]
    x_in = nc.dram_tensor("x", [T, C], F32, kind="ExternalInput").ap()
    y_out = nc.dram_tensor("y", [T, C], F32, kind="ExternalOutput").ap()
    dram = {}
    for k in keys:
        arr = wdict[k]
        dram[k] = nc.dram_tensor(k, list(arr.shape), F32, kind="ExternalInput").ap()
    with ExitStack() as es:
        c = alloc_ctx(nc, es)
        emit_setup(c, x_in)
        for li in layers:
            if li == 0:
                build_l0(c, dram)
            elif li == 1:
                build_l1(c, dram)
            elif li == 2:
                build_l2(c, dram)
            elif li == 3:
                build_l3(c, dram)
        emit_store(c, y_out)
        import os
        dumps = os.environ.get("K_DUMP", "")
        if dumps:
            for nm in dumps.split(","):
                tn = getattr(c, nm)
                dt = nc.dram_tensor("dump_" + nm, list(tn.shape),
                                    tn.dtype, kind="ExternalOutput").ap()
                st0 = c.s_st.n

                def sdump(s, tn=tn, dt=dt, st0=st0):
                    c.s_st.inc(s.dma_start(out=dt, in_=tn[:]), 16)
                    s.wait_ge(c.s_st.h, st0 + 16)
                phase(nc, sync=sdump)
    return nc, keys


def kernel(**inputs):
    w = host_prep(inputs)
    x = _as_f32(inputs["L0_attn_x"])
    import os
    gs = os.environ.get("K_GROUPS", "0123")
    groups = [[int(ch) for ch in g] for g in gs.split(",")]
    cur = [np.ascontiguousarray(x[b]) for b in range(8)]
    for layers in groups:
        nc, keys = build_program(layers, w)
        in_maps = []
        for b in range(8):
            m = {"x": cur[b]}
            for k in keys:
                m[k] = w[k]
            in_maps.append(m)
        res = run_bass_kernel_spmd(nc, in_maps, core_ids=list(range(8)))
        cur = [np.asarray(res.results[b]["y"]) for b in range(8)]
    return np.stack(cur, axis=0)
